# version 10
# baseline (speedup 1.0000x reference)
import numpy as np
import ml_dtypes
import concourse.bass as bass
import concourse.mybir as mybir
from concourse.bass_utils import run_bass_kernel_spmd
from contextlib import ExitStack

F32 = mybir.dt.float32
BF16 = mybir.dt.bfloat16
BF = ml_dtypes.bfloat16
AX = mybir.AxisListType.X
ALU = mybir.AluOpType
ACT = mybir.ActivationFunctionType

NCORES = 8
T = 2048
D = 1024
DH = 64
NHEAD = 16
CHUNK = 64
NCH = 256
FH = 4096
NE = 8
NEG = np.float32(-1e30)
TS = 256
NSUB = T // TS
NTT = T // 128

ENGS = ("sync", "gpsimd", "vector", "scalar", "tensor")
V, Sc, Tn, Sy, Gp = "vector", "scalar", "tensor", "sync", "gpsimd"

BASE = 20480
SB = {}
_off = [BASE]
def _reg(name, nbytes, align=64):
    _off[0] = (_off[0] + align - 1) // align * align
    SB[name] = _off[0]
    _off[0] += nbytes

_reg("xnhi", 32768)
_reg("sreg", 32768)
_reg("big", 32768)
_reg("wring", 4096)
_reg("w1buf", 4096)
_reg("w2buf", 4096)
_reg("scrA", 8192)
_reg("scrB", 8192)
_reg("scrC", 8192)
_reg("scrD", 8192)
_reg("scrE", 4096)
_reg("msb", 8192)
_reg("ckk", 4096)
_reg("ckx", 4096)
_reg("oh", 4096)
_reg("rowscr", 1536)
_reg("pmcols", 64)
_reg("sel8", 2048)
_reg("b2hi", 2048)
_reg("b2lo", 2048)
_reg("idf", 512)
_reg("idb", 256)
_reg("onesf", 1024)
_reg("onesb", 256)
_reg("selb", 32)
_reg("ub", 256)
_reg("wrf", 256)
_reg("wrhib", 128)
_reg("wrlob", 128)
_reg("gbs", 32)
_reg("wrmsfm", 32)
_reg("b1s", 256)
_reg("qf", 1024)
_reg("qb", 512)
_reg("kf", 1024)
_reg("ilse", 128)
_reg("nilse", 128)
_reg("stat", 256)
_reg("tmph", 1024)
_reg("tmpl", 512)
_reg("stg", 2048)
_reg("fstg", 2048)
_reg("istg", 512)
_reg("istg2", 512)
_reg("r8", 512)
_reg("lgt", 128)
_reg("ohtm", 64)
_reg("mx8", 256)
_reg("repsb", 1024)
_reg("hsub", 4096)
_reg("rrmscols", 64)
_reg("warmt", 64)
_reg("wbias", 64)
_reg("iota8", 32)
_reg("idxu", 64)
_reg("bo8", 8192)
_reg("selh", 2048)
_reg("r8b", 1024)
_reg("statln", 256)
_reg("mrepsb", 512)
_reg("rrepsb", 512)
_reg("rrrepsb", 512)
assert _off[0] <= 229376, _off[0]


class Prog:

    def __init__(self, nc):
        self.nc = nc
        self.q = {e: [] for e in ENGS}
        self.sem = {}
        self.cnt = {}
        self.ctx = ExitStack()

    def new_sem(self, name):
        self.sem[name] = self.ctx.enter_context(self.nc.semaphore(name))
        self.cnt[name] = 0

    def op(self, eng, meth, *args, inc=None, n=1, **kw):
        if inc is not None:
            self.cnt[inc] += n
        self.q[eng].append((meth, args, kw, inc, n))
        return self.cnt[inc] if inc else None

    def dma(self, eng, out_ap, in_ap, inc):
        return self.op(eng, "dma_start", inc=inc, n=16, out=out_ap, in_=in_ap)

    def wait(self, eng, sem, val=None):
        v = self.cnt[sem] if val is None else val
        if v > 0:
            self.q[eng].append(("__wait__", (sem, v), {}, None, 0))

    def barrier(self, engs=ENGS, sems=("v", "s", "t")):
        vals = [(sm, self.cnt[sm]) for sm in sems]
        for e in engs:
            for sm, v in vals:
                self.wait(e, sm, v)

    def build(self):
        block = self.ctx.enter_context(self.nc.Block())
        for eng in ENGS:
            items = self.q[eng]
            def body(e, items=items):
                for meth, args, kw, inc, n in items:
                    if meth == "__wait__":
                        e.wait_ge(self.sem[args[0]], args[1])
                    else:
                        ins = getattr(e, meth)(*args, **kw)
                        if inc is not None:
                            ins.then_inc(self.sem[inc], n)
            getattr(block, eng)(body)
        self.ctx.close()


def build(core_debug=False):
    nc = bass.Bass("TRN2", target_bir_lowering=False,
                   detect_race_conditions=False)

    dr = {}
    def din(name, shape, dt=F32):
        dr[name] = nc.dram_tensor(name, list(shape), dt,
                                  kind="ExternalInput").ap()
    din("xs", (T, D))
    din("xsum", (NCH, D))
    din("wint", (26, 128, 1024))
    din("woutt", (8, 128, 1024))
    din("wrt", (128, 64))
    din("w1t", (NE, 4, 8, 128, 1024))
    din("w2t", (NE, 8, 4, 128, 1024))
    din("b1t", (NE, 128, 32))
    din("b2m", (NE, D))
    din("rmswt", (128, 8))
    din("gbt", (128, 8))
    din("msbh", (2, 16 * 256), BF16)
    din("ubh", (2, 128), BF16)
    din("idfh", (128, 128))
    din("idbh", (128, 128), BF16)
    din("onesfh", (1, 256))
    din("onesbh", (1, 128), BF16)
    din("selbh", (128, 2), BF16)
    din("sel8h", (8, 1024), BF16)
    din("iota8h", (128, 8))
    din("bo8h", (8, 2048))
    din("selhh", (8, 512))
    y = nc.dram_tensor("y", [T, D], F32, kind="ExternalOutput").ap()

    taps = {}
    def tapout(name, shape, dt=F32):
        taps[name] = nc.dram_tensor("tap_" + name, list(shape), dt,
                                    kind="ExternalOutput").ap()

    sb = {}
    def S_(name, shape, dt, key=None, extra=0):
        sb[name] = nc.alloc_sbuf_tensor_at(
            name, list(shape), dt, offset=SB[key or name] + extra).ap()
    S_("xnhi", (128, 16384), BF16)
    S_("xrl", (128, 2048), F32, "xnhi")
    S_("otile", (128, 2048), F32, "xnhi", 8192)
    S_("sreg", (128, 16384), BF16)
    S_("big", (128, 16384), BF16)
    S_("wring", (128, 2048), BF16)
    S_("w1buf", (128, 2048), BF16)
    S_("w2buf", (128, 2048), BF16)
    S_("scrA", (128, 2048), F32)
    S_("xa", (128, 2048), F32, "scrA")
    S_("azt", (128, 2048), BF16, "scrA")
    S_("scrB", (128, 2048), F32)
    S_("scrC", (128, 2048), F32)
    S_("scrD", (128, 2048), F32)
    S_("cxbf", (128, 2048), BF16, "scrD")
    S_("offn", (128, 16384), BF16, "scrA")
    S_("scrE", (128, 2048), BF16)
    S_("msb", (2, 4096), BF16)
    S_("ckk", (128, 2048), BF16)
    S_("ckx", (128, 2048), BF16)
    S_("rowscr", (1, 384), F32)
    S_("pmcols", (128, 16), F32)
    S_("sel8", (8, 1024), BF16)
    S_("oh", (8, 2048), BF16)
    S_("b2hi", (8, 1024), BF16)
    S_("b2lo", (8, 1024), BF16)
    S_("b2f32", (8, 1024), F32, "scrB")
    S_("idf", (128, 128), F32)
    S_("idb", (128, 128), BF16)
    S_("onesf", (1, 256), F32)
    S_("onesb", (1, 128), BF16)
    S_("selb", (128, 2), BF16)
    S_("ub", (2, 128), BF16)
    S_("wrf", (128, 64), F32)
    S_("wrhib", (128, 64), BF16)
    S_("wrlob", (128, 64), BF16)
    S_("gbs", (128, 8), F32)
    S_("wrmsfm", (128, 8), F32)
    S_("b1s", (128, 64), F32)
    S_("qf", (128, 256), F32)
    S_("qb", (128, 256), BF16)
    S_("kf", (128, 256), F32)
    S_("ilse", (128, 32), F32)
    S_("nilse", (128, 32), F32)
    S_("stat", (128, 64), F32)
    S_("tmph", (128, 512), BF16)
    S_("tmpl", (128, 256), BF16)
    S_("stg", (128, 1024), BF16)
    S_("fstg", (128, 512), F32)
    S_("istg", (128, 128), F32)
    S_("istg2", (128, 128), F32)
    S_("xloT", (128, 128), BF16, "istg")
    S_("r8", (8, 128), F32)
    S_("lgt", (128, 16), F32)
    S_("ohtm", (128, 16), BF16)
    S_("mx8", (128, 64), F32)
    S_("repsb", (128, 256), F32)
    S_("hsub", (128, 2048), BF16)
    S_("rrmscols", (128, 16), F32)
    S_("warmt", (128, 16), F32)
    S_("wbias", (128, 1), F32)
    S_("dbgln", (128, 512), F32, "scrD")
    S_("dbglg", (128, 128), F32, "scrD", 2048)
    S_("iota8", (128, 8), F32)
    S_("idxu", (128, 16), mybir.dt.uint32)
    S_("bo8", (8, 2048), F32)
    S_("selh", (8, 512), F32)
    S_("r8b", (8, 256), F32)
    S_("scsB", (128, 2048), F32, "scrB")
    S_("wbig", (128, 4096), BF16, "scrC")
    S_("statln", (128, 64), F32)
    S_("mrepsb", (128, 128), F32)
    S_("rrepsb", (128, 128), F32)
    S_("rrrepsb", (128, 128), F32)

    p = Prog(nc)
    ps = {}
    for nm, shape, dt in [
        ("pm0", (128, 512), F32), ("pm1", (128, 512), F32),
        ("pu0", (128, 512), F32), ("pu1", (128, 512), F32),
        ("ptA", (128, 1024), BF16), ("ptB", (128, 1024), BF16),
        ("pu2", (128, 512), F32), ("pu3", (128, 512), F32),
    ]:
        ps[nm] = p.ctx.enter_context(nc.psum_tensor(nm, list(shape), dt)).ap()

    for s in ("dL", "dX0", "dX1", "dR0", "dR1", "dW10", "dW11", "dW20",
              "dW21", "db0", "db1", "dO", "v", "s", "t"):
        p.new_sem(s)
    ps["ptAf"] = ps["ptA"].bitcast(F32)
    ps["ptBf"] = ps["ptB"].bitcast(F32)

    p.dma(Sy, sb["msb"][:, :], dr["msbh"], "dL")
    p.dma(Sy, sb["ub"][:, :], dr["ubh"], "dL")
    p.dma(Sy, sb["idf"][:, :], dr["idfh"], "dL")
    p.dma(Sy, sb["idb"][:, :], dr["idbh"], "dL")
    p.dma(Sy, sb["onesf"][:, :], dr["onesfh"], "dL")
    p.dma(Sy, sb["onesb"][:, :], dr["onesbh"], "dL")
    p.dma(Sy, sb["selb"][:, :], dr["selbh"], "dL")
    p.dma(Sy, sb["sel8"][:, :], dr["sel8h"], "dL")
    p.dma(Sy, sb["iota8"][:, :], dr["iota8h"], "dL")
    p.dma(Sy, sb["bo8"][:, :], dr["bo8h"], "dL")
    p.dma(Sy, sb["selh"][:, :], dr["selhh"], "dL")
    p.dma(Sy, sb["wrf"][:, :], dr["wrt"], "dL")
    p.dma(Sy, sb["gbs"][:, :], dr["gbt"], "dL")
    p.dma(Sy, sb["wrmsfm"][:, :], dr["rmswt"], "dL")
    p.dma(Sy, sb["b2f32"][:, :], dr["b2m"], "dL")
    cload = p.cnt["dL"]
    db0_early = p.dma(Gp, sb["b1s"][:, 0:32], dr["b1t"][0], "db0")

    p.wait(V, "dL", cload)
    p.op(V, "memset", sb["wbias"][:, :], 0.25)
    p.op(V, "memset", sb["statln"][:, :], 0.0)
    p.op(V, "memset", sb["mx8"][:, :], 0.0)
    p.op(V, "reduce_max", sb["warmt"][:, 0:1], sb["wbias"][:, :], axis=AX,
         inc="v")
    p.wait(V, "v")
    p.op(V, "tensor_scalar", sb["warmt"][:, 1:2], sb["wbias"][:, :],
         sb["wbias"][:, :], 2.0, ALU.subtract, ALU.mult)
    p.op(V, "scalar_tensor_tensor", sb["warmt"][:, 1:2], sb["wbias"][:, :],
         sb["wbias"][:, :], sb["wbias"][:, :], ALU.mult, ALU.add)
    p.op(V, "tensor_scalar", sb["warmt"][:, 1:2], sb["wbias"][:, :],
         sb["wbias"][:, :], None, ALU.is_equal, inc="v")
    p.wait(Sc, "v")
    for fn in (ACT.Exp, ACT.Gelu, ACT.Sigmoid, ACT.Silu, ACT.Square,
               ACT.Sqrt, ACT.Identity):
        p.op(Sc, "activation", sb["warmt"][:, 2:3], sb["wbias"][:, :], fn,
             bias=sb["wbias"][:, :], scale=1.0, accum_out=sb["warmt"][:, 3:4])
    p.op(Sc, "nop", inc="s")
    p.op(V, "tensor_copy", sb["wrhib"][:, :], sb["wrf"][:, :], inc="v")
    p.wait(V, "v")
    p.op(V, "tensor_tensor", sb["wrlob"][:, :], sb["wrf"][:, :],
         sb["wrhib"][:, :], ALU.subtract)
    p.op(V, "tensor_copy", sb["b2hi"][:, :], sb["b2f32"][:, :], inc="v")
    p.wait(V, "v")
    p.op(V, "tensor_tensor", sb["b2lo"][:, :], sb["b2f32"][:, :],
         sb["b2hi"][:, :], ALU.subtract, inc="v")

    evac = {k: 0 for k in ("pm0", "pm1", "pu0", "pu1", "pu2", "pu3",
                           "ptA", "ptB", "ptAf", "ptBf")}
    xa_done = [[], []]
    rowscr_t = [0]

    def ln_stats(src, dump, so):
        sl = lambda i: sb["statln"][:, so + i:so + i + 1]

        def fv(*args, **kw):
            c = p.op(V, *args, inc="v", **kw)
            p.wait(V, "v", c)
            return c

        fv("reduce_sum", sl(0), src, axis=AX)
        p.wait(Sc, "v")
        p.op(Sc, "activation", dump, src, ACT.Square, bias=0.0, scale=1.0,
             inc="s")
        fv("tensor_scalar", sl(2), sl(0), 1.0 / D, None, ALU.mult)
        fv("tensor_tensor", sl(3), sl(2), sl(2), ALU.mult)
        p.wait(V, "s")
        fv("reduce_sum", sl(1), dump, axis=AX)
        fv("tensor_scalar", sl(4), sl(1), 1.0 / D, None, ALU.mult)
        fv("tensor_tensor", sl(5), sl(4), sl(3), ALU.subtract)

        def newton(tin, eps, b):
            fv("tensor_scalar", sl(b), tin, 1.0, eps, ALU.mult, ALU.add)
            p.wait(Sc, "v")
            p.op(Sc, "activation", sl(b + 1), sl(b), ACT.Sqrt, inc="s")
            p.wait(V, "s")
            fv("reciprocal", sl(b + 2), sl(b + 1))
            fv("tensor_tensor", sl(b + 3), sl(b + 2), sl(b + 2), ALU.mult)
            fv("tensor_tensor", sl(b + 4), sl(b), sl(b + 3), ALU.mult)
            fv("tensor_scalar", sl(b + 5), sl(b + 4), -0.5, 1.5, ALU.mult,
               ALU.add)
            fv("tensor_tensor", sl(b + 6), sl(b + 2), sl(b + 5), ALU.mult)
            return sl(b + 6)

        rstd = newton(sl(5), 1e-5, 8)
        fv("tensor_scalar", sl(16), sl(2), -1.0, None, ALU.mult)
        fv("tensor_tensor", sl(17), rstd, rstd, ALU.mult)
        fv("tensor_tensor", sl(18), sl(5), sl(17), ALU.mult)
        rrms = newton(sl(18), 1e-6, 20)
        return sl(16), rstd, rrms

    rowps = lambda: ps["pu2"][0:1, 128:256]
    repps = lambda: ps["pu2"][:, 0:128]

    def make_rep(col, rep_sb):
        p.wait(Tn, "v")
        p.wait(Tn, "v", evac["pu2"])
        wt = p.op(Tn, "transpose", rowps(), col, sb["idf"][:, :], inc="t")
        p.wait(V, "t", wt)
        p.wait(V, "t", rowscr_t[0])
        rs = p.op(V, "tensor_copy", sb["rowscr"][0:1, 0:128], rowps(),
                  inc="v")
        p.wait(Tn, "v", rs)
        wb = p.op(Tn, "matmul", repps(), sb["onesf"][0:1, 0:128],
                  sb["rowscr"][0:1, 0:128], start=True, stop=True, inc="t")
        rowscr_t[0] = wb
        p.wait(V, "t", wb)
        ev = p.op(V, "tensor_copy", rep_sb, repps(), inc="v")
        evac["pu2"] = ev
        return ev

    hs_guard = [0, 0]
    for tt in range(NTT):
        par = tt % 2
        so = 32 * par
        xa_sl = sb["xa"][:, par * 1024:(par + 1) * 1024]
        for semn, vv in xa_done[par]:
            p.wait(Sy, semn, vv)
        dx = p.dma(Sy, xa_sl, dr["xs"][tt * 128:(tt + 1) * 128, :],
                   "dX%d" % par)
        p.wait(V, "dX%d" % par, dx)
        p.wait(Sc, "dX%d" % par, dx)
        dump = sb["scrB"][:, par * 1024:(par + 1) * 1024]
        negm, rstd, rrms = ln_stats(xa_sl, dump, so)
        if core_debug:
            p.op(V, "tensor_copy", sb["dbgln"][:, tt * 32:(tt + 1) * 32],
                 sb["statln"][:, so:so + 32])
        p.op(V, "tensor_copy", sb["rrmscols"][:, tt:tt + 1], rrms, inc="v")
        make_rep(negm, sb["mrepsb"][:, :])
        make_rep(rstd, sb["rrepsb"][:, :])

        for k in range(8):
            pmname = "pm%d" % (k % 2)
            fx = ps[pmname][:, 0:128]
            p.wait(Tn, "v", evac[pmname])
            wt = p.op(Tn, "transpose", fx, xa_sl[:, k * 128:(k + 1) * 128],
                      sb["idf"][:, :], inc="t")
            fsl = sb["fstg"][:, (k % 2) * 128:(k % 2) * 128 + 128]
            p.wait(V, "t", wt)
            p.wait(V, "s", hs_guard[k % 2])
            p.op(V, "tensor_tensor", fsl, fx, sb["mrepsb"][:, :], ALU.add,
                 inc="v")
            evac[pmname] = p.cnt["v"]
            p.wait(V, "v")
            p.op(V, "tensor_tensor", sb["istg2"][:, :], fsl,
                 sb["rrepsb"][:, :], ALU.mult, inc="v")
            p.wait(V, "v")
            hi_sl = sb["xnhi"][:, k * 2048 + tt * 128:k * 2048 + tt * 128 + 128]
            p.op(V, "tensor_copy", hi_sl, sb["istg2"][:, :], inc="v")
            p.wait(V, "v")
            lo_ready = p.op(V, "tensor_tensor", sb["xloT"][:, :],
                            sb["istg2"][:, :], hi_sl, ALU.subtract, inc="v")
            p.wait(Tn, "v", lo_ready)
            if k == 0:
                p.wait(Tn, "v", evac["pu1"])
            p.op(Tn, "matmul", ps["pu1"][0:8, 0:128],
                 sb["wrhib"][:, k * 8:(k + 1) * 8], hi_sl,
                 start=(k == 0), stop=False)
            p.op(Tn, "matmul", ps["pu1"][0:8, 0:128],
                 sb["wrhib"][:, k * 8:(k + 1) * 8], sb["xloT"][:, :],
                 start=False, stop=False)
            rmm = p.op(Tn, "matmul", ps["pu1"][0:8, 0:128],
                       sb["wrlob"][:, k * 8:(k + 1) * 8], hi_sl,
                       start=False, stop=(k == 7), inc="t")
            p.wait(V, "t", rmm)
        xa_done[par] = [("v", p.cnt["v"]), ("s", p.cnt["s"]),
                        ("t", p.cnt["t"])]

        p.wait(V, "t")
        ev = p.op(V, "tensor_copy", sb["r8"][:, :], ps["pu1"][0:8, 0:128],
                  inc="v")
        evac["pu1"] = ev
        p.wait(Tn, "v", ev)
        p.wait(Tn, "v", evac["pu0"])
        wt = p.op(Tn, "transpose", ps["pu0"][0:128, 0:8], sb["r8"][:, :],
                  sb["idf"][0:8, 0:8], inc="t")
        lg = sb["lgt"][:, par * 8:(par + 1) * 8]
        ohc = sb["ohtm"][:, par * 8:(par + 1) * 8]
        m8 = lambda i: sb["mx8"][:, 32 * par + i:32 * par + i + 1]
        vmax8 = sb["mx8"][:, 32 * par + 8:32 * par + 16]
        vidx8 = sb["mx8"][:, 32 * par + 16:32 * par + 24]
        p.wait(V, "t", wt)
        evac["pu0"] = p.op(V, "tensor_copy", lg, ps["pu0"][:, 0:8], inc="v")
        p.wait(V, "v")
        if core_debug:
            p.op(V, "tensor_copy", sb["dbglg"][:, tt * 8:(tt + 1) * 8], lg)
        p.op(V, "max", vmax8, lg, inc="v")
        p.wait(V, "v")
        idxu = sb["idxu"][:, par * 8:(par + 1) * 8]
        p.op(V, "max_index", idxu, vmax8, lg, inc="v")
        p.wait(V, "v")
        mxi = p.op(V, "tensor_copy", vidx8, idxu, inc="v")
        p.wait(V, "v", mxi)
        p.op(V, "tensor_copy", m8(0), vmax8[:, 0:1], inc="v")
        p.wait(Tn, "v", mxi)
        p.wait(Tn, "v", evac["pu3"])
        w1 = p.op(Tn, "transpose", ps["pu3"][0:1, 0:128],
                  vidx8[:, 0:1], sb["idf"][:, :], inc="t")
        p.wait(V, "t", w1)
        p.wait(V, "t", rowscr_t[0])
        rs = p.op(V, "tensor_copy", sb["rowscr"][0:1, 0:128],
                  ps["pu3"][0:1, 0:128], inc="v")
        p.wait(Tn, "v", rs)
        w2 = p.op(Tn, "matmul", ps["pu3"][:, 128:136],
                  sb["rowscr"][0:1, 0:128], sb["onesf"][0:1, 0:8],
                  start=True, stop=True, inc="t")
        rowscr_t[0] = w2
        p.wait(V, "t", w2)
        oh_v = p.op(V, "tensor_tensor", ohc, sb["iota8"][:, :],
                    ps["pu3"][:, 128:136], ALU.is_equal, inc="v")
        evac["pu3"] = oh_v
        p.wait(Sc, "v")
        p.op(Sc, "activation", sb["warmt"][:, 8:16], lg, ACT.Exp, bias=0.0,
             scale=1.0, inc="s")
        ws2 = p.op(Sc, "activation", m8(4), m8(0), ACT.Exp, bias=0.0,
                   scale=1.0, inc="s")
        p.wait(V, "s", ws2)
        p.op(V, "reduce_sum", m8(2), sb["warmt"][:, 8:16], axis=AX, inc="v")
        p.wait(V, "v")
        p.op(V, "reciprocal", m8(5), m8(2), inc="v")
        p.wait(V, "v")
        pm_ready = p.op(V, "tensor_tensor", sb["pmcols"][:, tt:tt + 1],
                        m8(4), m8(5), ALU.mult, inc="v")
        p.wait(Tn, "v", oh_v)
        p.wait(Tn, "v", evac["ptA"])
        wt2 = p.op(Tn, "transpose", ps["ptA"][0:8, 0:128], ohc,
                   sb["idb"][:, :], inc="t")
        p.wait(V, "t", wt2)
        evac["ptA"] = p.op(V, "tensor_copy",
                           sb["oh"][0:8, tt * 128:(tt + 1) * 128],
                           ps["ptA"][0:8, 0:128], inc="v")

    p.barrier()
    if core_debug:
        tapout("xnhi", (128, 16384), BF16)
        tapout("pmcols", (128, 16))
        tapout("oh", (8, 2048), BF16)
        tapout("dbgln", (128, 512))
        tapout("dbglg", (128, 128))
        p.wait(Sy, "v")
        p.dma(Sy, taps["dbgln"], sb["dbgln"][:, :], "dO")
        p.dma(Sy, taps["dbglg"], sb["dbglg"][:, :], "dO")
        p.dma(Sy, taps["xnhi"], sb["xnhi"][:, :], "dO")
        p.dma(Sy, taps["pmcols"], sb["pmcols"][:, :], "dO")
        p.dma(Sy, taps["oh"], sb["oh"][:, :], "dO")
        for _e in ENGS:
            p.wait(_e, "dO", p.cnt["dO"])

    ring_state = {"n": 0, "tcnt": [0, 0]}

    def ring_load(src_ap):
        j = ring_state["n"]
        slot = j % 2
        p.wait(Gp, "t", ring_state["tcnt"][slot])
        c = p.dma(Gp, sb["wring"][:, slot * 1024:(slot + 1) * 1024], src_ap,
                  "dR%d" % slot)
        ring_state["n"] = j + 1
        return ("dR%d" % slot, c, slot)

    def ring_use(tok):
        semn, c, slot = tok
        p.wait(Tn, semn, c)
        return sb["wring"][:, slot * 1024:(slot + 1) * 1024], slot

    def ring_done(slot):
        ring_state["tcnt"][slot] = p.cnt["t"]

    for st2 in range(2):
        par = st2
        so = 32 * par
        xa_sl = sb["xa"][:, par * 1024:(par + 1) * 1024]
        for semn, vv in xa_done[par]:
            p.wait(Sy, semn, vv)
        dx = p.dma(Sy, xa_sl, dr["xsum"][st2 * 128:(st2 + 1) * 128, :],
                   "dX%d" % par)
        p.wait(V, "dX%d" % par, dx)
        p.wait(Sc, "dX%d" % par, dx)
        dump = sb["scrB"][:, par * 1024:(par + 1) * 1024]
        negm, rstd, rrms = ln_stats(xa_sl, dump, so)
        make_rep(negm, sb["mrepsb"][:, :])
        make_rep(rstd, sb["rrepsb"][:, :])
        make_rep(rrms, sb["rrrepsb"][:, :])
        p.wait(V, "v")
        for k in range(8):
            pmname = "pm%d" % (k % 2)
            fx = ps[pmname][:, 0:128]
            p.wait(Tn, "v", evac[pmname])
            wt = p.op(Tn, "transpose", fx, xa_sl[:, k * 128:(k + 1) * 128],
                      sb["idf"][:, :], inc="t")
            fsl = sb["fstg"][:, (k % 2) * 128:(k % 2) * 128 + 128]
            p.wait(V, "t", wt)
            p.wait(V, "s", hs_guard[k % 2])
            p.op(V, "tensor_tensor", fsl, fx, sb["mrepsb"][:, :], ALU.add,
                 inc="v")
            evac[pmname] = p.cnt["v"]
            p.op(V, "tensor_tensor", sb["istg2"][:, :], fsl,
                 sb["rrepsb"][:, :], ALU.mult)
            tv = p.op(V, "tensor_tensor", fsl, sb["istg2"][:, :],
                      sb["rrrepsb"][:, :], ALU.mult, inc="v")
            p.wait(Sc, "v", tv)
            hs_sl = sb["scrE"][:, k * 256 + st2 * 128:k * 256 + st2 * 128 + 128]
            hs_guard[k % 2] = p.op(Sc, "activation", hs_sl, fsl, ACT.Identity,
                                   bias=0.0, scale=sb["wrmsfm"][:, k:k + 1],
                                   inc="s")
        xa_done[par] = [("v", p.cnt["v"]), ("s", p.cnt["s"]),
                        ("t", p.cnt["t"])]

    hs_ready = p.cnt["v"]
    gsum_s = {}
    kf_ready = 0
    for oi, o in enumerate(list(range(16, 24)) + list(range(8, 16)) + [25]):
        tok = ring_load(dr["wint"][o])
        pmname = "pm%d" % (oi % 2)
        pp = ps[pmname][:, 0:256]
        p.wait(Tn, "v", evac[pmname])
        p.wait(Tn, "v", hs_ready)
        wsl, slot = ring_use(tok)
        for k in range(8):
            p.op(Tn, "matmul", pp, wsl[:, k * 128:(k + 1) * 128],
                 sb["scrE"][:, k * 256:(k + 1) * 256],
                 start=(k == 0), stop=(k == 7), inc="t" if k == 7 else None)
        ring_done(slot)
        wt = p.cnt["t"]
        if 16 <= o < 24:
            tf = o - 16
            p.wait(Sc, "t", wt)
            ws = p.op(Sc, "activation", sb["scrC"][:, tf * 256:(tf + 1) * 256],
                      pp, ACT.Sigmoid, bias=sb["gbs"][:, tf:tf + 1], scale=1.0,
                      inc="s")
            gsum_s[tf] = ws
            p.wait(Tn, "s", ws)
            evac[pmname] = 0
        elif 8 <= o < 16:
            tf = o - 8
            p.wait(V, "t", wt)
            p.wait(V, "s", gsum_s[tf])
            p.op(V, "tensor_tensor", sb["fstg"][:, 0:256], pp,
                 sb["scrC"][:, tf * 256:(tf + 1) * 256], ALU.mult, inc="v")
            p.wait(V, "v")
            evac[pmname] = p.op(V, "tensor_tensor",
                                sb["cxbf"][:, tf * 256:(tf + 1) * 256], pp,
                                sb["fstg"][:, 0:256], ALU.subtract, inc="v")
        else:
            p.wait(V, "t", wt)
            kf_ready = p.op(V, "tensor_copy", sb["kf"][:, :], pp, inc="v")
            evac[pmname] = kf_ready
    for tf in range(8):
        p.wait(V, "v", kf_ready)
        p.wait(V, "s", gsum_s[tf])
        p.op(V, "tensor_tensor", sb["fstg"][:, 0:256], sb["kf"][:, :],
             sb["scrC"][:, tf * 256:(tf + 1) * 256], ALU.mult, inc="v")
        p.wait(V, "v")
        p.op(V, "tensor_tensor", sb["ckk"][:, tf * 256:(tf + 1) * 256],
             sb["kf"][:, :], sb["fstg"][:, 0:256], ALU.subtract, inc="v")
    p.wait(Tn, "v")
    for tf in range(8):
        for ct in range(2):
            ptn = "ptA" if (tf * 2 + ct) % 2 == 0 else "ptB"
            p.wait(Tn, "v", evac[ptn])
            wt = p.op(Tn, "transpose",
                      ps[ptn][0:128, tf * 128:(tf + 1) * 128],
                      sb["cxbf"][:, tf * 256 + ct * 128:
                                 tf * 256 + ct * 128 + 128],
                      sb["idb"][:, :], inc="t")
            p.wait(V, "t", wt)
            evac[ptn] = p.op(V, "tensor_copy",
                             sb["ckx"][:, ct * 1024 + tf * 128:
                                       ct * 1024 + (tf + 1) * 128],
                             ps[ptn][0:128, tf * 128:(tf + 1) * 128], inc="v")

    p.barrier()
    if core_debug:
        tapout("ckk", (128, 2048), BF16)
        tapout("ckx", (128, 2048), BF16)
        p.wait(Sy, "v")
        p.dma(Sy, taps["ckk"], sb["ckk"][:, :], "dO")
        p.dma(Sy, taps["ckx"], sb["ckx"][:, :], "dO")
        for _e in ENGS:
            p.wait(_e, "dO", p.cnt["dO"])

    unit_hist = []
    hsub_s = [0, 0]
    last_vW = [0]
    for sub in range(NSUB):
        s0 = sub * TS
        p.wait(Tn, "v", evac["ptBf"])
        for i2 in range(2):
            tt16 = sub * 2 + i2
            wt = p.op(Tn, "transpose", ps["ptBf"][0:1, 256:384],
                      sb["rrmscols"][:, tt16:tt16 + 1], sb["idf"][:, :],
                      inc="t")
            p.wait(V, "t", wt)
            p.wait(V, "t", rowscr_t[0])
            rs = p.op(V, "tensor_copy",
                      sb["rowscr"][0:1, i2 * 128:(i2 + 1) * 128],
                      ps["ptBf"][0:1, 256:384], inc="v")
            p.wait(Tn, "v", rs)
        wb = p.op(Tn, "matmul", ps["ptBf"][:, 0:256], sb["onesf"][0:1, 0:128],
                  sb["rowscr"][0:1, 0:256], start=True, stop=True, inc="t")
        rowscr_t[0] = wb
        p.wait(V, "t", wb)
        rep_r = p.op(V, "tensor_copy", sb["repsb"][:, 0:256],
                     ps["ptBf"][:, 0:256], inc="v")
        evac["ptBf"] = rep_r
        for k in range(8):
            fsl2 = sb["fstg"][:, (k % 2) * 256:(k % 2) * 256 + 256]
            p.wait(V, "v", rep_r)
            p.wait(V, "s", hsub_s[k % 2])
            hv = p.op(V, "tensor_tensor", fsl2,
                      sb["xnhi"][:, k * 2048 + s0:k * 2048 + s0 + TS],
                      sb["repsb"][:, 0:256], ALU.mult, inc="v")
            p.wait(Sc, "v", hv)
            hsub_s[k % 2] = p.op(Sc, "activation",
                                 sb["hsub"][:, k * 256:(k + 1) * 256], fsl2,
                                 ACT.Identity, bias=0.0,
                                 scale=sb["wrmsfm"][:, k:k + 1], inc="s")
        hsub_ready = p.cnt["s"]
        o_order = (list(range(16, 24)) + list(range(0, 8)) +
                   list(range(8, 16)) + [24, 25])
        g_evac = {}
        for oi, o in enumerate(o_order):
            tok = ring_load(dr["wint"][o])
            pmname = "pu%d" % (2 + oi % 2)
            pp = ps[pmname][:, 0:256]
            p.wait(Tn, "v", evac[pmname])
            if oi == 0:
                p.wait(Tn, "s", hsub_ready)
            wsl, slot = ring_use(tok)
            for k in range(8):
                p.op(Tn, "matmul", pp, wsl[:, k * 128:(k + 1) * 128],
                     sb["hsub"][:, k * 256:(k + 1) * 256],
                     start=(k == 0), stop=(k == 7),
                     inc="t" if k == 7 else None)
            ring_done(slot)
            wt = p.cnt["t"]
            if 16 <= o < 24:
                tf = o - 16
                p.wait(Sc, "t", wt)
                ws = p.op(Sc, "activation",
                          sb["scrA"][:, tf * 256:(tf + 1) * 256], pp,
                          ACT.Sigmoid, bias=sb["gbs"][:, tf:tf + 1],
                          scale=1.0, inc="s")
                g_evac[tf] = ws
                p.wait(Tn, "s", ws)
                evac[pmname] = 0
            elif o < 8:
                tf = o
                p.wait(Sc, "t", wt)
                ws = p.op(Sc, "activation",
                          sb["scrE"][:, tf * 256:(tf + 1) * 256], pp,
                          ACT.Sigmoid, inc="s")
                p.wait(Tn, "s", ws)
                evac[pmname] = 0
            elif o < 16:
                tf = o - 8
                p.wait(V, "t", wt)
                p.wait(V, "s", g_evac[tf])
                p.op(V, "tensor_tensor",
                     sb["scrB"][:, tf * 256:(tf + 1) * 256], pp,
                     sb["scrA"][:, tf * 256:(tf + 1) * 256], ALU.mult,
                     inc="v")
                p.wait(V, "v")
                evac[pmname] = p.op(V, "tensor_tensor",
                                    sb["scrB"][:, tf * 256:(tf + 1) * 256],
                                    pp, sb["scrB"][:, tf * 256:(tf + 1) * 256],
                                    ALU.subtract, inc="v")
            elif o == 24:
                p.wait(V, "t", wt)
                p.op(V, "tensor_scalar", sb["qf"][:, :], pp, 0.125, None,
                     ALU.mult, inc="v")
                evac[pmname] = p.op(V, "tensor_scalar", sb["qb"][:, :], pp,
                                    0.125, None, ALU.mult, inc="v")
            else:
                p.wait(V, "t", wt)
                evac[pmname] = p.op(V, "tensor_copy", sb["kf"][:, :], pp,
                                    inc="v")
                for tf in range(8):
                    p.wait(V, "s", g_evac[tf])
                    p.op(V, "tensor_tensor", sb["fstg"][:, 0:256],
                         sb["kf"][:, :],
                         sb["scrA"][:, tf * 256:(tf + 1) * 256], ALU.mult,
                         inc="v")
                    p.wait(V, "v")
                    p.op(V, "tensor_tensor",
                         sb["scrC"][:, tf * 256:(tf + 1) * 256],
                         sb["kf"][:, :], sb["fstg"][:, 0:256], ALU.subtract,
                         inc="v")
        p.op(V, "memset", sb["scrA"][:, 0:2048:64], 0.0, inc="v")
        p.wait(V, "v")
        for tf in range(8):
            p.op(V, "tensor_tensor_scan",
                 sb["scrD"][:, tf * 256:(tf + 1) * 256],
                 sb["scrA"][:, tf * 256:(tf + 1) * 256],
                 sb["scrB"][:, tf * 256:(tf + 1) * 256], 0.0, ALU.mult,
                 ALU.add, inc="v")
        for tf in range(8):
            p.op(V, "tensor_tensor_scan", sb["fstg"][:, 0:256],
                 sb["scrA"][:, tf * 256:(tf + 1) * 256],
                 sb["scrC"][:, tf * 256:(tf + 1) * 256], 0.0, ALU.mult,
                 ALU.add, inc="v")
            p.wait(V, "v")
            p.op(V, "tensor_copy", sb["scrB"][:, tf * 256:(tf + 1) * 256],
                 sb["fstg"][:, 0:256], inc="v")
            p.wait(V, "v")
        for tf in range(8):
            p.op(V, "tensor_tensor", sb["fstg"][:, 0:256], sb["qf"][:, :],
                 sb["scrB"][:, tf * 256:(tf + 1) * 256], ALU.mult, inc="v")
            p.wait(V, "v")
            p.op(V, "tensor_copy", sb["tmph"][:, 0:256], sb["fstg"][:, 0:256],
                 inc="v")
            p.wait(V, "v")
            tl_ready = p.op(V, "tensor_tensor", sb["tmpl"][:, :],
                            sb["fstg"][:, 0:256], sb["tmph"][:, 0:256],
                            ALU.subtract, inc="v")
            p.wait(Tn, "v", tl_ready)
            if tf == 0:
                p.wait(Tn, "v", evac["pu2"])
            for tti in range(2):
                col = tti * 16 + 2 * tf
                p.op(Tn, "matmul", ps["pu2"][:, col:col + 2],
                     sb["tmph"][:, tti * 128:(tti + 1) * 128],
                     sb["selb"][:, :], start=True, stop=False)
                p.op(Tn, "matmul", ps["pu2"][:, col:col + 2],
                     sb["tmpl"][:, tti * 128:(tti + 1) * 128],
                     sb["selb"][:, :], start=False, stop=True,
                     inc="t" if (tf == 7 and tti == 1) else None)
        p.wait(V, "t")
        p.op(V, "tensor_copy", sb["ilse"][:, :], ps["pu2"][:, 0:32], inc="v")
        p.wait(V, "v")
        ilse_ready = p.op(V, "tensor_scalar", sb["nilse"][:, :],
                          sb["ilse"][:, :], -1.0, None, ALU.mult, inc="v")
        evac["pu2"] = ilse_ready

        ST = lambda i: sb["stat"][:, 32 + i:32 + i + 1]
        for bt in range(4):
            tti, hb = bt // 2, bt % 2
            gt = sub * 2 + tti
            heads = range(8 * hb, 8 * hb + 8)
            prev_sI = unit_hist[-1]["sI"] if unit_hist else 0
            if len(unit_hist) >= 2:
                p.wait(Tn, "v", unit_hist[-2]["evac_v"])
            p.wait(Tn, "v", ilse_ready)
            BQ = ((ps["pu2"], ps["pu3"], ps["ptAf"], ps["ptBf"]),
                  (ps["pm0"], ps["pm1"], ps["pu0"], ps["pu1"]))
            nb = len(unit_hist)
            scb = list(BQ[nb % 2])
            for j, hd in enumerate(heads):
                sc_ps = scb[j // 2][:, (j % 2) * 256:(j % 2) * 256 + 256]
                rq = (hd % 2) * 64
                tfh = hd // 2
                p.op(Tn, "matmul", sc_ps,
                     sb["qb"][rq:rq + 64, tti * 128:(tti + 1) * 128],
                     sb["ckk"][rq:rq + 64, tfh * 256:(tfh + 1) * 256],
                     start=True, stop=False)
                p.op(Tn, "matmul", sc_ps, sb["ub"][:, :],
                     sb["msb"][0:2, gt * 256:(gt + 1) * 256],
                     start=False, stop=True,
                     inc="t" if j == 7 else None)
            wtS = p.cnt["t"]
            p.wait(V, "t", wtS)
            for j in range(8):
                p.op(V, "tensor_copy", sb["scsB"][:, j * 256:(j + 1) * 256],
                     scb[j // 2][:, (j % 2) * 256:(j % 2) * 256 + 256],
                     inc="v" if j == 7 else None)
            p.wait(V, "v")
            m0_8 = sb["stat"][:, 32:40]
            mm8 = sb["stat"][:, 40:48]
            mn8 = sb["stat"][:, 48:56]
            st8 = sb["stat"][:, 56:64]
            den0 = sb["mx8"][:, 0:8]
            den8 = sb["mx8"][:, 32:40]
            rec8 = sb["mx8"][:, 40:48]
            ils = sb["ilse"][:, tti * 16 + 8 * hb:tti * 16 + 8 * hb + 8]
            scv = sb["scsB"][:, :].rearrange("p (j c) -> p j c", c=256)
            p.op(V, "reduce_max", m0_8, scv, axis=AX, inc="v")
            p.wait(V, "v")
            p.op(V, "tensor_tensor", mm8, m0_8, ils, ALU.max, inc="v")
            p.wait(V, "v")
            vMN = p.op(V, "tensor_scalar", mn8, mm8, -1.0, None, ALU.mult,
                       inc="v")
            p.wait(V, "v", vMN)
            p.wait(V, "s", prev_sI)
            vST = p.op(V, "tensor_tensor", st8, ils, mn8, ALU.add, inc="v")
            p.wait(V, "v", vST)
            p.wait(Tn, "v", vST)
            wT1 = p.op(Tn, "transpose", scb[0][0:8, 0:128], mn8,
                       sb["idf"][:, :], inc="t")
            p.wait(V, "t", wT1)
            vR1 = p.op(V, "tensor_copy", sb["r8"][:, :], scb[0][0:8, 0:128],
                       inc="v")
            p.wait(Tn, "v", vR1)
            for q4 in range(4):
                p.op(Tn, "matmul", scb[q4][:, 0:512], sb["r8"][:, :],
                     sb["bo8"][0:8, q4 * 512:(q4 + 1) * 512],
                     start=True, stop=True, inc="t" if q4 == 3 else None)
            wT2 = p.cnt["t"]
            p.wait(V, "t", wT2)
            for q4 in range(4):
                p.op(V, "tensor_tensor",
                     sb["scsB"][:, q4 * 512:(q4 + 1) * 512],
                     scb[q4][:, 0:512],
                     sb["scsB"][:, q4 * 512:(q4 + 1) * 512], ALU.add,
                     inc="v" if q4 == 3 else None)
            vS2 = p.cnt["v"]
            p.wait(V, "v", vS2)
            p.wait(Sc, "v", vS2)
            p.op(Sc, "activation", sb["wbig"][:, 0:2048], sb["scsB"][:, :],
                 ACT.Exp, bias=0.0, scale=1.0, inc="s")
            iw8 = sb["mx8"][:, 48:56]
            sI = p.op(Sc, "activation", iw8, st8, ACT.Exp, bias=0.0,
                      scale=1.0, inc="s")
            p.wait(V, "s", sI)
            wv = sb["wbig"][:, 0:2048].rearrange("p (j c) -> p j c", c=256)
            p.op(V, "reduce_sum", den0, wv, axis=AX, inc="v")
            p.wait(V, "v")
            p.op(V, "tensor_tensor", den8, den0, iw8, ALU.add, inc="v")
            p.wait(V, "v")
            vD = p.op(V, "reciprocal", rec8, den8, inc="v")
            p.wait(V, "v", vD)
            p.wait(Tn, "v", vD)
            p.op(Tn, "transpose", scb[0][0:8, 0:128], iw8, sb["idf"][:, :])
            wT3 = p.op(Tn, "transpose", scb[0][0:8, 128:256], rec8,
                       sb["idf"][:, :], inc="t")
            p.wait(V, "t", wT3)
            p.op(V, "tensor_copy", sb["r8b"][0:8, 0:128],
                 scb[0][0:8, 0:128])
            vR3 = p.op(V, "tensor_copy", sb["r8b"][0:8, 128:256],
                       scb[0][0:8, 128:256], inc="v")
            p.wait(Tn, "v", vR3)
            for tfl in range(4):
                p.op(Tn, "matmul", scb[0][:, tfl * 128:(tfl + 1) * 128],
                     sb["selh"][0:8, tfl * 128:(tfl + 1) * 128],
                     sb["r8b"][0:8, 0:128], start=True, stop=True)
            for tfl in range(4):
                p.op(Tn, "matmul", scb[1][:, tfl * 128:(tfl + 1) * 128],
                     sb["selh"][0:8, tfl * 128:(tfl + 1) * 128],
                     sb["r8b"][0:8, 128:256], start=True, stop=True)
            p.wait(Tn, "v", last_vW[0])
            for j in range(8):
                for ct in range(2):
                    idx = j * 2 + ct
                    pt = ps["ptA"] if idx < 8 else ps["ptB"]
                    p.op(Tn, "transpose",
                         pt[0:128, (idx % 8) * 128:(idx % 8) * 128 + 128],
                         sb["wbig"][:, j * 256 + ct * 128:
                                    j * 256 + ct * 128 + 128],
                         sb["idb"][:, :],
                         inc="t" if idx == 15 else None)
            wT4 = p.cnt["t"]
            p.wait(V, "t", wT4)
            p.op(V, "tensor_copy", sb["wbig"][:, 2048:3072],
                 ps["ptA"][0:128, 0:1024])
            vW = p.op(V, "tensor_copy", sb["wbig"][:, 3072:4096],
                      ps["ptB"][0:128, 0:1024], inc="v")
            last_vW[0] = vW
            p.wait(Tn, "v", vW)
            for j, hd in enumerate(heads):
                rq = (hd % 2) * 64
                tfl = j // 2
                for ct in range(2):
                    idx = j * 2 + ct
                    p.op(Tn, "matmul",
                         scb[2][rq:rq + 64, tfl * 128:(tfl + 1) * 128],
                         sb["ckx"][:, ct * 1024 + hd * 64:
                                   ct * 1024 + hd * 64 + 64],
                         sb["wbig"][:, 2048 + idx * 128:2048 + idx * 128 + 128],
                         start=(ct == 0), stop=(ct == 1),
                         inc="t" if j == 7 and ct == 1 else None)
            wtO = p.cnt["t"]
            p.wait(V, "t", wtO)
            xkv = sb["scrD"][:, :].rearrange("p (tf c) -> p tf c", c=256)[
                :, 4 * hb:4 * hb + 4, tti * 128:(tti + 1) * 128]
            zv = sb["scrE"][:, :].rearrange("p (tf c) -> p tf c", c=256)[
                :, 4 * hb:4 * hb + 4, tti * 128:(tti + 1) * 128]
            azv = sb["azt"][:, :].rearrange("p (tf c) -> p tf c", c=256)[
                :, 4 * hb:4 * hb + 4, tti * 128:(tti + 1) * 128]
            p.op(V, "tensor_tensor", sb["fstg"][:, :], scb[0][:, 0:512],
                 xkv, ALU.mult, inc="v")
            p.wait(V, "v")
            p.op(V, "tensor_tensor", sb["scsB"][:, 0:512],
                 scb[2][:, 0:512], sb["fstg"][:, :], ALU.add, inc="v")
            p.wait(V, "v")
            p.op(V, "tensor_tensor", sb["fstg"][:, :], sb["scsB"][:, 0:512],
                 scb[1][:, 0:512], ALU.mult, inc="v")
            p.wait(V, "v")
            ev = p.op(V, "tensor_tensor", azv, sb["fstg"][:, :], zv,
                      ALU.mult, inc="v")
            unit_hist.append({"evac_v": ev, "sI": sI})
            qnames = (("pu2", "pu3", "ptAf", "ptBf"),
                      ("pm0", "pm1", "pu0", "pu1"))[nb % 2]
            for kk in qnames:
                evac[kk] = ev
            evac["ptA"] = ev
            evac["ptB"] = ev
        p.wait(Tn, "v", unit_hist[-1]["evac_v"])
        for o in range(8):
            tok = ring_load(dr["woutt"][o])
            pmname = "pu%d" % (2 + o % 2)
            pp = ps[pmname][:, 0:256]
            p.wait(Tn, "v", evac[pmname])
            wsl, slot = ring_use(tok)
            for k in range(8):
                p.op(Tn, "matmul", pp, wsl[:, k * 128:(k + 1) * 128],
                     sb["azt"][:, k * 256:(k + 1) * 256],
                     start=(k == 0), stop=(k == 7),
                     inc="t" if k == 7 else None)
            ring_done(slot)
            p.wait(V, "t")
            evac[pmname] = p.op(V, "tensor_tensor",
                                sb["sreg"][:, o * 2048 + s0:
                                           o * 2048 + s0 + TS],
                                pp, sb["xnhi"][:, o * 2048 + s0:
                                               o * 2048 + s0 + TS],
                                ALU.add, inc="v")

    p.barrier()
    if core_debug:
        tapout("sreg", (128, 16384), BF16)
        p.wait(Sy, "v")
        p.dma(Sy, taps["sreg"], sb["sreg"][:, :], "dO")
        for _e in ENGS:
            p.wait(_e, "dO", p.cnt["dO"])

    w1_tcnt = [0, 0]
    w2_tcnt = [0, 0]
    evac_s = {}
    QUADS = (("pm0", "pm1", "pu0", "pu1"), ("pm0", "pm1", "pu0", "pu1"))
    b1_scnt = [0, 0]
    stg_vcnt = [0, 0]
    db_tok = {0: db0_early}
    for e_i in range(NE):
        bp = e_i % 2
        db = db_tok[e_i]
        if e_i + 1 < NE:
            bpn = (e_i + 1) % 2
            p.wait(Gp, "s", b1_scnt[bpn])
            db_tok[e_i + 1] = p.dma(Gp, sb["b1s"][:, bpn * 32:(bpn + 1) * 32],
                                    dr["b1t"][e_i + 1], "db%d" % bpn)
        for q4 in range(4):
            pmname = "pm%d" % (q4 % 2)
            pp = ps[pmname][:, 0:512]
            p.wait(Tn, "v", evac[pmname])
            p.wait(Tn, "s", evac_s.get(pmname, 0))
            wt = p.op(Tn, "matmul", pp,
                      sb["sel8"][:, e_i * 128:(e_i + 1) * 128],
                      sb["oh"][0:8, q4 * 512:(q4 + 1) * 512],
                      start=True, stop=True, inc="t")
            p.wait(V, "t", wt)
            evac[pmname] = p.op(V, "tensor_copy",
                                sb["scrE"][:, q4 * 512:(q4 + 1) * 512], pp,
                                inc="v")
        mask_ready = p.cnt["v"]
        for q in range(4):
            p.wait(V, "t")
            for ft in range(8):
                wp = ft % 2
                p.wait(Gp, "t", w1_tcnt[wp])
                c1 = p.dma(Gp, sb["w1buf"][:, wp * 1024:(wp + 1) * 1024],
                           dr["w1t"][e_i, q, ft], "dW1%d" % wp)
                w1sl = sb["w1buf"][:, wp * 1024:(wp + 1) * 1024]
                banks = QUADS[ft % 2]
                for bn in banks:
                    p.wait(Tn, "v", evac[bn])
                    p.wait(Tn, "s", evac_s.get(bn, 0))
                p.wait(Tn, "dW1%d" % wp, c1)
                for k in range(8):
                    for s4 in range(4):
                        p.op(Tn, "matmul", ps[banks[s4]][:, 0:512],
                             w1sl[:, k * 128:(k + 1) * 128],
                             sb["xnhi"][:, k * 2048 + s4 * 512:
                                        k * 2048 + (s4 + 1) * 512],
                             start=(k == 0), stop=(k == 7),
                             inc="t" if (k == 7 and s4 == 3) else None)
                wt = p.cnt["t"]
                bcol = sb["b1s"][:, bp * 32 + q * 8 + ft:
                                 bp * 32 + q * 8 + ft + 1]
                p.wait(Sc, "t", wt)
                p.wait(Sc, "db%d" % bp, db)
                for s4 in range(4):
                    sp = s4 % 2
                    stgsl = sb["stg"][:, sp * 512:(sp + 1) * 512]
                    p.wait(Sc, "v", stg_vcnt[sp])
                    ws = p.op(Sc, "activation", stgsl, ps[banks[s4]][:, 0:512],
                              ACT.Gelu, bias=bcol, scale=1.0, inc="s")
                    evac_s[banks[s4]] = ws
                    evac[banks[s4]] = 0
                    p.wait(V, "s", ws)
                    p.wait(V, "v", mask_ready)
                    hsl = sb["big"][:, ft * 2048 + s4 * 512:
                                    ft * 2048 + (s4 + 1) * 512]
                    stg_vcnt[sp] = p.op(V, "tensor_tensor", hsl, stgsl,
                                        sb["scrE"][:, s4 * 512:(s4 + 1) * 512],
                                        ALU.mult, inc="v")
                w1_tcnt[wp] = p.cnt["t"]
            h_ready = p.cnt["v"]
            for d in range(8):
                wp = d % 2
                p.wait(Gp, "t", w2_tcnt[wp])
                c2 = p.dma(Gp, sb["w2buf"][:, wp * 1024:(wp + 1) * 1024],
                           dr["w2t"][e_i, d, q], "dW2%d" % wp)
                w2sl = sb["w2buf"][:, wp * 1024:(wp + 1) * 1024]
                banks = QUADS[d % 2]
                for bn in banks:
                    p.wait(Tn, "v", evac[bn])
                    p.wait(Tn, "s", evac_s.get(bn, 0))
                p.wait(Tn, "dW2%d" % wp, c2)
                p.wait(Tn, "v", h_ready)
                last = (e_i == NE - 1 and q == 3)
                for k in range(8):
                    for s4 in range(4):
                        p.op(Tn, "matmul", ps[banks[s4]][:, 0:512],
                             w2sl[:, k * 128:(k + 1) * 128],
                             sb["big"][:, k * 2048 + s4 * 512:
                                       k * 2048 + (s4 + 1) * 512],
                             start=(k == 0), stop=(k == 7 and not last),
                             inc="t" if (k == 7 and s4 == 3 and not last)
                             else None)
                if last:
                    for s4 in range(4):
                        p.op(Tn, "matmul", ps[banks[s4]][:, 0:512],
                             sb["b2hi"][0:8, d * 128:(d + 1) * 128],
                             sb["oh"][0:8, s4 * 512:(s4 + 1) * 512],
                             start=False, stop=False)
                        p.op(Tn, "matmul", ps[banks[s4]][:, 0:512],
                             sb["b2lo"][0:8, d * 128:(d + 1) * 128],
                             sb["oh"][0:8, s4 * 512:(s4 + 1) * 512],
                             start=False, stop=True,
                             inc="t" if s4 == 3 else None)
                wt = p.cnt["t"]
                p.wait(V, "t", wt)
                for s4 in range(4):
                    osl = sb["offn"][:, d * 2048 + s4 * 512:
                                     d * 2048 + (s4 + 1) * 512]
                    if e_i == 0 and q == 0:
                        ev = p.op(V, "tensor_copy", osl,
                                  ps[banks[s4]][:, 0:512], inc="v")
                    else:
                        ev = p.op(V, "tensor_tensor", osl,
                                  ps[banks[s4]][:, 0:512], osl, ALU.add,
                                  inc="v")
                    evac[banks[s4]] = ev
                w2_tcnt[wp] = p.cnt["t"]
        b1_scnt[bp] = p.cnt["s"]

    p.barrier()
    if core_debug:
        tapout("offn", (128, 16384), BF16)
        p.wait(Sy, "v")
        p.dma(Sy, taps["offn"], sb["offn"][:, :], "dO")
        for _e in ENGS:
            p.wait(_e, "dO", p.cnt["dO"])

    xrl_used = [0, 0]
    ot_stored = [0, 0]
    for tt in range(NTT):
        par = tt % 2
        xr = sb["xrl"][:, par * 1024:(par + 1) * 1024]
        ot = sb["otile"][:, par * 1024:(par + 1) * 1024]
        p.wait(Sy, "v", xrl_used[par])
        dx = p.dma(Sy, xr, dr["xs"][tt * 128:(tt + 1) * 128, :], "dX%d" % par)
        p.wait(Tn, "v", evac["pu3"])
        wtp = p.op(Tn, "transpose", ps["pu3"][0:1, 128:256],
                   sb["pmcols"][:, tt:tt + 1], sb["idf"][:, :], inc="t")
        p.wait(V, "t", wtp)
        p.wait(V, "t", rowscr_t[0])
        rsc = p.op(V, "tensor_copy", sb["rowscr"][0:1, 0:128],
                   ps["pu3"][0:1, 128:256], inc="v")
        p.wait(Tn, "v", rsc)
        wt = p.op(Tn, "matmul", ps["pu3"][:, 0:128], sb["onesf"][0:1, 0:128],
                  sb["rowscr"][0:1, 0:128], start=True, stop=True, inc="t")
        rowscr_t[0] = wt
        ptn = "ptA" if par == 0 else "ptB"
        pt = ps[ptn]
        p.wait(Tn, "v", evac[ptn])
        last_tp = 0
        for d in range(8):
            dsl = slice(d * 2048 + tt * 128, d * 2048 + tt * 128 + 128)
            p.wait(V, "t", wt)
            p.op(V, "tensor_tensor", sb["fstg"][:, 0:128],
                 sb["offn"][:, dsl], ps["pu3"][:, 0:128], ALU.mult, inc="v")
            p.wait(Sc, "v")
            ws = p.op(Sc, "activation", sb["stg"][:, 0:128],
                      sb["fstg"][:, 0:128], ACT.Silu, inc="s")
            p.wait(V, "s", ws)
            p.wait(V, "t", last_tp)
            rv = p.op(V, "tensor_tensor", sb["tmpl"][:, 0:128],
                      sb["stg"][:, 0:128], sb["sreg"][:, dsl], ALU.mult,
                      inc="v")
            p.wait(Tn, "v", rv)
            p.op(Tn, "transpose", pt[0:128, d * 128:(d + 1) * 128],
                 sb["tmpl"][:, 0:128], sb["idb"][:, :], inc="t")
            last_tp = p.cnt["t"]
        evac["pu3"] = p.cnt["v"]
        wt2 = p.cnt["t"]
        p.wait(V, "t", wt2)
        p.wait(V, "dX%d" % par, dx)
        addv = p.op(V, "tensor_tensor", ot, xr, pt[0:128, 0:1024], ALU.add,
                    inc="v")
        evac[ptn] = addv
        evac["pu3"] = addv
        xrl_used[par] = addv
        p.wait(Sy, "v", addv)
        ot_stored[par] = p.dma(Sy, y[tt * 128:(tt + 1) * 128, :], ot, "dO")

    p.wait(Sy, "dO", p.cnt["dO"])
    p.build()
    return nc, taps


def _prep_inputs(L0_attn_hidden_states, L0_attn_rms_weight, L0_attn_W_in,
                 L0_attn_gate_bias, L0_attn_W_out, L0_ffn_Wr, L0_ffn_W1,
                 L0_ffn_b1, L0_ffn_W2, L0_ffn_b2):
    x2 = np.ascontiguousarray(
        np.asarray(L0_attn_hidden_states, np.float32).reshape(16384, D))
    W_in = np.asarray(L0_attn_W_in, np.float32)
    W_out = np.asarray(L0_attn_W_out, np.float32)
    Wr = np.asarray(L0_ffn_Wr, np.float32)
    W1 = np.asarray(L0_ffn_W1, np.float32)
    W2 = np.asarray(L0_ffn_W2, np.float32)
    b1 = np.asarray(L0_ffn_b1, np.float32)
    b2 = np.asarray(L0_ffn_b2, np.float32)
    rms = np.asarray(L0_attn_rms_weight, np.float32)
    gb = np.asarray(L0_attn_gate_bias, np.float32)

    com = {}
    com["xsum"] = np.ascontiguousarray(x2[0::CHUNK])
    com["wint"] = np.ascontiguousarray(
        W_in.reshape(8, 128, 26, 128).transpose(2, 1, 0, 3)
        .reshape(26, 128, 1024))
    com["woutt"] = np.ascontiguousarray(
        W_out.reshape(8, 128, 8, 128).transpose(2, 1, 0, 3)
        .reshape(8, 128, 1024))
    com["wrt"] = np.ascontiguousarray(
        Wr.reshape(8, 128, 8).transpose(1, 0, 2).reshape(128, 64))
    com["w1t"] = np.ascontiguousarray(
        W1.reshape(NE, 8, 128, 32, 128).transpose(0, 3, 2, 1, 4)
        .reshape(NE, 4, 8, 128, 1024))
    com["w2t"] = np.ascontiguousarray(
        W2.reshape(NE, 4, 8, 128, 8, 128).transpose(0, 4, 1, 3, 2, 5)
        .reshape(NE, 8, 4, 128, 1024))
    com["b1t"] = np.ascontiguousarray(
        b1.reshape(NE, 32, 128).transpose(0, 2, 1))
    com["b2m"] = b2
    com["rmswt"] = np.ascontiguousarray(rms.reshape(8, 128).T)
    com["gbt"] = np.ascontiguousarray(gb.reshape(8, 128).T)
    com["ubh"] = np.zeros((2, 128), BF)
    com["ubh"][0, 0:64] = 1
    com["ubh"][1, 64:128] = 1
    com["idfh"] = np.eye(128, dtype=np.float32)
    com["idbh"] = np.eye(128, dtype=np.float32).astype(BF)
    com["onesfh"] = np.ones((1, 256), np.float32)
    com["onesbh"] = np.ones((1, 128), BF)
    selb = np.zeros((128, 2), BF)
    selb[0:64, 0] = 1
    selb[64:128, 1] = 1
    com["selbh"] = selb
    sel8 = np.zeros((8, 1024), BF)
    for e in range(8):
        sel8[e, e * 128:(e + 1) * 128] = 1
    com["sel8h"] = sel8
    com["iota8h"] = np.tile(np.arange(8, dtype=np.float32), (128, 1))
    bo8 = np.zeros((8, 2048), np.float32)
    for h in range(8):
        bo8[h, h * 256:(h + 1) * 256] = 1.0
    com["bo8h"] = bo8
    selh = np.zeros((8, 512), np.float32)
    for tfl in range(4):
        for pp_ in range(128):
            selh[2 * tfl + pp_ // 64, tfl * 128 + pp_] = 1.0
    com["selhh"] = selh

    in_maps = []
    for c in range(NCORES):
        m = dict(com)
        m["xs"] = np.ascontiguousarray(x2[c * T:(c + 1) * T])
        msb = np.zeros((2, 16 * 256), np.float32)
        for tt in range(16):
            c0 = (c * T + tt * 128) // CHUNK
            cc = np.arange(NCH)
            msb[0, tt * 256:(tt + 1) * 256] = np.where(cc >= c0, NEG, 0.0)
            msb[1, tt * 256:(tt + 1) * 256] = np.where(cc >= c0 + 1, NEG, 0.0)
        m["msbh"] = msb.astype(BF)
        in_maps.append(m)
    return in_maps


_NC_CACHE = {}


def kernel(**inputs):
    in_maps = _prep_inputs(**inputs)
    if "nc" not in _NC_CACHE:
        _NC_CACHE["nc"] = build(core_debug=False)[0]
    nc = _NC_CACHE["nc"]
    res = run_bass_kernel_spmd(nc, in_maps, core_ids=list(range(NCORES)))
    out = np.concatenate([np.asarray(r["y"]) for r in res.results], axis=0)
    return out.reshape(1, 16384, D)


# version 11
# speedup vs baseline: 1.1805x; 1.1805x over previous
import numpy as np
import ml_dtypes
import concourse.bass as bass
import concourse.mybir as mybir
from concourse.bass_utils import run_bass_kernel_spmd
from contextlib import ExitStack

F32 = mybir.dt.float32
BF16 = mybir.dt.bfloat16
BF = ml_dtypes.bfloat16
AX = mybir.AxisListType.X
ALU = mybir.AluOpType
ACT = mybir.ActivationFunctionType

NCORES = 8
T = 2048
D = 1024
DH = 64
NHEAD = 16
CHUNK = 64
NCH = 256
FH = 4096
NE = 8
NEG = np.float32(-1e30)
TS = 256
NSUB = T // TS
NTT = T // 128

ENGS = ("sync", "gpsimd", "vector", "scalar", "tensor")
V, Sc, Tn, Sy, Gp = "vector", "scalar", "tensor", "sync", "gpsimd"

BASE = 20480
SB = {}
_off = [BASE]
def _reg(name, nbytes, align=64):
    _off[0] = (_off[0] + align - 1) // align * align
    SB[name] = _off[0]
    _off[0] += nbytes

_reg("xnhi", 32768)
_reg("sreg", 32768)
_reg("big", 32768)
_reg("wring", 4096)
_reg("w1buf", 4096)
_reg("w2buf", 4096)
_reg("scrA", 8192)
_reg("scrB", 8192)
_reg("scrC", 8192)
_reg("scrD", 8192)
_reg("scrE", 4096)
_reg("msb", 8192)
_reg("ckk", 4096)
_reg("ckx", 4096)
_reg("oh", 4096)
_reg("rowscr", 1536)
_reg("pmcols", 64)
_reg("sel8", 2048)
_reg("b2hi", 2048)
_reg("b2lo", 2048)
_reg("idf", 512)
_reg("idb", 256)
_reg("onesf", 1024)
_reg("onesb", 256)
_reg("selb", 32)
_reg("ub", 256)
_reg("wrf", 256)
_reg("wrhib", 128)
_reg("wrlob", 128)
_reg("gbs", 32)
_reg("wrmsfm", 32)
_reg("b1s", 256)
_reg("qf", 1024)
_reg("qb", 512)
_reg("kf", 1024)
_reg("ilse", 128)
_reg("nilse", 128)
_reg("stat", 256)
_reg("tmph", 1024)
_reg("tmpl", 512)
_reg("stg", 2048)
_reg("fstg", 2048)
_reg("istg", 512)
_reg("istg2", 512)
_reg("r8", 512)
_reg("lgt", 128)
_reg("ohtm", 64)
_reg("mx8", 256)
_reg("repsb", 1024)
_reg("hsub", 4096)
_reg("rrmscols", 64)
_reg("warmt", 64)
_reg("wbias", 64)
_reg("iota8", 32)
_reg("idxu", 64)
_reg("bo8", 8192)
_reg("selh", 2048)
_reg("r8b", 1024)
_reg("statln", 256)
_reg("mrepsb", 512)
_reg("rrepsb", 512)
_reg("rrrepsb", 512)
assert _off[0] <= 229376, _off[0]


class Prog:

    def __init__(self, nc):
        self.nc = nc
        self.q = {e: [] for e in ENGS}
        self.sem = {}
        self.cnt = {}
        self.ctx = ExitStack()

    def new_sem(self, name):
        self.sem[name] = self.ctx.enter_context(self.nc.semaphore(name))
        self.cnt[name] = 0

    def op(self, eng, meth, *args, inc=None, n=1, **kw):
        if inc is not None:
            self.cnt[inc] += n
        self.q[eng].append((meth, args, kw, inc, n))
        return self.cnt[inc] if inc else None

    def dma(self, eng, out_ap, in_ap, inc):
        return self.op(eng, "dma_start", inc=inc, n=16, out=out_ap, in_=in_ap)

    def wait(self, eng, sem, val=None):
        v = self.cnt[sem] if val is None else val
        if v > 0:
            self.q[eng].append(("__wait__", (sem, v), {}, None, 0))

    def barrier(self, engs=ENGS, sems=("v", "s", "t")):
        vals = [(sm, self.cnt[sm]) for sm in sems]
        for e in engs:
            for sm, v in vals:
                self.wait(e, sm, v)

    def build(self):
        block = self.ctx.enter_context(self.nc.Block())
        for eng in ENGS:
            items = self.q[eng]
            def body(e, items=items):
                for meth, args, kw, inc, n in items:
                    if meth == "__wait__":
                        e.wait_ge(self.sem[args[0]], args[1])
                    else:
                        ins = getattr(e, meth)(*args, **kw)
                        if inc is not None:
                            ins.then_inc(self.sem[inc], n)
            getattr(block, eng)(body)
        self.ctx.close()


def build(core_debug=False):
    nc = bass.Bass("TRN2", target_bir_lowering=False,
                   detect_race_conditions=False)

    dr = {}
    def din(name, shape, dt=F32):
        dr[name] = nc.dram_tensor(name, list(shape), dt,
                                  kind="ExternalInput").ap()
    din("xs", (T, D))
    din("xsum", (NCH, D))
    din("wint", (26, 128, 1024))
    din("woutt", (8, 128, 1024))
    din("wrt", (128, 64))
    din("w1t", (NE, 4, 8, 128, 1024))
    din("w2t", (NE, 8, 4, 128, 1024))
    din("b1t", (NE, 128, 32))
    din("b2m", (NE, D))
    din("rmswt", (128, 8))
    din("gbt", (128, 8))
    din("msbh", (2, 16 * 256), BF16)
    din("ubh", (2, 128), BF16)
    din("idfh", (128, 128))
    din("idbh", (128, 128), BF16)
    din("onesfh", (1, 256))
    din("onesbh", (1, 128), BF16)
    din("selbh", (128, 2), BF16)
    din("sel8h", (8, 1024), BF16)
    din("iota8h", (128, 8))
    din("bo8h", (8, 2048))
    din("selhh", (8, 512))
    y = nc.dram_tensor("y", [T, D], F32, kind="ExternalOutput").ap()

    taps = {}
    def tapout(name, shape, dt=F32):
        taps[name] = nc.dram_tensor("tap_" + name, list(shape), dt,
                                    kind="ExternalOutput").ap()

    sb = {}
    def S_(name, shape, dt, key=None, extra=0):
        sb[name] = nc.alloc_sbuf_tensor_at(
            name, list(shape), dt, offset=SB[key or name] + extra).ap()
    S_("xnhi", (128, 16384), BF16)
    S_("xrl", (128, 2048), F32, "xnhi")
    S_("otile", (128, 2048), F32, "xnhi", 8192)
    S_("sreg", (128, 16384), BF16)
    S_("big", (128, 16384), BF16)
    S_("wring", (128, 2048), BF16)
    S_("w1buf", (128, 2048), BF16)
    S_("w2buf", (128, 2048), BF16)
    S_("scrA", (128, 2048), F32)
    S_("xa", (128, 2048), F32, "scrA")
    S_("azt", (128, 2048), BF16, "scrA")
    S_("scrB", (128, 2048), F32)
    S_("scrC", (128, 2048), F32)
    S_("scrD", (128, 2048), F32)
    S_("cxbf", (128, 2048), BF16, "scrD")
    S_("offn", (128, 16384), BF16, "scrA")
    S_("scrE", (128, 2048), BF16)
    S_("msb", (2, 4096), BF16)
    S_("ckk", (128, 2048), BF16)
    S_("ckx", (128, 2048), BF16)
    S_("rowscr", (1, 384), F32)
    S_("pmcols", (128, 16), F32)
    S_("sel8", (8, 1024), BF16)
    S_("oh", (8, 2048), BF16)
    S_("b2hi", (8, 1024), BF16)
    S_("b2lo", (8, 1024), BF16)
    S_("b2f32", (8, 1024), F32, "scrB")
    S_("idf", (128, 128), F32)
    S_("idb", (128, 128), BF16)
    S_("onesf", (1, 256), F32)
    S_("onesb", (1, 128), BF16)
    S_("selb", (128, 2), BF16)
    S_("ub", (2, 128), BF16)
    S_("wrf", (128, 64), F32)
    S_("wrhib", (128, 64), BF16)
    S_("wrlob", (128, 64), BF16)
    S_("gbs", (128, 8), F32)
    S_("wrmsfm", (128, 8), F32)
    S_("b1s", (128, 64), F32)
    S_("qf", (128, 256), F32)
    S_("qb", (128, 256), BF16)
    S_("kf", (128, 256), F32)
    S_("ilse", (128, 32), F32)
    S_("nilse", (128, 32), F32)
    S_("stat", (128, 64), F32)
    S_("tmph", (128, 512), BF16)
    S_("tmpl", (128, 256), BF16)
    S_("stg", (128, 1024), BF16)
    S_("fstg", (128, 512), F32)
    S_("istg", (128, 128), F32)
    S_("istg2", (128, 128), F32)
    S_("xloT", (128, 128), BF16, "istg")
    S_("r8", (8, 128), F32)
    S_("lgt", (128, 16), F32)
    S_("ohtm", (128, 16), BF16)
    S_("mx8", (128, 64), F32)
    S_("repsb", (128, 256), F32)
    S_("hsub", (128, 2048), BF16)
    S_("rrmscols", (128, 16), F32)
    S_("warmt", (128, 16), F32)
    S_("wbias", (128, 1), F32)
    S_("dbgln", (128, 512), F32, "scrD")
    S_("dbglg", (128, 128), F32, "scrD", 2048)
    S_("iota8", (128, 8), F32)
    S_("idxu", (128, 16), mybir.dt.uint32)
    S_("bo8", (8, 2048), F32)
    S_("selh", (8, 512), F32)
    S_("r8b", (8, 256), F32)
    S_("scsB", (128, 2048), F32, "scrB")
    S_("wbig", (128, 4096), BF16, "scrC")
    S_("statln", (128, 64), F32)
    S_("mrepsb", (128, 128), F32)
    S_("rrepsb", (128, 128), F32)
    S_("rrrepsb", (128, 128), F32)

    p = Prog(nc)
    ps = {}
    for nm, shape, dt in [
        ("pm0", (128, 512), F32), ("pm1", (128, 512), F32),
        ("pu0", (128, 512), F32), ("pu1", (128, 512), F32),
        ("ptA", (128, 1024), BF16), ("ptB", (128, 1024), BF16),
        ("pu2", (128, 512), F32), ("pu3", (128, 512), F32),
    ]:
        ps[nm] = p.ctx.enter_context(nc.psum_tensor(nm, list(shape), dt)).ap()

    for s in ("dL", "dX0", "dX1", "dR0", "dR1", "dW10", "dW11", "dW20",
              "dW21", "db0", "db1", "dO", "v", "s", "t"):
        p.new_sem(s)
    ps["ptAf"] = ps["ptA"].bitcast(F32)
    ps["ptBf"] = ps["ptB"].bitcast(F32)

    p.dma(Sy, sb["msb"][:, :], dr["msbh"], "dL")
    p.dma(Sy, sb["ub"][:, :], dr["ubh"], "dL")
    p.dma(Sy, sb["idf"][:, :], dr["idfh"], "dL")
    p.dma(Sy, sb["idb"][:, :], dr["idbh"], "dL")
    p.dma(Sy, sb["onesf"][:, :], dr["onesfh"], "dL")
    p.dma(Sy, sb["onesb"][:, :], dr["onesbh"], "dL")
    p.dma(Sy, sb["selb"][:, :], dr["selbh"], "dL")
    p.dma(Sy, sb["sel8"][:, :], dr["sel8h"], "dL")
    p.dma(Sy, sb["iota8"][:, :], dr["iota8h"], "dL")
    p.dma(Sy, sb["bo8"][:, :], dr["bo8h"], "dL")
    p.dma(Sy, sb["selh"][:, :], dr["selhh"], "dL")
    p.dma(Sy, sb["wrf"][:, :], dr["wrt"], "dL")
    p.dma(Sy, sb["gbs"][:, :], dr["gbt"], "dL")
    p.dma(Sy, sb["wrmsfm"][:, :], dr["rmswt"], "dL")
    p.dma(Sy, sb["b2f32"][:, :], dr["b2m"], "dL")
    cload = p.cnt["dL"]
    db0_early = p.dma(Gp, sb["b1s"][:, 0:32], dr["b1t"][0], "db0")

    p.wait(V, "dL", cload)
    p.op(V, "memset", sb["wbias"][:, :], 0.25)
    p.op(V, "memset", sb["statln"][:, :], 0.0)
    p.op(V, "memset", sb["mx8"][:, :], 0.0)
    p.op(V, "reduce_max", sb["warmt"][:, 0:1], sb["wbias"][:, :], axis=AX,
         inc="v")
    p.wait(V, "v")
    p.op(V, "tensor_scalar", sb["warmt"][:, 1:2], sb["wbias"][:, :],
         sb["wbias"][:, :], 2.0, ALU.subtract, ALU.mult)
    p.op(V, "scalar_tensor_tensor", sb["warmt"][:, 1:2], sb["wbias"][:, :],
         sb["wbias"][:, :], sb["wbias"][:, :], ALU.mult, ALU.add)
    p.op(V, "tensor_scalar", sb["warmt"][:, 1:2], sb["wbias"][:, :],
         sb["wbias"][:, :], None, ALU.is_equal, inc="v")
    p.wait(Sc, "v")
    for fn in (ACT.Exp, ACT.Gelu, ACT.Sigmoid, ACT.Silu, ACT.Square,
               ACT.Sqrt, ACT.Identity):
        p.op(Sc, "activation", sb["warmt"][:, 2:3], sb["wbias"][:, :], fn,
             bias=sb["wbias"][:, :], scale=1.0, accum_out=sb["warmt"][:, 3:4])
    p.op(Sc, "nop", inc="s")
    p.op(V, "tensor_copy", sb["wrhib"][:, :], sb["wrf"][:, :], inc="v")
    p.wait(V, "v")
    p.op(V, "tensor_tensor", sb["wrlob"][:, :], sb["wrf"][:, :],
         sb["wrhib"][:, :], ALU.subtract)
    p.op(V, "tensor_copy", sb["b2hi"][:, :], sb["b2f32"][:, :], inc="v")
    p.wait(V, "v")
    p.op(V, "tensor_tensor", sb["b2lo"][:, :], sb["b2f32"][:, :],
         sb["b2hi"][:, :], ALU.subtract, inc="v")

    evac = {k: 0 for k in ("pm0", "pm1", "pu0", "pu1", "pu2", "pu3",
                           "ptA", "ptB", "ptAf", "ptBf")}
    xa_done = [[], []]
    rowscr_t = [0]

    def ln_stats(src, dump, so):
        sl = lambda i: sb["statln"][:, so + i:so + i + 1]

        def fv(*args, **kw):
            c = p.op(V, *args, inc="v", **kw)
            p.wait(V, "v", c)
            return c

        fv("reduce_sum", sl(0), src, axis=AX)
        p.wait(Sc, "v")
        p.op(Sc, "activation", dump, src, ACT.Square, bias=0.0, scale=1.0,
             inc="s")
        fv("tensor_scalar", sl(2), sl(0), 1.0 / D, None, ALU.mult)
        fv("tensor_tensor", sl(3), sl(2), sl(2), ALU.mult)
        p.wait(V, "s")
        fv("reduce_sum", sl(1), dump, axis=AX)
        fv("tensor_scalar", sl(4), sl(1), 1.0 / D, None, ALU.mult)
        fv("tensor_tensor", sl(5), sl(4), sl(3), ALU.subtract)

        def newton(tin, eps, b):
            fv("tensor_scalar", sl(b), tin, 1.0, eps, ALU.mult, ALU.add)
            p.wait(Sc, "v")
            p.op(Sc, "activation", sl(b + 1), sl(b), ACT.Sqrt, inc="s")
            p.wait(V, "s")
            fv("reciprocal", sl(b + 2), sl(b + 1))
            fv("tensor_tensor", sl(b + 3), sl(b + 2), sl(b + 2), ALU.mult)
            fv("tensor_tensor", sl(b + 4), sl(b), sl(b + 3), ALU.mult)
            fv("tensor_scalar", sl(b + 5), sl(b + 4), -0.5, 1.5, ALU.mult,
               ALU.add)
            fv("tensor_tensor", sl(b + 6), sl(b + 2), sl(b + 5), ALU.mult)
            return sl(b + 6)

        rstd = newton(sl(5), 1e-5, 8)
        fv("tensor_scalar", sl(16), sl(2), -1.0, None, ALU.mult)
        fv("tensor_tensor", sl(17), rstd, rstd, ALU.mult)
        fv("tensor_tensor", sl(18), sl(5), sl(17), ALU.mult)
        rrms = newton(sl(18), 1e-6, 20)
        return sl(16), rstd, rrms

    rowps = lambda: ps["pu2"][0:1, 128:256]
    repps = lambda: ps["pu2"][:, 0:128]

    def make_rep(col, rep_sb):
        p.wait(Tn, "v")
        p.wait(Tn, "v", evac["pu2"])
        wt = p.op(Tn, "transpose", rowps(), col, sb["idf"][:, :], inc="t")
        p.wait(V, "t", wt)
        p.wait(V, "t", rowscr_t[0])
        rs = p.op(V, "tensor_copy", sb["rowscr"][0:1, 0:128], rowps(),
                  inc="v")
        p.wait(Tn, "v", rs)
        wb = p.op(Tn, "matmul", repps(), sb["onesf"][0:1, 0:128],
                  sb["rowscr"][0:1, 0:128], start=True, stop=True, inc="t")
        rowscr_t[0] = wb
        p.wait(V, "t", wb)
        ev = p.op(V, "tensor_copy", rep_sb, repps(), inc="v")
        evac["pu2"] = ev
        return ev

    hs_guard = [0, 0]
    for tt in range(NTT):
        par = tt % 2
        so = 32 * par
        xa_sl = sb["xa"][:, par * 1024:(par + 1) * 1024]
        for semn, vv in xa_done[par]:
            p.wait(Sy, semn, vv)
        dx = p.dma(Sy, xa_sl, dr["xs"][tt * 128:(tt + 1) * 128, :],
                   "dX%d" % par)
        p.wait(V, "dX%d" % par, dx)
        p.wait(Sc, "dX%d" % par, dx)
        dump = sb["scrB"][:, par * 1024:(par + 1) * 1024]
        negm, rstd, rrms = ln_stats(xa_sl, dump, so)
        if core_debug:
            p.op(V, "tensor_copy", sb["dbgln"][:, tt * 32:(tt + 1) * 32],
                 sb["statln"][:, so:so + 32])
        p.op(V, "tensor_copy", sb["rrmscols"][:, tt:tt + 1], rrms, inc="v")
        make_rep(negm, sb["mrepsb"][:, :])
        make_rep(rstd, sb["rrepsb"][:, :])

        for k in range(8):
            pmname = "pm%d" % (k % 2)
            fx = ps[pmname][:, 0:128]
            p.wait(Tn, "v", evac[pmname])
            wt = p.op(Tn, "transpose", fx, xa_sl[:, k * 128:(k + 1) * 128],
                      sb["idf"][:, :], inc="t")
            fsl = sb["fstg"][:, (k % 2) * 128:(k % 2) * 128 + 128]
            p.wait(V, "t", wt)
            p.wait(V, "s", hs_guard[k % 2])
            p.op(V, "tensor_tensor", fsl, fx, sb["mrepsb"][:, :], ALU.add,
                 inc="v")
            evac[pmname] = p.cnt["v"]
            p.wait(V, "v")
            p.op(V, "tensor_tensor", sb["istg2"][:, :], fsl,
                 sb["rrepsb"][:, :], ALU.mult, inc="v")
            p.wait(V, "v")
            hi_sl = sb["xnhi"][:, k * 2048 + tt * 128:k * 2048 + tt * 128 + 128]
            p.op(V, "tensor_copy", hi_sl, sb["istg2"][:, :], inc="v")
            p.wait(V, "v")
            lo_ready = p.op(V, "tensor_tensor", sb["xloT"][:, :],
                            sb["istg2"][:, :], hi_sl, ALU.subtract, inc="v")
            p.wait(Tn, "v", lo_ready)
            if k == 0:
                p.wait(Tn, "v", evac["pu1"])
            p.op(Tn, "matmul", ps["pu1"][0:8, 0:128],
                 sb["wrhib"][:, k * 8:(k + 1) * 8], hi_sl,
                 start=(k == 0), stop=False)
            p.op(Tn, "matmul", ps["pu1"][0:8, 0:128],
                 sb["wrhib"][:, k * 8:(k + 1) * 8], sb["xloT"][:, :],
                 start=False, stop=False)
            rmm = p.op(Tn, "matmul", ps["pu1"][0:8, 0:128],
                       sb["wrlob"][:, k * 8:(k + 1) * 8], hi_sl,
                       start=False, stop=(k == 7), inc="t")
            p.wait(V, "t", rmm)
        xa_done[par] = [("v", p.cnt["v"]), ("s", p.cnt["s"]),
                        ("t", p.cnt["t"])]

        p.wait(V, "t")
        ev = p.op(V, "tensor_copy", sb["r8"][:, :], ps["pu1"][0:8, 0:128],
                  inc="v")
        evac["pu1"] = ev
        p.wait(Tn, "v", ev)
        p.wait(Tn, "v", evac["pu0"])
        wt = p.op(Tn, "transpose", ps["pu0"][0:128, 0:8], sb["r8"][:, :],
                  sb["idf"][0:8, 0:8], inc="t")
        lg = sb["lgt"][:, par * 8:(par + 1) * 8]
        ohc = sb["ohtm"][:, par * 8:(par + 1) * 8]
        m8 = lambda i: sb["mx8"][:, 32 * par + i:32 * par + i + 1]
        vmax8 = sb["mx8"][:, 32 * par + 8:32 * par + 16]
        vidx8 = sb["mx8"][:, 32 * par + 16:32 * par + 24]
        p.wait(V, "t", wt)
        evac["pu0"] = p.op(V, "tensor_copy", lg, ps["pu0"][:, 0:8], inc="v")
        p.wait(V, "v")
        if core_debug:
            p.op(V, "tensor_copy", sb["dbglg"][:, tt * 8:(tt + 1) * 8], lg)
        p.op(V, "max", vmax8, lg, inc="v")
        p.wait(V, "v")
        idxu = sb["idxu"][:, par * 8:(par + 1) * 8]
        p.op(V, "max_index", idxu, vmax8, lg, inc="v")
        p.wait(V, "v")
        mxi = p.op(V, "tensor_copy", vidx8, idxu, inc="v")
        p.wait(V, "v", mxi)
        p.op(V, "tensor_copy", m8(0), vmax8[:, 0:1], inc="v")
        p.wait(Tn, "v", mxi)
        p.wait(Tn, "v", evac["pu3"])
        w1 = p.op(Tn, "transpose", ps["pu3"][0:1, 0:128],
                  vidx8[:, 0:1], sb["idf"][:, :], inc="t")
        p.wait(V, "t", w1)
        p.wait(V, "t", rowscr_t[0])
        rs = p.op(V, "tensor_copy", sb["rowscr"][0:1, 0:128],
                  ps["pu3"][0:1, 0:128], inc="v")
        p.wait(Tn, "v", rs)
        w2 = p.op(Tn, "matmul", ps["pu3"][:, 128:136],
                  sb["rowscr"][0:1, 0:128], sb["onesf"][0:1, 0:8],
                  start=True, stop=True, inc="t")
        rowscr_t[0] = w2
        p.wait(V, "t", w2)
        oh_v = p.op(V, "tensor_tensor", ohc, sb["iota8"][:, :],
                    ps["pu3"][:, 128:136], ALU.is_equal, inc="v")
        evac["pu3"] = oh_v
        p.wait(Sc, "v")
        p.op(Sc, "activation", sb["warmt"][:, 8:16], lg, ACT.Exp, bias=0.0,
             scale=1.0, inc="s")
        ws2 = p.op(Sc, "activation", m8(4), m8(0), ACT.Exp, bias=0.0,
                   scale=1.0, inc="s")
        p.wait(V, "s", ws2)
        p.op(V, "reduce_sum", m8(2), sb["warmt"][:, 8:16], axis=AX, inc="v")
        p.wait(V, "v")
        p.op(V, "reciprocal", m8(5), m8(2), inc="v")
        p.wait(V, "v")
        pm_ready = p.op(V, "tensor_tensor", sb["pmcols"][:, tt:tt + 1],
                        m8(4), m8(5), ALU.mult, inc="v")
        p.wait(Tn, "v", oh_v)
        p.wait(Tn, "v", evac["ptA"])
        wt2 = p.op(Tn, "transpose", ps["ptA"][0:8, 0:128], ohc,
                   sb["idb"][:, :], inc="t")
        p.wait(V, "t", wt2)
        evac["ptA"] = p.op(V, "tensor_copy",
                           sb["oh"][0:8, tt * 128:(tt + 1) * 128],
                           ps["ptA"][0:8, 0:128], inc="v")

    p.barrier()
    if core_debug:
        tapout("xnhi", (128, 16384), BF16)
        tapout("pmcols", (128, 16))
        tapout("oh", (8, 2048), BF16)
        tapout("dbgln", (128, 512))
        tapout("dbglg", (128, 128))
        p.wait(Sy, "v")
        p.dma(Sy, taps["dbgln"], sb["dbgln"][:, :], "dO")
        p.dma(Sy, taps["dbglg"], sb["dbglg"][:, :], "dO")
        p.dma(Sy, taps["xnhi"], sb["xnhi"][:, :], "dO")
        p.dma(Sy, taps["pmcols"], sb["pmcols"][:, :], "dO")
        p.dma(Sy, taps["oh"], sb["oh"][:, :], "dO")
        for _e in ENGS:
            p.wait(_e, "dO", p.cnt["dO"])

    ring_state = {"n": 0, "tcnt": [0, 0]}

    def ring_load(src_ap):
        j = ring_state["n"]
        slot = j % 2
        p.wait(Gp, "t", ring_state["tcnt"][slot])
        c = p.dma(Gp, sb["wring"][:, slot * 1024:(slot + 1) * 1024], src_ap,
                  "dR%d" % slot)
        ring_state["n"] = j + 1
        return ("dR%d" % slot, c, slot)

    def ring_use(tok):
        semn, c, slot = tok
        p.wait(Tn, semn, c)
        return sb["wring"][:, slot * 1024:(slot + 1) * 1024], slot

    def ring_done(slot):
        ring_state["tcnt"][slot] = p.cnt["t"]

    for st2 in range(2):
        par = st2
        so = 32 * par
        xa_sl = sb["xa"][:, par * 1024:(par + 1) * 1024]
        for semn, vv in xa_done[par]:
            p.wait(Sy, semn, vv)
        dx = p.dma(Sy, xa_sl, dr["xsum"][st2 * 128:(st2 + 1) * 128, :],
                   "dX%d" % par)
        p.wait(V, "dX%d" % par, dx)
        p.wait(Sc, "dX%d" % par, dx)
        dump = sb["scrB"][:, par * 1024:(par + 1) * 1024]
        negm, rstd, rrms = ln_stats(xa_sl, dump, so)
        make_rep(negm, sb["mrepsb"][:, :])
        make_rep(rstd, sb["rrepsb"][:, :])
        make_rep(rrms, sb["rrrepsb"][:, :])
        p.wait(V, "v")
        for k in range(8):
            pmname = "pm%d" % (k % 2)
            fx = ps[pmname][:, 0:128]
            p.wait(Tn, "v", evac[pmname])
            wt = p.op(Tn, "transpose", fx, xa_sl[:, k * 128:(k + 1) * 128],
                      sb["idf"][:, :], inc="t")
            fsl = sb["fstg"][:, (k % 2) * 128:(k % 2) * 128 + 128]
            p.wait(V, "t", wt)
            p.wait(V, "s", hs_guard[k % 2])
            p.op(V, "tensor_tensor", fsl, fx, sb["mrepsb"][:, :], ALU.add,
                 inc="v")
            evac[pmname] = p.cnt["v"]
            p.op(V, "tensor_tensor", sb["istg2"][:, :], fsl,
                 sb["rrepsb"][:, :], ALU.mult)
            tv = p.op(V, "tensor_tensor", fsl, sb["istg2"][:, :],
                      sb["rrrepsb"][:, :], ALU.mult, inc="v")
            p.wait(Sc, "v", tv)
            hs_sl = sb["scrE"][:, k * 256 + st2 * 128:k * 256 + st2 * 128 + 128]
            hs_guard[k % 2] = p.op(Sc, "activation", hs_sl, fsl, ACT.Identity,
                                   bias=0.0, scale=sb["wrmsfm"][:, k:k + 1],
                                   inc="s")
        xa_done[par] = [("v", p.cnt["v"]), ("s", p.cnt["s"]),
                        ("t", p.cnt["t"])]

    hs_ready = p.cnt["v"]
    gsum_s = {}
    kf_ready = 0
    for oi, o in enumerate(list(range(16, 24)) + list(range(8, 16)) + [25]):
        tok = ring_load(dr["wint"][o])
        pmname = "pm%d" % (oi % 2)
        pp = ps[pmname][:, 0:256]
        p.wait(Tn, "v", evac[pmname])
        p.wait(Tn, "v", hs_ready)
        wsl, slot = ring_use(tok)
        for k in range(8):
            p.op(Tn, "matmul", pp, wsl[:, k * 128:(k + 1) * 128],
                 sb["scrE"][:, k * 256:(k + 1) * 256],
                 start=(k == 0), stop=(k == 7), inc="t" if k == 7 else None)
        ring_done(slot)
        wt = p.cnt["t"]
        if 16 <= o < 24:
            tf = o - 16
            p.wait(Sc, "t", wt)
            ws = p.op(Sc, "activation", sb["scrC"][:, tf * 256:(tf + 1) * 256],
                      pp, ACT.Sigmoid, bias=sb["gbs"][:, tf:tf + 1], scale=1.0,
                      inc="s")
            gsum_s[tf] = ws
            p.wait(Tn, "s", ws)
            evac[pmname] = 0
        elif 8 <= o < 16:
            tf = o - 8
            p.wait(V, "t", wt)
            p.wait(V, "s", gsum_s[tf])
            p.op(V, "tensor_tensor", sb["fstg"][:, 0:256], pp,
                 sb["scrC"][:, tf * 256:(tf + 1) * 256], ALU.mult, inc="v")
            p.wait(V, "v")
            evac[pmname] = p.op(V, "tensor_tensor",
                                sb["cxbf"][:, tf * 256:(tf + 1) * 256], pp,
                                sb["fstg"][:, 0:256], ALU.subtract, inc="v")
        else:
            p.wait(V, "t", wt)
            kf_ready = p.op(V, "tensor_copy", sb["kf"][:, :], pp, inc="v")
            evac[pmname] = kf_ready
    for tf in range(8):
        p.wait(V, "v", kf_ready)
        p.wait(V, "s", gsum_s[tf])
        p.op(V, "tensor_tensor", sb["fstg"][:, 0:256], sb["kf"][:, :],
             sb["scrC"][:, tf * 256:(tf + 1) * 256], ALU.mult, inc="v")
        p.wait(V, "v")
        p.op(V, "tensor_tensor", sb["ckk"][:, tf * 256:(tf + 1) * 256],
             sb["kf"][:, :], sb["fstg"][:, 0:256], ALU.subtract, inc="v")
    p.wait(Tn, "v")
    for tf in range(8):
        for ct in range(2):
            ptn = "ptA" if (tf * 2 + ct) % 2 == 0 else "ptB"
            p.wait(Tn, "v", evac[ptn])
            wt = p.op(Tn, "transpose",
                      ps[ptn][0:128, tf * 128:(tf + 1) * 128],
                      sb["cxbf"][:, tf * 256 + ct * 128:
                                 tf * 256 + ct * 128 + 128],
                      sb["idb"][:, :], inc="t")
            p.wait(V, "t", wt)
            evac[ptn] = p.op(V, "tensor_copy",
                             sb["ckx"][:, ct * 1024 + tf * 128:
                                       ct * 1024 + (tf + 1) * 128],
                             ps[ptn][0:128, tf * 128:(tf + 1) * 128], inc="v")

    p.barrier()
    if core_debug:
        tapout("ckk", (128, 2048), BF16)
        tapout("ckx", (128, 2048), BF16)
        p.wait(Sy, "v")
        p.dma(Sy, taps["ckk"], sb["ckk"][:, :], "dO")
        p.dma(Sy, taps["ckx"], sb["ckx"][:, :], "dO")
        for _e in ENGS:
            p.wait(_e, "dO", p.cnt["dO"])

    unit_hist = []
    hsub_s = [0, 0]
    last_vW = [0]
    for sub in range(NSUB):
        s0 = sub * TS
        p.wait(Tn, "v", evac["ptBf"])
        for i2 in range(2):
            tt16 = sub * 2 + i2
            wt = p.op(Tn, "transpose", ps["ptBf"][0:1, 256:384],
                      sb["rrmscols"][:, tt16:tt16 + 1], sb["idf"][:, :],
                      inc="t")
            p.wait(V, "t", wt)
            p.wait(V, "t", rowscr_t[0])
            rs = p.op(V, "tensor_copy",
                      sb["rowscr"][0:1, i2 * 128:(i2 + 1) * 128],
                      ps["ptBf"][0:1, 256:384], inc="v")
            p.wait(Tn, "v", rs)
        wb = p.op(Tn, "matmul", ps["ptBf"][:, 0:256], sb["onesf"][0:1, 0:128],
                  sb["rowscr"][0:1, 0:256], start=True, stop=True, inc="t")
        rowscr_t[0] = wb
        p.wait(V, "t", wb)
        rep_r = p.op(V, "tensor_copy", sb["repsb"][:, 0:256],
                     ps["ptBf"][:, 0:256], inc="v")
        evac["ptBf"] = rep_r
        for k in range(8):
            fsl2 = sb["fstg"][:, (k % 2) * 256:(k % 2) * 256 + 256]
            p.wait(V, "v", rep_r)
            p.wait(V, "s", hsub_s[k % 2])
            hv = p.op(V, "tensor_tensor", fsl2,
                      sb["xnhi"][:, k * 2048 + s0:k * 2048 + s0 + TS],
                      sb["repsb"][:, 0:256], ALU.mult, inc="v")
            p.wait(Sc, "v", hv)
            hsub_s[k % 2] = p.op(Sc, "activation",
                                 sb["hsub"][:, k * 256:(k + 1) * 256], fsl2,
                                 ACT.Identity, bias=0.0,
                                 scale=sb["wrmsfm"][:, k:k + 1], inc="s")
        hsub_ready = p.cnt["s"]
        o_order = (list(range(16, 24)) + list(range(0, 8)) +
                   list(range(8, 16)) + [24, 25])
        g_evac = {}
        for oi, o in enumerate(o_order):
            tok = ring_load(dr["wint"][o])
            pmname = "pu%d" % (2 + oi % 2)
            pp = ps[pmname][:, 0:256]
            p.wait(Tn, "v", evac[pmname])
            if oi == 0:
                p.wait(Tn, "s", hsub_ready)
            wsl, slot = ring_use(tok)
            for k in range(8):
                p.op(Tn, "matmul", pp, wsl[:, k * 128:(k + 1) * 128],
                     sb["hsub"][:, k * 256:(k + 1) * 256],
                     start=(k == 0), stop=(k == 7),
                     inc="t" if k == 7 else None)
            ring_done(slot)
            wt = p.cnt["t"]
            if 16 <= o < 24:
                tf = o - 16
                p.wait(Sc, "t", wt)
                ws = p.op(Sc, "activation",
                          sb["scrA"][:, tf * 256:(tf + 1) * 256], pp,
                          ACT.Sigmoid, bias=sb["gbs"][:, tf:tf + 1],
                          scale=1.0, inc="s")
                g_evac[tf] = ws
                p.wait(Tn, "s", ws)
                evac[pmname] = 0
            elif o < 8:
                tf = o
                p.wait(Sc, "t", wt)
                ws = p.op(Sc, "activation",
                          sb["scrE"][:, tf * 256:(tf + 1) * 256], pp,
                          ACT.Sigmoid, inc="s")
                p.wait(Tn, "s", ws)
                evac[pmname] = 0
            elif o < 16:
                tf = o - 8
                p.wait(V, "t", wt)
                p.wait(V, "s", g_evac[tf])
                p.op(V, "tensor_tensor",
                     sb["scrB"][:, tf * 256:(tf + 1) * 256], pp,
                     sb["scrA"][:, tf * 256:(tf + 1) * 256], ALU.mult,
                     inc="v")
                p.wait(V, "v")
                evac[pmname] = p.op(V, "tensor_tensor",
                                    sb["scrB"][:, tf * 256:(tf + 1) * 256],
                                    pp, sb["scrB"][:, tf * 256:(tf + 1) * 256],
                                    ALU.subtract, inc="v")
            elif o == 24:
                p.wait(V, "t", wt)
                p.op(V, "tensor_scalar", sb["qf"][:, :], pp, 0.125, None,
                     ALU.mult, inc="v")
                evac[pmname] = p.op(V, "tensor_scalar", sb["qb"][:, :], pp,
                                    0.125, None, ALU.mult, inc="v")
            else:
                p.wait(V, "t", wt)
                evac[pmname] = p.op(V, "tensor_copy", sb["kf"][:, :], pp,
                                    inc="v")
                for tf in range(8):
                    p.wait(V, "s", g_evac[tf])
                    p.op(V, "tensor_tensor", sb["fstg"][:, 0:256],
                         sb["kf"][:, :],
                         sb["scrA"][:, tf * 256:(tf + 1) * 256], ALU.mult,
                         inc="v")
                    p.wait(V, "v")
                    p.op(V, "tensor_tensor",
                         sb["scrC"][:, tf * 256:(tf + 1) * 256],
                         sb["kf"][:, :], sb["fstg"][:, 0:256], ALU.subtract,
                         inc="v")
        p.op(V, "memset", sb["scrA"][:, 0:2048:64], 0.0, inc="v")
        p.wait(V, "v")
        for tf in range(8):
            p.op(V, "tensor_tensor_scan",
                 sb["scrD"][:, tf * 256:(tf + 1) * 256],
                 sb["scrA"][:, tf * 256:(tf + 1) * 256],
                 sb["scrB"][:, tf * 256:(tf + 1) * 256], 0.0, ALU.mult,
                 ALU.add, inc="v")
        for tf in range(8):
            p.op(V, "tensor_tensor_scan", sb["fstg"][:, 0:256],
                 sb["scrA"][:, tf * 256:(tf + 1) * 256],
                 sb["scrC"][:, tf * 256:(tf + 1) * 256], 0.0, ALU.mult,
                 ALU.add, inc="v")
            p.wait(V, "v")
            p.op(V, "tensor_copy", sb["scrB"][:, tf * 256:(tf + 1) * 256],
                 sb["fstg"][:, 0:256], inc="v")
            p.wait(V, "v")
        for tf in range(8):
            p.op(V, "tensor_tensor", sb["fstg"][:, 0:256], sb["qf"][:, :],
                 sb["scrB"][:, tf * 256:(tf + 1) * 256], ALU.mult, inc="v")
            p.wait(V, "v")
            p.op(V, "tensor_copy", sb["tmph"][:, 0:256], sb["fstg"][:, 0:256],
                 inc="v")
            p.wait(V, "v")
            tl_ready = p.op(V, "tensor_tensor", sb["tmpl"][:, :],
                            sb["fstg"][:, 0:256], sb["tmph"][:, 0:256],
                            ALU.subtract, inc="v")
            p.wait(Tn, "v", tl_ready)
            if tf == 0:
                p.wait(Tn, "v", evac["pu2"])
            for tti in range(2):
                col = tti * 16 + 2 * tf
                p.op(Tn, "matmul", ps["pu2"][:, col:col + 2],
                     sb["tmph"][:, tti * 128:(tti + 1) * 128],
                     sb["selb"][:, :], start=True, stop=False)
                p.op(Tn, "matmul", ps["pu2"][:, col:col + 2],
                     sb["tmpl"][:, tti * 128:(tti + 1) * 128],
                     sb["selb"][:, :], start=False, stop=True,
                     inc="t" if (tf == 7 and tti == 1) else None)
        p.wait(V, "t")
        p.op(V, "tensor_copy", sb["ilse"][:, :], ps["pu2"][:, 0:32], inc="v")
        p.wait(V, "v")
        ilse_ready = p.op(V, "tensor_scalar", sb["nilse"][:, :],
                          sb["ilse"][:, :], -1.0, None, ALU.mult, inc="v")
        evac["pu2"] = ilse_ready

        ST = lambda i: sb["stat"][:, 32 + i:32 + i + 1]
        for bt in range(4):
            tti, hb = bt // 2, bt % 2
            gt = sub * 2 + tti
            heads = range(8 * hb, 8 * hb + 8)
            prev_sI = unit_hist[-1]["sI"] if unit_hist else 0
            if len(unit_hist) >= 2:
                p.wait(Tn, "v", unit_hist[-2]["evac_v"])
            p.wait(Tn, "v", ilse_ready)
            BQ = ((ps["pu2"], ps["pu3"], ps["ptAf"], ps["ptBf"]),
                  (ps["pm0"], ps["pm1"], ps["pu0"], ps["pu1"]))
            nb = len(unit_hist)
            scb = list(BQ[nb % 2])
            for j, hd in enumerate(heads):
                sc_ps = scb[j // 2][:, (j % 2) * 256:(j % 2) * 256 + 256]
                rq = (hd % 2) * 64
                tfh = hd // 2
                p.op(Tn, "matmul", sc_ps,
                     sb["qb"][rq:rq + 64, tti * 128:(tti + 1) * 128],
                     sb["ckk"][rq:rq + 64, tfh * 256:(tfh + 1) * 256],
                     start=True, stop=False)
                p.op(Tn, "matmul", sc_ps, sb["ub"][:, :],
                     sb["msb"][0:2, gt * 256:(gt + 1) * 256],
                     start=False, stop=True,
                     inc="t" if j == 7 else None)
            wtS = p.cnt["t"]
            p.wait(V, "t", wtS)
            for j in range(8):
                p.op(V, "tensor_copy", sb["scsB"][:, j * 256:(j + 1) * 256],
                     scb[j // 2][:, (j % 2) * 256:(j % 2) * 256 + 256],
                     inc="v" if j == 7 else None)
            p.wait(V, "v")
            m0_8 = sb["stat"][:, 32:40]
            mm8 = sb["stat"][:, 40:48]
            mn8 = sb["stat"][:, 48:56]
            st8 = sb["stat"][:, 56:64]
            den0 = sb["mx8"][:, 0:8]
            den8 = sb["mx8"][:, 32:40]
            rec8 = sb["mx8"][:, 40:48]
            ils = sb["ilse"][:, tti * 16 + 8 * hb:tti * 16 + 8 * hb + 8]
            scv = sb["scsB"][:, :].rearrange("p (j c) -> p j c", c=256)
            p.op(V, "reduce_max", m0_8, scv, axis=AX, inc="v")
            p.wait(V, "v")
            p.op(V, "tensor_tensor", mm8, m0_8, ils, ALU.max, inc="v")
            p.wait(V, "v")
            vMN = p.op(V, "tensor_scalar", mn8, mm8, -1.0, None, ALU.mult,
                       inc="v")
            p.wait(V, "v", vMN)
            p.wait(V, "s", prev_sI)
            vST = p.op(V, "tensor_tensor", st8, ils, mn8, ALU.add, inc="v")
            p.wait(V, "v", vST)
            p.wait(Tn, "v", vST)
            wT1 = p.op(Tn, "transpose", scb[0][0:8, 0:128], mn8,
                       sb["idf"][:, :], inc="t")
            p.wait(V, "t", wT1)
            vR1 = p.op(V, "tensor_copy", sb["r8"][:, :], scb[0][0:8, 0:128],
                       inc="v")
            p.wait(Tn, "v", vR1)
            for q4 in range(4):
                p.op(Tn, "matmul", scb[q4][:, 0:512], sb["r8"][:, :],
                     sb["bo8"][0:8, q4 * 512:(q4 + 1) * 512],
                     start=True, stop=True, inc="t" if q4 == 3 else None)
            wT2 = p.cnt["t"]
            p.wait(V, "t", wT2)
            for q4 in range(4):
                p.op(V, "tensor_tensor",
                     sb["scsB"][:, q4 * 512:(q4 + 1) * 512],
                     scb[q4][:, 0:512],
                     sb["scsB"][:, q4 * 512:(q4 + 1) * 512], ALU.add,
                     inc="v" if q4 == 3 else None)
            vS2 = p.cnt["v"]
            p.wait(V, "v", vS2)
            p.wait(Sc, "v", vS2)
            p.op(Sc, "activation", sb["wbig"][:, 0:2048], sb["scsB"][:, :],
                 ACT.Exp, bias=0.0, scale=1.0, inc="s")
            iw8 = sb["mx8"][:, 48:56]
            sI = p.op(Sc, "activation", iw8, st8, ACT.Exp, bias=0.0,
                      scale=1.0, inc="s")
            p.wait(V, "s", sI)
            wv = sb["wbig"][:, 0:2048].rearrange("p (j c) -> p j c", c=256)
            p.op(V, "reduce_sum", den0, wv, axis=AX, inc="v")
            p.wait(V, "v")
            p.op(V, "tensor_tensor", den8, den0, iw8, ALU.add, inc="v")
            p.wait(V, "v")
            vD = p.op(V, "reciprocal", rec8, den8, inc="v")
            p.wait(V, "v", vD)
            p.wait(Tn, "v", vD)
            p.op(Tn, "transpose", scb[0][0:8, 0:128], iw8, sb["idf"][:, :])
            wT3 = p.op(Tn, "transpose", scb[0][0:8, 128:256], rec8,
                       sb["idf"][:, :], inc="t")
            p.wait(V, "t", wT3)
            p.op(V, "tensor_copy", sb["r8b"][0:8, 0:128],
                 scb[0][0:8, 0:128])
            vR3 = p.op(V, "tensor_copy", sb["r8b"][0:8, 128:256],
                       scb[0][0:8, 128:256], inc="v")
            p.wait(Tn, "v", vR3)
            for tfl in range(4):
                p.op(Tn, "matmul", scb[0][:, tfl * 128:(tfl + 1) * 128],
                     sb["selh"][0:8, tfl * 128:(tfl + 1) * 128],
                     sb["r8b"][0:8, 0:128], start=True, stop=True)
            for tfl in range(4):
                p.op(Tn, "matmul", scb[1][:, tfl * 128:(tfl + 1) * 128],
                     sb["selh"][0:8, tfl * 128:(tfl + 1) * 128],
                     sb["r8b"][0:8, 128:256], start=True, stop=True)
            p.wait(Tn, "v", last_vW[0])
            for j in range(8):
                for ct in range(2):
                    idx = j * 2 + ct
                    pt = ps["ptA"] if idx < 8 else ps["ptB"]
                    p.op(Tn, "transpose",
                         pt[0:128, (idx % 8) * 128:(idx % 8) * 128 + 128],
                         sb["wbig"][:, j * 256 + ct * 128:
                                    j * 256 + ct * 128 + 128],
                         sb["idb"][:, :],
                         inc="t" if idx == 15 else None)
            wT4 = p.cnt["t"]
            p.wait(V, "t", wT4)
            p.op(V, "tensor_copy", sb["wbig"][:, 2048:3072],
                 ps["ptA"][0:128, 0:1024])
            vW = p.op(V, "tensor_copy", sb["wbig"][:, 3072:4096],
                      ps["ptB"][0:128, 0:1024], inc="v")
            last_vW[0] = vW
            p.wait(Tn, "v", vW)
            for j, hd in enumerate(heads):
                rq = (hd % 2) * 64
                tfl = j // 2
                for ct in range(2):
                    idx = j * 2 + ct
                    p.op(Tn, "matmul",
                         scb[2][rq:rq + 64, tfl * 128:(tfl + 1) * 128],
                         sb["ckx"][:, ct * 1024 + hd * 64:
                                   ct * 1024 + hd * 64 + 64],
                         sb["wbig"][:, 2048 + idx * 128:2048 + idx * 128 + 128],
                         start=(ct == 0), stop=(ct == 1),
                         inc="t" if j == 7 and ct == 1 else None)
            wtO = p.cnt["t"]
            p.wait(V, "t", wtO)
            xkv = sb["scrD"][:, :].rearrange("p (tf c) -> p tf c", c=256)[
                :, 4 * hb:4 * hb + 4, tti * 128:(tti + 1) * 128]
            zv = sb["scrE"][:, :].rearrange("p (tf c) -> p tf c", c=256)[
                :, 4 * hb:4 * hb + 4, tti * 128:(tti + 1) * 128]
            azv = sb["azt"][:, :].rearrange("p (tf c) -> p tf c", c=256)[
                :, 4 * hb:4 * hb + 4, tti * 128:(tti + 1) * 128]
            p.op(V, "tensor_tensor", sb["fstg"][:, :], scb[0][:, 0:512],
                 xkv, ALU.mult, inc="v")
            p.wait(V, "v")
            p.op(V, "tensor_tensor", sb["scsB"][:, 0:512],
                 scb[2][:, 0:512], sb["fstg"][:, :], ALU.add, inc="v")
            p.wait(V, "v")
            p.op(V, "tensor_tensor", sb["fstg"][:, :], sb["scsB"][:, 0:512],
                 scb[1][:, 0:512], ALU.mult, inc="v")
            p.wait(V, "v")
            ev = p.op(V, "tensor_tensor", azv, sb["fstg"][:, :], zv,
                      ALU.mult, inc="v")
            unit_hist.append({"evac_v": ev, "sI": sI})
            qnames = (("pu2", "pu3", "ptAf", "ptBf"),
                      ("pm0", "pm1", "pu0", "pu1"))[nb % 2]
            for kk in qnames:
                evac[kk] = ev
            evac["ptA"] = ev
            evac["ptB"] = ev
        p.wait(Tn, "v", unit_hist[-1]["evac_v"])
        for o in range(8):
            tok = ring_load(dr["woutt"][o])
            pmname = "pu%d" % (2 + o % 2)
            pp = ps[pmname][:, 0:256]
            p.wait(Tn, "v", evac[pmname])
            wsl, slot = ring_use(tok)
            for k in range(8):
                p.op(Tn, "matmul", pp, wsl[:, k * 128:(k + 1) * 128],
                     sb["azt"][:, k * 256:(k + 1) * 256],
                     start=(k == 0), stop=(k == 7),
                     inc="t" if k == 7 else None)
            ring_done(slot)
            p.wait(V, "t")
            evac[pmname] = p.op(V, "tensor_tensor",
                                sb["sreg"][:, o * 2048 + s0:
                                           o * 2048 + s0 + TS],
                                pp, sb["xnhi"][:, o * 2048 + s0:
                                               o * 2048 + s0 + TS],
                                ALU.add, inc="v")

    p.barrier()
    if core_debug:
        tapout("sreg", (128, 16384), BF16)
        p.wait(Sy, "v")
        p.dma(Sy, taps["sreg"], sb["sreg"][:, :], "dO")
        for _e in ENGS:
            p.wait(_e, "dO", p.cnt["dO"])

    w1_tcnt = [0, 0]
    w2_tcnt = [0, 0]
    evac_s = {}
    QUADS = (("pm0", "pm1", "pu0", "pu1"), ("pu2", "pu3", "ptAf", "ptBf"))
    b1_scnt = [0, 0]
    stg_vcnt = [0, 0]
    db_tok = {0: db0_early}
    for e_i in range(NE):
        bp = e_i % 2
        db = db_tok[e_i]
        if e_i + 1 < NE:
            bpn = (e_i + 1) % 2
            p.wait(Gp, "s", b1_scnt[bpn])
            db_tok[e_i + 1] = p.dma(Gp, sb["b1s"][:, bpn * 32:(bpn + 1) * 32],
                                    dr["b1t"][e_i + 1], "db%d" % bpn)
        for q4 in range(4):
            pmname = "pm%d" % (q4 % 2)
            pp = ps[pmname][:, 0:512]
            p.wait(Tn, "v", evac[pmname])
            p.wait(Tn, "s", evac_s.get(pmname, 0))
            wt = p.op(Tn, "matmul", pp,
                      sb["sel8"][:, e_i * 128:(e_i + 1) * 128],
                      sb["oh"][0:8, q4 * 512:(q4 + 1) * 512],
                      start=True, stop=True, inc="t")
            p.wait(V, "t", wt)
            evac[pmname] = p.op(V, "tensor_copy",
                                sb["scrE"][:, q4 * 512:(q4 + 1) * 512], pp,
                                inc="v")
        mask_ready = p.cnt["v"]
        for q in range(4):
            p.wait(V, "t")
            for ft in range(8):
                wp = ft % 2
                p.wait(Gp, "t", w1_tcnt[wp])
                c1 = p.dma(Gp, sb["w1buf"][:, wp * 1024:(wp + 1) * 1024],
                           dr["w1t"][e_i, q, ft], "dW1%d" % wp)
                w1sl = sb["w1buf"][:, wp * 1024:(wp + 1) * 1024]
                banks = QUADS[ft % 2]
                for bn in banks:
                    p.wait(Tn, "v", evac[bn])
                    p.wait(Tn, "s", evac_s.get(bn, 0))
                p.wait(Tn, "dW1%d" % wp, c1)
                for k in range(8):
                    for s4 in range(4):
                        p.op(Tn, "matmul", ps[banks[s4]][:, 0:512],
                             w1sl[:, k * 128:(k + 1) * 128],
                             sb["xnhi"][:, k * 2048 + s4 * 512:
                                        k * 2048 + (s4 + 1) * 512],
                             start=(k == 0), stop=(k == 7),
                             inc="t" if (k == 7 and s4 == 3) else None)
                wt = p.cnt["t"]
                bcol = sb["b1s"][:, bp * 32 + q * 8 + ft:
                                 bp * 32 + q * 8 + ft + 1]
                p.wait(Sc, "t", wt)
                p.wait(Sc, "db%d" % bp, db)
                for s4 in range(4):
                    sp = s4 % 2
                    stgsl = sb["stg"][:, sp * 512:(sp + 1) * 512]
                    p.wait(Sc, "v", stg_vcnt[sp])
                    ws = p.op(Sc, "activation", stgsl, ps[banks[s4]][:, 0:512],
                              ACT.Gelu, bias=bcol, scale=1.0, inc="s")
                    evac_s[banks[s4]] = ws
                    evac[banks[s4]] = 0
                    p.wait(V, "s", ws)
                    p.wait(V, "v", mask_ready)
                    hsl = sb["big"][:, ft * 2048 + s4 * 512:
                                    ft * 2048 + (s4 + 1) * 512]
                    stg_vcnt[sp] = p.op(V, "tensor_tensor", hsl, stgsl,
                                        sb["scrE"][:, s4 * 512:(s4 + 1) * 512],
                                        ALU.mult, inc="v")
                w1_tcnt[wp] = p.cnt["t"]
            h_ready = p.cnt["v"]
            for d in range(8):
                wp = d % 2
                p.wait(Gp, "t", w2_tcnt[wp])
                c2 = p.dma(Gp, sb["w2buf"][:, wp * 1024:(wp + 1) * 1024],
                           dr["w2t"][e_i, d, q], "dW2%d" % wp)
                w2sl = sb["w2buf"][:, wp * 1024:(wp + 1) * 1024]
                banks = QUADS[d % 2]
                for bn in banks:
                    p.wait(Tn, "v", evac[bn])
                    p.wait(Tn, "s", evac_s.get(bn, 0))
                p.wait(Tn, "dW2%d" % wp, c2)
                p.wait(Tn, "v", h_ready)
                last = (e_i == NE - 1 and q == 3)
                for k in range(8):
                    for s4 in range(4):
                        p.op(Tn, "matmul", ps[banks[s4]][:, 0:512],
                             w2sl[:, k * 128:(k + 1) * 128],
                             sb["big"][:, k * 2048 + s4 * 512:
                                       k * 2048 + (s4 + 1) * 512],
                             start=(k == 0), stop=(k == 7 and not last),
                             inc="t" if (k == 7 and s4 == 3 and not last)
                             else None)
                if last:
                    for s4 in range(4):
                        p.op(Tn, "matmul", ps[banks[s4]][:, 0:512],
                             sb["b2hi"][0:8, d * 128:(d + 1) * 128],
                             sb["oh"][0:8, s4 * 512:(s4 + 1) * 512],
                             start=False, stop=False)
                        p.op(Tn, "matmul", ps[banks[s4]][:, 0:512],
                             sb["b2lo"][0:8, d * 128:(d + 1) * 128],
                             sb["oh"][0:8, s4 * 512:(s4 + 1) * 512],
                             start=False, stop=True,
                             inc="t" if s4 == 3 else None)
                wt = p.cnt["t"]
                p.wait(V, "t", wt)
                for s4 in range(4):
                    osl = sb["offn"][:, d * 2048 + s4 * 512:
                                     d * 2048 + (s4 + 1) * 512]
                    if e_i == 0 and q == 0:
                        ev = p.op(V, "tensor_copy", osl,
                                  ps[banks[s4]][:, 0:512], inc="v")
                    else:
                        ev = p.op(V, "tensor_tensor", osl,
                                  ps[banks[s4]][:, 0:512], osl, ALU.add,
                                  inc="v")
                    evac[banks[s4]] = ev
                w2_tcnt[wp] = p.cnt["t"]
        b1_scnt[bp] = p.cnt["s"]

    p.barrier()
    if core_debug:
        tapout("offn", (128, 16384), BF16)
        p.wait(Sy, "v")
        p.dma(Sy, taps["offn"], sb["offn"][:, :], "dO")
        for _e in ENGS:
            p.wait(_e, "dO", p.cnt["dO"])

    xrl_used = [0, 0]
    ot_stored = [0, 0]
    for tt in range(NTT):
        par = tt % 2
        xr = sb["xrl"][:, par * 1024:(par + 1) * 1024]
        ot = sb["otile"][:, par * 1024:(par + 1) * 1024]
        p.wait(Sy, "v", xrl_used[par])
        dx = p.dma(Sy, xr, dr["xs"][tt * 128:(tt + 1) * 128, :], "dX%d" % par)
        p.wait(Tn, "v", evac["pu3"])
        wtp = p.op(Tn, "transpose", ps["pu3"][0:1, 128:256],
                   sb["pmcols"][:, tt:tt + 1], sb["idf"][:, :], inc="t")
        p.wait(V, "t", wtp)
        p.wait(V, "t", rowscr_t[0])
        rsc = p.op(V, "tensor_copy", sb["rowscr"][0:1, 0:128],
                   ps["pu3"][0:1, 128:256], inc="v")
        p.wait(Tn, "v", rsc)
        wt = p.op(Tn, "matmul", ps["pu3"][:, 0:128], sb["onesf"][0:1, 0:128],
                  sb["rowscr"][0:1, 0:128], start=True, stop=True, inc="t")
        rowscr_t[0] = wt
        ptn = "ptA" if par == 0 else "ptB"
        pt = ps[ptn]
        p.wait(Tn, "v", evac[ptn])
        last_tp = 0
        for d in range(8):
            dsl = slice(d * 2048 + tt * 128, d * 2048 + tt * 128 + 128)
            p.wait(V, "t", wt)
            p.op(V, "tensor_tensor", sb["fstg"][:, 0:128],
                 sb["offn"][:, dsl], ps["pu3"][:, 0:128], ALU.mult, inc="v")
            p.wait(Sc, "v")
            ws = p.op(Sc, "activation", sb["stg"][:, 0:128],
                      sb["fstg"][:, 0:128], ACT.Silu, inc="s")
            p.wait(V, "s", ws)
            p.wait(V, "t", last_tp)
            rv = p.op(V, "tensor_tensor", sb["tmpl"][:, 0:128],
                      sb["stg"][:, 0:128], sb["sreg"][:, dsl], ALU.mult,
                      inc="v")
            p.wait(Tn, "v", rv)
            p.op(Tn, "transpose", pt[0:128, d * 128:(d + 1) * 128],
                 sb["tmpl"][:, 0:128], sb["idb"][:, :], inc="t")
            last_tp = p.cnt["t"]
        evac["pu3"] = p.cnt["v"]
        wt2 = p.cnt["t"]
        p.wait(V, "t", wt2)
        p.wait(V, "dX%d" % par, dx)
        addv = p.op(V, "tensor_tensor", ot, xr, pt[0:128, 0:1024], ALU.add,
                    inc="v")
        evac[ptn] = addv
        evac["pu3"] = addv
        xrl_used[par] = addv
        p.wait(Sy, "v", addv)
        ot_stored[par] = p.dma(Sy, y[tt * 128:(tt + 1) * 128, :], ot, "dO")

    p.wait(Sy, "dO", p.cnt["dO"])
    p.build()
    return nc, taps


def _prep_inputs(L0_attn_hidden_states, L0_attn_rms_weight, L0_attn_W_in,
                 L0_attn_gate_bias, L0_attn_W_out, L0_ffn_Wr, L0_ffn_W1,
                 L0_ffn_b1, L0_ffn_W2, L0_ffn_b2):
    x2 = np.ascontiguousarray(
        np.asarray(L0_attn_hidden_states, np.float32).reshape(16384, D))
    W_in = np.asarray(L0_attn_W_in, np.float32)
    W_out = np.asarray(L0_attn_W_out, np.float32)
    Wr = np.asarray(L0_ffn_Wr, np.float32)
    W1 = np.asarray(L0_ffn_W1, np.float32)
    W2 = np.asarray(L0_ffn_W2, np.float32)
    b1 = np.asarray(L0_ffn_b1, np.float32)
    b2 = np.asarray(L0_ffn_b2, np.float32)
    rms = np.asarray(L0_attn_rms_weight, np.float32)
    gb = np.asarray(L0_attn_gate_bias, np.float32)

    com = {}
    com["xsum"] = np.ascontiguousarray(x2[0::CHUNK])
    com["wint"] = np.ascontiguousarray(
        W_in.reshape(8, 128, 26, 128).transpose(2, 1, 0, 3)
        .reshape(26, 128, 1024))
    com["woutt"] = np.ascontiguousarray(
        W_out.reshape(8, 128, 8, 128).transpose(2, 1, 0, 3)
        .reshape(8, 128, 1024))
    com["wrt"] = np.ascontiguousarray(
        Wr.reshape(8, 128, 8).transpose(1, 0, 2).reshape(128, 64))
    com["w1t"] = np.ascontiguousarray(
        W1.reshape(NE, 8, 128, 32, 128).transpose(0, 3, 2, 1, 4)
        .reshape(NE, 4, 8, 128, 1024))
    com["w2t"] = np.ascontiguousarray(
        W2.reshape(NE, 4, 8, 128, 8, 128).transpose(0, 4, 1, 3, 2, 5)
        .reshape(NE, 8, 4, 128, 1024))
    com["b1t"] = np.ascontiguousarray(
        b1.reshape(NE, 32, 128).transpose(0, 2, 1))
    com["b2m"] = b2
    com["rmswt"] = np.ascontiguousarray(rms.reshape(8, 128).T)
    com["gbt"] = np.ascontiguousarray(gb.reshape(8, 128).T)
    com["ubh"] = np.zeros((2, 128), BF)
    com["ubh"][0, 0:64] = 1
    com["ubh"][1, 64:128] = 1
    com["idfh"] = np.eye(128, dtype=np.float32)
    com["idbh"] = np.eye(128, dtype=np.float32).astype(BF)
    com["onesfh"] = np.ones((1, 256), np.float32)
    com["onesbh"] = np.ones((1, 128), BF)
    selb = np.zeros((128, 2), BF)
    selb[0:64, 0] = 1
    selb[64:128, 1] = 1
    com["selbh"] = selb
    sel8 = np.zeros((8, 1024), BF)
    for e in range(8):
        sel8[e, e * 128:(e + 1) * 128] = 1
    com["sel8h"] = sel8
    com["iota8h"] = np.tile(np.arange(8, dtype=np.float32), (128, 1))
    bo8 = np.zeros((8, 2048), np.float32)
    for h in range(8):
        bo8[h, h * 256:(h + 1) * 256] = 1.0
    com["bo8h"] = bo8
    selh = np.zeros((8, 512), np.float32)
    for tfl in range(4):
        for pp_ in range(128):
            selh[2 * tfl + pp_ // 64, tfl * 128 + pp_] = 1.0
    com["selhh"] = selh

    in_maps = []
    for c in range(NCORES):
        m = dict(com)
        m["xs"] = np.ascontiguousarray(x2[c * T:(c + 1) * T])
        msb = np.zeros((2, 16 * 256), np.float32)
        for tt in range(16):
            c0 = (c * T + tt * 128) // CHUNK
            cc = np.arange(NCH)
            msb[0, tt * 256:(tt + 1) * 256] = np.where(cc >= c0, NEG, 0.0)
            msb[1, tt * 256:(tt + 1) * 256] = np.where(cc >= c0 + 1, NEG, 0.0)
        m["msbh"] = msb.astype(BF)
        in_maps.append(m)
    return in_maps


_NC_CACHE = {}


def kernel(**inputs):
    in_maps = _prep_inputs(**inputs)
    if "nc" not in _NC_CACHE:
        _NC_CACHE["nc"] = build(core_debug=False)[0]
    nc = _NC_CACHE["nc"]
    res = run_bass_kernel_spmd(nc, in_maps, core_ids=list(range(NCORES)))
    out = np.concatenate([np.asarray(r["y"]) for r in res.results], axis=0)
    return out.reshape(1, 16384, D)


# version 12
# speedup vs baseline: 1.2539x; 1.0622x over previous
import numpy as np
import ml_dtypes
import concourse.bass as bass
import concourse.mybir as mybir
from concourse.bass_utils import run_bass_kernel_spmd
from contextlib import ExitStack

F32 = mybir.dt.float32
BF16 = mybir.dt.bfloat16
BF = ml_dtypes.bfloat16
AX = mybir.AxisListType.X
ALU = mybir.AluOpType
ACT = mybir.ActivationFunctionType

NCORES = 8
T = 2048
D = 1024
DH = 64
NHEAD = 16
CHUNK = 64
NCH = 256
FH = 4096
NE = 8
NEG = np.float32(-1e30)
TS = 256
NSUB = T // TS
NTT = T // 128

ENGS = ("sync", "gpsimd", "vector", "scalar", "tensor")
V, Sc, Tn, Sy, Gp = "vector", "scalar", "tensor", "sync", "gpsimd"

BASE = 20480
SB = {}
_off = [BASE]
def _reg(name, nbytes, align=64):
    _off[0] = (_off[0] + align - 1) // align * align
    SB[name] = _off[0]
    _off[0] += nbytes

_reg("xnhi", 32768)
_reg("sreg", 32768)
_reg("big", 32768)
_reg("wring", 4096)
_reg("w1buf", 4096)
_reg("w2buf", 4096)
_reg("scrA", 8192)
_reg("scrB", 8192)
_reg("scrC", 8192)
_reg("scrD", 8192)
_reg("scrE", 4096)
_reg("msb", 8192)
_reg("ckk", 4096)
_reg("ckx", 4096)
_reg("oh", 4096)
_reg("rowscr", 1536)
_reg("pmcols", 64)
_reg("sel8", 2048)
_reg("b2hi", 2048)
_reg("b2lo", 2048)
_reg("idf", 512)
_reg("idb", 256)
_reg("onesf", 1024)
_reg("onesb", 256)
_reg("selb", 32)
_reg("ub", 256)
_reg("wrf", 256)
_reg("wrhib", 128)
_reg("wrlob", 128)
_reg("gbs", 32)
_reg("wrmsfm", 32)
_reg("b1s", 256)
_reg("qf", 1024)
_reg("qb", 512)
_reg("kf", 1024)
_reg("ilse", 128)
_reg("nilse", 128)
_reg("stat", 256)
_reg("tmph", 1024)
_reg("tmpl", 512)
_reg("stg", 2048)
_reg("fstg", 2048)
_reg("istg", 512)
_reg("istg2", 512)
_reg("r8", 512)
_reg("lgt", 128)
_reg("ohtm", 64)
_reg("mx8", 256)
_reg("repsb", 1024)
_reg("hsub", 4096)
_reg("rrmscols", 64)
_reg("warmt", 64)
_reg("wbias", 64)
_reg("iota8", 32)
_reg("idxu", 64)
_reg("bo8", 8192)
_reg("selh", 2048)
_reg("r8b", 1024)
_reg("statln", 256)
_reg("mrepsb", 512)
_reg("rrepsb", 512)
_reg("rrrepsb", 512)
assert _off[0] <= 229376, _off[0]


class Prog:

    def __init__(self, nc):
        self.nc = nc
        self.q = {e: [] for e in ENGS}
        self.sem = {}
        self.cnt = {}
        self.ctx = ExitStack()

    def new_sem(self, name):
        self.sem[name] = self.ctx.enter_context(self.nc.semaphore(name))
        self.cnt[name] = 0

    def op(self, eng, meth, *args, inc=None, n=1, **kw):
        if inc is not None:
            self.cnt[inc] += n
        self.q[eng].append((meth, args, kw, inc, n))
        return self.cnt[inc] if inc else None

    def dma(self, eng, out_ap, in_ap, inc):
        return self.op(eng, "dma_start", inc=inc, n=16, out=out_ap, in_=in_ap)

    def wait(self, eng, sem, val=None):
        v = self.cnt[sem] if val is None else val
        if v > 0:
            self.q[eng].append(("__wait__", (sem, v), {}, None, 0))

    def barrier(self, engs=ENGS, sems=("v", "s", "t")):
        vals = [(sm, self.cnt[sm]) for sm in sems]
        for e in engs:
            for sm, v in vals:
                self.wait(e, sm, v)

    def build(self):
        block = self.ctx.enter_context(self.nc.Block())
        for eng in ENGS:
            items = self.q[eng]
            def body(e, items=items):
                for meth, args, kw, inc, n in items:
                    if meth == "__wait__":
                        e.wait_ge(self.sem[args[0]], args[1])
                    else:
                        ins = getattr(e, meth)(*args, **kw)
                        if inc is not None:
                            ins.then_inc(self.sem[inc], n)
            getattr(block, eng)(body)
        self.ctx.close()


def build(core_debug=False):
    nc = bass.Bass("TRN2", target_bir_lowering=False,
                   detect_race_conditions=False)

    dr = {}
    def din(name, shape, dt=F32):
        dr[name] = nc.dram_tensor(name, list(shape), dt,
                                  kind="ExternalInput").ap()
    din("xs", (T, D))
    din("xsum", (NCH, D))
    din("wint", (26, 128, 1024))
    din("woutt", (8, 128, 1024))
    din("wrt", (128, 64))
    din("w1t", (NE, 4, 8, 128, 1024))
    din("w2t", (NE, 8, 4, 128, 1024))
    din("b1t", (NE, 128, 32))
    din("b2m", (NE, D))
    din("rmswt", (128, 8))
    din("gbt", (128, 8))
    din("msbh", (2, 16 * 256), BF16)
    din("ubh", (2, 128), BF16)
    din("idfh", (128, 128))
    din("idbh", (128, 128), BF16)
    din("onesfh", (1, 256))
    din("onesbh", (1, 128), BF16)
    din("selbh", (128, 2), BF16)
    din("sel8h", (8, 1024), BF16)
    din("iota8h", (128, 8))
    din("bo8h", (8, 2048))
    din("selhh", (8, 512))
    y = nc.dram_tensor("y", [T, D], F32, kind="ExternalOutput").ap()

    taps = {}
    def tapout(name, shape, dt=F32):
        taps[name] = nc.dram_tensor("tap_" + name, list(shape), dt,
                                    kind="ExternalOutput").ap()

    sb = {}
    def S_(name, shape, dt, key=None, extra=0):
        sb[name] = nc.alloc_sbuf_tensor_at(
            name, list(shape), dt, offset=SB[key or name] + extra).ap()
    S_("xnhi", (128, 16384), BF16)
    S_("xrl", (128, 2048), F32, "xnhi")
    S_("otile", (128, 2048), F32, "xnhi", 8192)
    S_("sreg", (128, 16384), BF16)
    S_("big", (128, 16384), BF16)
    S_("wring", (128, 2048), BF16)
    S_("w1buf", (128, 2048), BF16)
    S_("w2buf", (128, 2048), BF16)
    S_("scrA", (128, 2048), F32)
    S_("xa", (128, 2048), F32, "scrA")
    S_("azt", (128, 2048), BF16, "scrA")
    S_("scrB", (128, 2048), F32)
    S_("scrC", (128, 2048), F32)
    S_("scrD", (128, 2048), F32)
    S_("cxbf", (128, 2048), BF16, "scrD")
    S_("offn", (128, 16384), BF16, "scrA")
    S_("scrE", (128, 2048), BF16)
    S_("msb", (2, 4096), BF16)
    S_("ckk", (128, 2048), BF16)
    S_("ckx", (128, 2048), BF16)
    S_("rowscr", (1, 384), F32)
    S_("pmcols", (128, 16), F32)
    S_("sel8", (8, 1024), BF16)
    S_("oh", (8, 2048), BF16)
    S_("b2hi", (8, 1024), BF16)
    S_("b2lo", (8, 1024), BF16)
    S_("b2f32", (8, 1024), F32, "scrB")
    S_("idf", (128, 128), F32)
    S_("idb", (128, 128), BF16)
    S_("onesf", (1, 256), F32)
    S_("onesb", (1, 128), BF16)
    S_("selb", (128, 2), BF16)
    S_("ub", (2, 128), BF16)
    S_("wrf", (128, 64), F32)
    S_("wrhib", (128, 64), BF16)
    S_("wrlob", (128, 64), BF16)
    S_("gbs", (128, 8), F32)
    S_("wrmsfm", (128, 8), F32)
    S_("b1s", (128, 64), F32)
    S_("qf", (128, 256), F32)
    S_("qb", (128, 256), BF16)
    S_("kf", (128, 256), F32)
    S_("ilse", (128, 32), F32)
    S_("nilse", (128, 32), F32)
    S_("stat", (128, 64), F32)
    S_("tmph", (128, 512), BF16)
    S_("tmpl", (128, 256), BF16)
    S_("stg", (128, 1024), BF16)
    S_("fstg", (128, 512), F32)
    S_("istg", (128, 128), F32)
    S_("istg2", (128, 128), F32)
    S_("xloT", (128, 128), BF16, "istg")
    S_("r8", (8, 128), F32)
    S_("lgt", (128, 16), F32)
    S_("ohtm", (128, 16), BF16)
    S_("mx8", (128, 64), F32)
    S_("repsb", (128, 256), F32)
    S_("hsub", (128, 2048), BF16)
    S_("rrmscols", (128, 16), F32)
    S_("warmt", (128, 16), F32)
    S_("wbias", (128, 1), F32)
    S_("dbgln", (128, 512), F32, "scrD")
    S_("dbglg", (128, 128), F32, "scrD", 2048)
    S_("iota8", (128, 8), F32)
    S_("idxu", (128, 16), mybir.dt.uint32)
    S_("bo8", (8, 2048), F32)
    S_("selh", (8, 512), F32)
    S_("r8b", (8, 256), F32)
    S_("scsB", (128, 2048), F32, "scrB")
    S_("wbig", (128, 4096), BF16, "scrC")
    S_("statln", (128, 64), F32)
    S_("mrepsb", (128, 128), F32)
    S_("rrepsb", (128, 128), F32)
    S_("rrrepsb", (128, 128), F32)

    p = Prog(nc)
    ps = {}
    for nm, shape, dt in [
        ("pm0", (128, 512), F32), ("pm1", (128, 512), F32),
        ("pu0", (128, 512), F32), ("pu1", (128, 512), F32),
        ("ptA", (128, 1024), BF16), ("ptB", (128, 1024), BF16),
        ("pu2", (128, 512), F32), ("pu3", (128, 512), F32),
    ]:
        ps[nm] = p.ctx.enter_context(nc.psum_tensor(nm, list(shape), dt)).ap()

    for s in ("dL", "dX0", "dX1", "dR0", "dR1", "dW10", "dW11", "dW20",
              "dW21", "db0", "db1", "dO", "v", "s", "t"):
        p.new_sem(s)
    ps["ptAf"] = ps["ptA"].bitcast(F32)
    ps["ptBf"] = ps["ptB"].bitcast(F32)

    p.dma(Sy, sb["msb"][:, :], dr["msbh"], "dL")
    p.dma(Sy, sb["ub"][:, :], dr["ubh"], "dL")
    p.dma(Sy, sb["idf"][:, :], dr["idfh"], "dL")
    p.dma(Sy, sb["idb"][:, :], dr["idbh"], "dL")
    p.dma(Sy, sb["onesf"][:, :], dr["onesfh"], "dL")
    p.dma(Sy, sb["onesb"][:, :], dr["onesbh"], "dL")
    p.dma(Sy, sb["selb"][:, :], dr["selbh"], "dL")
    p.dma(Sy, sb["sel8"][:, :], dr["sel8h"], "dL")
    p.dma(Sy, sb["iota8"][:, :], dr["iota8h"], "dL")
    p.dma(Sy, sb["bo8"][:, :], dr["bo8h"], "dL")
    p.dma(Sy, sb["selh"][:, :], dr["selhh"], "dL")
    p.dma(Sy, sb["wrf"][:, :], dr["wrt"], "dL")
    p.dma(Sy, sb["gbs"][:, :], dr["gbt"], "dL")
    p.dma(Sy, sb["wrmsfm"][:, :], dr["rmswt"], "dL")
    p.dma(Sy, sb["b2f32"][:, :], dr["b2m"], "dL")
    cload = p.cnt["dL"]
    db0_early = p.dma(Gp, sb["b1s"][:, 0:32], dr["b1t"][0], "db0")

    p.wait(V, "dL", cload)
    p.op(V, "memset", sb["wbias"][:, :], 0.25)
    p.op(V, "memset", sb["statln"][:, :], 0.0)
    p.op(V, "memset", sb["mx8"][:, :], 0.0)
    p.op(V, "reduce_max", sb["warmt"][:, 0:1], sb["wbias"][:, :], axis=AX,
         inc="v")
    p.wait(V, "v")
    p.op(V, "tensor_scalar", sb["warmt"][:, 1:2], sb["wbias"][:, :],
         sb["wbias"][:, :], 2.0, ALU.subtract, ALU.mult)
    p.op(V, "scalar_tensor_tensor", sb["warmt"][:, 1:2], sb["wbias"][:, :],
         sb["wbias"][:, :], sb["wbias"][:, :], ALU.mult, ALU.add)
    p.op(V, "tensor_scalar", sb["warmt"][:, 1:2], sb["wbias"][:, :],
         sb["wbias"][:, :], None, ALU.is_equal, inc="v")
    p.wait(Sc, "v")
    for fn in (ACT.Exp, ACT.Gelu, ACT.Sigmoid, ACT.Silu, ACT.Square,
               ACT.Sqrt, ACT.Identity):
        p.op(Sc, "activation", sb["warmt"][:, 2:3], sb["wbias"][:, :], fn,
             bias=sb["wbias"][:, :], scale=1.0, accum_out=sb["warmt"][:, 3:4])
    p.op(Sc, "nop", inc="s")
    p.op(V, "tensor_copy", sb["wrhib"][:, :], sb["wrf"][:, :], inc="v")
    p.wait(V, "v")
    p.op(V, "tensor_tensor", sb["wrlob"][:, :], sb["wrf"][:, :],
         sb["wrhib"][:, :], ALU.subtract)
    p.op(V, "tensor_copy", sb["b2hi"][:, :], sb["b2f32"][:, :], inc="v")
    p.wait(V, "v")
    p.op(V, "tensor_tensor", sb["b2lo"][:, :], sb["b2f32"][:, :],
         sb["b2hi"][:, :], ALU.subtract, inc="v")

    evac = {k: 0 for k in ("pm0", "pm1", "pu0", "pu1", "pu2", "pu3",
                           "ptA", "ptB", "ptAf", "ptBf")}
    xa_done = [[], []]
    rowscr_t = [0]

    def ka(sem="v", at=None):
        vv = p.cnt[sem] if at is None else at
        if vv > 0:
            p.wait(Tn, sem, vv)
        p.op(Tn, "ldweights", sb["idb"][:, :])

    def ka_burst(n=30):
        for i in range(n):
            p.op(Tn, "ldweights", sb["idb"][:, :])
            p.op(Tn, "ldweights", sb["selb"][:, :])

    def ln_stats(src, dump, so):
        sl = lambda i: sb["statln"][:, so + i:so + i + 1]

        def fv(*args, **kw):
            c = p.op(V, *args, inc="v", **kw)
            p.wait(V, "v", c)
            return c

        fv("reduce_sum", sl(0), src, axis=AX)
        p.wait(Sc, "v")
        p.op(Sc, "activation", dump, src, ACT.Square, bias=0.0, scale=1.0,
             inc="s")
        fv("tensor_scalar", sl(2), sl(0), 1.0 / D, None, ALU.mult)
        fv("tensor_tensor", sl(3), sl(2), sl(2), ALU.mult)
        p.wait(V, "s")
        fv("reduce_sum", sl(1), dump, axis=AX)
        fv("tensor_scalar", sl(4), sl(1), 1.0 / D, None, ALU.mult)
        fv("tensor_tensor", sl(5), sl(4), sl(3), ALU.subtract)

        def newton(tin, eps, b):
            fv("tensor_scalar", sl(b), tin, 1.0, eps, ALU.mult, ALU.add)
            p.wait(Sc, "v")
            p.op(Sc, "activation", sl(b + 1), sl(b), ACT.Sqrt, inc="s")
            p.wait(V, "s")
            fv("reciprocal", sl(b + 2), sl(b + 1))
            fv("tensor_tensor", sl(b + 3), sl(b + 2), sl(b + 2), ALU.mult)
            fv("tensor_tensor", sl(b + 4), sl(b), sl(b + 3), ALU.mult)
            fv("tensor_scalar", sl(b + 5), sl(b + 4), -0.5, 1.5, ALU.mult,
               ALU.add)
            fv("tensor_tensor", sl(b + 6), sl(b + 2), sl(b + 5), ALU.mult)
            return sl(b + 6)

        rstd = newton(sl(5), 1e-5, 8)
        fv("tensor_scalar", sl(16), sl(2), -1.0, None, ALU.mult)
        fv("tensor_tensor", sl(17), rstd, rstd, ALU.mult)
        fv("tensor_tensor", sl(18), sl(5), sl(17), ALU.mult)
        rrms = newton(sl(18), 1e-6, 20)
        return sl(16), rstd, rrms

    rowps = lambda: ps["pu2"][0:1, 128:256]
    repps = lambda: ps["pu2"][:, 0:128]

    def make_rep(col, rep_sb):
        p.wait(Tn, "v")
        p.wait(Tn, "v", evac["pu2"])
        wt = p.op(Tn, "transpose", rowps(), col, sb["idf"][:, :], inc="t")
        p.wait(V, "t", wt)
        p.wait(V, "t", rowscr_t[0])
        rs = p.op(V, "tensor_copy", sb["rowscr"][0:1, 0:128], rowps(),
                  inc="v")
        p.wait(Tn, "v", rs)
        wb = p.op(Tn, "matmul", repps(), sb["onesf"][0:1, 0:128],
                  sb["rowscr"][0:1, 0:128], start=True, stop=True, inc="t")
        rowscr_t[0] = wb
        p.wait(V, "t", wb)
        ev = p.op(V, "tensor_copy", rep_sb, repps(), inc="v")
        evac["pu2"] = ev
        return ev

    hs_guard = [0, 0]
    for tt in range(NTT):
        par = tt % 2
        so = 32 * par
        xa_sl = sb["xa"][:, par * 1024:(par + 1) * 1024]
        for semn, vv in xa_done[par]:
            p.wait(Sy, semn, vv)
        dx = p.dma(Sy, xa_sl, dr["xs"][tt * 128:(tt + 1) * 128, :],
                   "dX%d" % par)
        p.wait(V, "dX%d" % par, dx)
        p.wait(Sc, "dX%d" % par, dx)
        dump = sb["scrB"][:, par * 1024:(par + 1) * 1024]
        negm, rstd, rrms = ln_stats(xa_sl, dump, so)
        if core_debug:
            p.op(V, "tensor_copy", sb["dbgln"][:, tt * 32:(tt + 1) * 32],
                 sb["statln"][:, so:so + 32])
        p.op(V, "tensor_copy", sb["rrmscols"][:, tt:tt + 1], rrms, inc="v")
        make_rep(negm, sb["mrepsb"][:, :])
        make_rep(rstd, sb["rrepsb"][:, :])

        for k in range(8):
            pmname = "pm%d" % (k % 2)
            fx = ps[pmname][:, 0:128]
            p.wait(Tn, "v", evac[pmname])
            wt = p.op(Tn, "transpose", fx, xa_sl[:, k * 128:(k + 1) * 128],
                      sb["idf"][:, :], inc="t")
            fsl = sb["fstg"][:, (k % 2) * 128:(k % 2) * 128 + 128]
            p.wait(V, "t", wt)
            p.wait(V, "s", hs_guard[k % 2])
            p.op(V, "tensor_tensor", fsl, fx, sb["mrepsb"][:, :], ALU.add,
                 inc="v")
            evac[pmname] = p.cnt["v"]
            p.wait(V, "v")
            p.op(V, "tensor_tensor", sb["istg2"][:, :], fsl,
                 sb["rrepsb"][:, :], ALU.mult, inc="v")
            p.wait(V, "v")
            hi_sl = sb["xnhi"][:, k * 2048 + tt * 128:k * 2048 + tt * 128 + 128]
            p.op(V, "tensor_copy", hi_sl, sb["istg2"][:, :], inc="v")
            p.wait(V, "v")
            lo_ready = p.op(V, "tensor_tensor", sb["xloT"][:, :],
                            sb["istg2"][:, :], hi_sl, ALU.subtract, inc="v")
            p.wait(Tn, "v", lo_ready)
            if k == 0:
                p.wait(Tn, "v", evac["pu1"])
            p.op(Tn, "matmul", ps["pu1"][0:8, 0:128],
                 sb["wrhib"][:, k * 8:(k + 1) * 8], hi_sl,
                 start=(k == 0), stop=False)
            p.op(Tn, "matmul", ps["pu1"][0:8, 0:128],
                 sb["wrhib"][:, k * 8:(k + 1) * 8], sb["xloT"][:, :],
                 start=False, stop=False)
            rmm = p.op(Tn, "matmul", ps["pu1"][0:8, 0:128],
                       sb["wrlob"][:, k * 8:(k + 1) * 8], hi_sl,
                       start=False, stop=(k == 7), inc="t")
            p.wait(V, "t", rmm)
        xa_done[par] = [("v", p.cnt["v"]), ("s", p.cnt["s"]),
                        ("t", p.cnt["t"])]

        p.wait(V, "t")
        ev = p.op(V, "tensor_copy", sb["r8"][:, :], ps["pu1"][0:8, 0:128],
                  inc="v")
        evac["pu1"] = ev
        p.wait(Tn, "v", ev)
        p.wait(Tn, "v", evac["pu0"])
        wt = p.op(Tn, "transpose", ps["pu0"][0:128, 0:8], sb["r8"][:, :],
                  sb["idf"][0:8, 0:8], inc="t")
        lg = sb["lgt"][:, par * 8:(par + 1) * 8]
        ohc = sb["ohtm"][:, par * 8:(par + 1) * 8]
        m8 = lambda i: sb["mx8"][:, 32 * par + i:32 * par + i + 1]
        vmax8 = sb["mx8"][:, 32 * par + 8:32 * par + 16]
        vidx8 = sb["mx8"][:, 32 * par + 16:32 * par + 24]
        p.wait(V, "t", wt)
        evac["pu0"] = p.op(V, "tensor_copy", lg, ps["pu0"][:, 0:8], inc="v")
        p.wait(V, "v")
        if core_debug:
            p.op(V, "tensor_copy", sb["dbglg"][:, tt * 8:(tt + 1) * 8], lg)
        p.op(V, "max", vmax8, lg, inc="v")
        p.wait(V, "v")
        idxu = sb["idxu"][:, par * 8:(par + 1) * 8]
        p.op(V, "max_index", idxu, vmax8, lg, inc="v")
        p.wait(V, "v")
        mxi = p.op(V, "tensor_copy", vidx8, idxu, inc="v")
        p.wait(V, "v", mxi)
        p.op(V, "tensor_copy", m8(0), vmax8[:, 0:1], inc="v")
        p.wait(Tn, "v", mxi)
        p.wait(Tn, "v", evac["pu3"])
        w1 = p.op(Tn, "transpose", ps["pu3"][0:1, 0:128],
                  vidx8[:, 0:1], sb["idf"][:, :], inc="t")
        p.wait(V, "t", w1)
        p.wait(V, "t", rowscr_t[0])
        rs = p.op(V, "tensor_copy", sb["rowscr"][0:1, 0:128],
                  ps["pu3"][0:1, 0:128], inc="v")
        p.wait(Tn, "v", rs)
        w2 = p.op(Tn, "matmul", ps["pu3"][:, 128:136],
                  sb["rowscr"][0:1, 0:128], sb["onesf"][0:1, 0:8],
                  start=True, stop=True, inc="t")
        rowscr_t[0] = w2
        p.wait(V, "t", w2)
        oh_v = p.op(V, "tensor_tensor", ohc, sb["iota8"][:, :],
                    ps["pu3"][:, 128:136], ALU.is_equal, inc="v")
        evac["pu3"] = oh_v
        p.wait(Sc, "v")
        p.op(Sc, "activation", sb["warmt"][:, 8:16], lg, ACT.Exp, bias=0.0,
             scale=1.0, inc="s")
        ws2 = p.op(Sc, "activation", m8(4), m8(0), ACT.Exp, bias=0.0,
                   scale=1.0, inc="s")
        p.wait(V, "s", ws2)
        p.op(V, "reduce_sum", m8(2), sb["warmt"][:, 8:16], axis=AX, inc="v")
        p.wait(V, "v")
        p.op(V, "reciprocal", m8(5), m8(2), inc="v")
        p.wait(V, "v")
        pm_ready = p.op(V, "tensor_tensor", sb["pmcols"][:, tt:tt + 1],
                        m8(4), m8(5), ALU.mult, inc="v")
        p.wait(Tn, "v", oh_v)
        p.wait(Tn, "v", evac["ptA"])
        wt2 = p.op(Tn, "transpose", ps["ptA"][0:8, 0:128], ohc,
                   sb["idb"][:, :], inc="t")
        p.wait(V, "t", wt2)
        evac["ptA"] = p.op(V, "tensor_copy",
                           sb["oh"][0:8, tt * 128:(tt + 1) * 128],
                           ps["ptA"][0:8, 0:128], inc="v")

    p.barrier()
    if core_debug:
        tapout("xnhi", (128, 16384), BF16)
        tapout("pmcols", (128, 16))
        tapout("oh", (8, 2048), BF16)
        tapout("dbgln", (128, 512))
        tapout("dbglg", (128, 128))
        p.wait(Sy, "v")
        p.dma(Sy, taps["dbgln"], sb["dbgln"][:, :], "dO")
        p.dma(Sy, taps["dbglg"], sb["dbglg"][:, :], "dO")
        p.dma(Sy, taps["xnhi"], sb["xnhi"][:, :], "dO")
        p.dma(Sy, taps["pmcols"], sb["pmcols"][:, :], "dO")
        p.dma(Sy, taps["oh"], sb["oh"][:, :], "dO")
        for _e in ENGS:
            p.wait(_e, "dO", p.cnt["dO"])

    ring_state = {"n": 0, "tcnt": [0, 0]}

    def ring_load(src_ap):
        j = ring_state["n"]
        slot = j % 2
        p.wait(Gp, "t", ring_state["tcnt"][slot])
        c = p.dma(Gp, sb["wring"][:, slot * 1024:(slot + 1) * 1024], src_ap,
                  "dR%d" % slot)
        ring_state["n"] = j + 1
        return ("dR%d" % slot, c, slot)

    def ring_use(tok):
        semn, c, slot = tok
        p.wait(Tn, semn, c)
        return sb["wring"][:, slot * 1024:(slot + 1) * 1024], slot

    def ring_done(slot):
        ring_state["tcnt"][slot] = p.cnt["t"]

    for st2 in range(2):
        par = st2
        so = 32 * par
        xa_sl = sb["xa"][:, par * 1024:(par + 1) * 1024]
        for semn, vv in xa_done[par]:
            p.wait(Sy, semn, vv)
        dx = p.dma(Sy, xa_sl, dr["xsum"][st2 * 128:(st2 + 1) * 128, :],
                   "dX%d" % par)
        p.wait(V, "dX%d" % par, dx)
        p.wait(Sc, "dX%d" % par, dx)
        dump = sb["scrB"][:, par * 1024:(par + 1) * 1024]
        negm, rstd, rrms = ln_stats(xa_sl, dump, so)
        make_rep(negm, sb["mrepsb"][:, :])
        make_rep(rstd, sb["rrepsb"][:, :])
        make_rep(rrms, sb["rrrepsb"][:, :])
        p.wait(V, "v")
        for k in range(8):
            pmname = "pm%d" % (k % 2)
            fx = ps[pmname][:, 0:128]
            p.wait(Tn, "v", evac[pmname])
            wt = p.op(Tn, "transpose", fx, xa_sl[:, k * 128:(k + 1) * 128],
                      sb["idf"][:, :], inc="t")
            fsl = sb["fstg"][:, (k % 2) * 128:(k % 2) * 128 + 128]
            p.wait(V, "t", wt)
            p.wait(V, "s", hs_guard[k % 2])
            p.op(V, "tensor_tensor", fsl, fx, sb["mrepsb"][:, :], ALU.add,
                 inc="v")
            evac[pmname] = p.cnt["v"]
            p.op(V, "tensor_tensor", sb["istg2"][:, :], fsl,
                 sb["rrepsb"][:, :], ALU.mult)
            tv = p.op(V, "tensor_tensor", fsl, sb["istg2"][:, :],
                      sb["rrrepsb"][:, :], ALU.mult, inc="v")
            p.wait(Sc, "v", tv)
            hs_sl = sb["scrE"][:, k * 256 + st2 * 128:k * 256 + st2 * 128 + 128]
            hs_guard[k % 2] = p.op(Sc, "activation", hs_sl, fsl, ACT.Identity,
                                   bias=0.0, scale=sb["wrmsfm"][:, k:k + 1],
                                   inc="s")
        xa_done[par] = [("v", p.cnt["v"]), ("s", p.cnt["s"]),
                        ("t", p.cnt["t"])]

    hs_ready = p.cnt["v"]
    gsum_s = {}
    kf_ready = 0
    for oi, o in enumerate(list(range(16, 24)) + list(range(8, 16)) + [25]):
        tok = ring_load(dr["wint"][o])
        pmname = "pm%d" % (oi % 2)
        pp = ps[pmname][:, 0:256]
        p.wait(Tn, "v", evac[pmname])
        p.wait(Tn, "v", hs_ready)
        wsl, slot = ring_use(tok)
        for k in range(8):
            p.op(Tn, "matmul", pp, wsl[:, k * 128:(k + 1) * 128],
                 sb["scrE"][:, k * 256:(k + 1) * 256],
                 start=(k == 0), stop=(k == 7), inc="t" if k == 7 else None)
        ring_done(slot)
        wt = p.cnt["t"]
        if 16 <= o < 24:
            tf = o - 16
            p.wait(Sc, "t", wt)
            ws = p.op(Sc, "activation", sb["scrC"][:, tf * 256:(tf + 1) * 256],
                      pp, ACT.Sigmoid, bias=sb["gbs"][:, tf:tf + 1], scale=1.0,
                      inc="s")
            gsum_s[tf] = ws
            p.wait(Tn, "s", ws)
            evac[pmname] = 0
        elif 8 <= o < 16:
            tf = o - 8
            p.wait(V, "t", wt)
            p.wait(V, "s", gsum_s[tf])
            p.op(V, "tensor_tensor", sb["fstg"][:, 0:256], pp,
                 sb["scrC"][:, tf * 256:(tf + 1) * 256], ALU.mult, inc="v")
            p.wait(V, "v")
            evac[pmname] = p.op(V, "tensor_tensor",
                                sb["cxbf"][:, tf * 256:(tf + 1) * 256], pp,
                                sb["fstg"][:, 0:256], ALU.subtract, inc="v")
        else:
            p.wait(V, "t", wt)
            kf_ready = p.op(V, "tensor_copy", sb["kf"][:, :], pp, inc="v")
            evac[pmname] = kf_ready
    for tf in range(8):
        p.wait(V, "v", kf_ready)
        p.wait(V, "s", gsum_s[tf])
        p.op(V, "tensor_tensor", sb["fstg"][:, 0:256], sb["kf"][:, :],
             sb["scrC"][:, tf * 256:(tf + 1) * 256], ALU.mult, inc="v")
        p.wait(V, "v")
        p.op(V, "tensor_tensor", sb["ckk"][:, tf * 256:(tf + 1) * 256],
             sb["kf"][:, :], sb["fstg"][:, 0:256], ALU.subtract, inc="v")
    p.wait(Tn, "v")
    for tf in range(8):
        for ct in range(2):
            ptn = "ptA" if (tf * 2 + ct) % 2 == 0 else "ptB"
            p.wait(Tn, "v", evac[ptn])
            wt = p.op(Tn, "transpose",
                      ps[ptn][0:128, tf * 128:(tf + 1) * 128],
                      sb["cxbf"][:, tf * 256 + ct * 128:
                                 tf * 256 + ct * 128 + 128],
                      sb["idb"][:, :], inc="t")
            p.wait(V, "t", wt)
            evac[ptn] = p.op(V, "tensor_copy",
                             sb["ckx"][:, ct * 1024 + tf * 128:
                                       ct * 1024 + (tf + 1) * 128],
                             ps[ptn][0:128, tf * 128:(tf + 1) * 128], inc="v")

    p.barrier()
    if core_debug:
        tapout("ckk", (128, 2048), BF16)
        tapout("ckx", (128, 2048), BF16)
        p.wait(Sy, "v")
        p.dma(Sy, taps["ckk"], sb["ckk"][:, :], "dO")
        p.dma(Sy, taps["ckx"], sb["ckx"][:, :], "dO")
        for _e in ENGS:
            p.wait(_e, "dO", p.cnt["dO"])

    unit_hist = []
    hsub_s = [0, 0]
    last_vW = [0]
    for sub in range(NSUB):
        s0 = sub * TS
        p.wait(Tn, "v", evac["ptBf"])
        for i2 in range(2):
            tt16 = sub * 2 + i2
            wt = p.op(Tn, "transpose", ps["ptBf"][0:1, 256:384],
                      sb["rrmscols"][:, tt16:tt16 + 1], sb["idf"][:, :],
                      inc="t")
            p.wait(V, "t", wt)
            p.wait(V, "t", rowscr_t[0])
            rs = p.op(V, "tensor_copy",
                      sb["rowscr"][0:1, i2 * 128:(i2 + 1) * 128],
                      ps["ptBf"][0:1, 256:384], inc="v")
            p.wait(Tn, "v", rs)
        wb = p.op(Tn, "matmul", ps["ptBf"][:, 0:256], sb["onesf"][0:1, 0:128],
                  sb["rowscr"][0:1, 0:256], start=True, stop=True, inc="t")
        rowscr_t[0] = wb
        p.wait(V, "t", wb)
        rep_r = p.op(V, "tensor_copy", sb["repsb"][:, 0:256],
                     ps["ptBf"][:, 0:256], inc="v")
        evac["ptBf"] = rep_r
        for k in range(8):
            fsl2 = sb["fstg"][:, (k % 2) * 256:(k % 2) * 256 + 256]
            p.wait(V, "v", rep_r)
            p.wait(V, "s", hsub_s[k % 2])
            hv = p.op(V, "tensor_tensor", fsl2,
                      sb["xnhi"][:, k * 2048 + s0:k * 2048 + s0 + TS],
                      sb["repsb"][:, 0:256], ALU.mult, inc="v")
            p.wait(Sc, "v", hv)
            hsub_s[k % 2] = p.op(Sc, "activation",
                                 sb["hsub"][:, k * 256:(k + 1) * 256], fsl2,
                                 ACT.Identity, bias=0.0,
                                 scale=sb["wrmsfm"][:, k:k + 1], inc="s")
            ka("s")
        hsub_ready = p.cnt["s"]
        o_order = (list(range(16, 24)) + list(range(0, 8)) +
                   list(range(8, 16)) + [24, 25])
        g_evac = {}
        for oi, o in enumerate(o_order):
            tok = ring_load(dr["wint"][o])
            pmname = "pu%d" % (2 + oi % 2)
            pp = ps[pmname][:, 0:256]
            p.wait(Tn, "v", evac[pmname])
            if oi == 0:
                p.wait(Tn, "s", hsub_ready)
            wsl, slot = ring_use(tok)
            for k in range(8):
                p.op(Tn, "matmul", pp, wsl[:, k * 128:(k + 1) * 128],
                     sb["hsub"][:, k * 256:(k + 1) * 256],
                     start=(k == 0), stop=(k == 7),
                     inc="t" if k == 7 else None)
            ring_done(slot)
            wt = p.cnt["t"]
            if 16 <= o < 24:
                tf = o - 16
                p.wait(Sc, "t", wt)
                ws = p.op(Sc, "activation",
                          sb["scrA"][:, tf * 256:(tf + 1) * 256], pp,
                          ACT.Sigmoid, bias=sb["gbs"][:, tf:tf + 1],
                          scale=1.0, inc="s")
                g_evac[tf] = ws
                p.wait(Tn, "s", ws)
                evac[pmname] = 0
            elif o < 8:
                tf = o
                p.wait(Sc, "t", wt)
                ws = p.op(Sc, "activation",
                          sb["scrE"][:, tf * 256:(tf + 1) * 256], pp,
                          ACT.Sigmoid, inc="s")
                p.wait(Tn, "s", ws)
                evac[pmname] = 0
            elif o < 16:
                tf = o - 8
                p.wait(V, "t", wt)
                p.wait(V, "s", g_evac[tf])
                p.op(V, "tensor_tensor",
                     sb["scrB"][:, tf * 256:(tf + 1) * 256], pp,
                     sb["scrA"][:, tf * 256:(tf + 1) * 256], ALU.mult,
                     inc="v")
                p.wait(V, "v")
                evac[pmname] = p.op(V, "tensor_tensor",
                                    sb["scrB"][:, tf * 256:(tf + 1) * 256],
                                    pp, sb["scrB"][:, tf * 256:(tf + 1) * 256],
                                    ALU.subtract, inc="v")
                ka("v")
            elif o == 24:
                p.wait(V, "t", wt)
                p.op(V, "tensor_scalar", sb["qf"][:, :], pp, 0.125, None,
                     ALU.mult, inc="v")
                evac[pmname] = p.op(V, "tensor_scalar", sb["qb"][:, :], pp,
                                    0.125, None, ALU.mult, inc="v")
            else:
                p.wait(V, "t", wt)
                evac[pmname] = p.op(V, "tensor_copy", sb["kf"][:, :], pp,
                                    inc="v")
                for tf in range(8):
                    p.wait(V, "s", g_evac[tf])
                    p.op(V, "tensor_tensor", sb["fstg"][:, 0:256],
                         sb["kf"][:, :],
                         sb["scrA"][:, tf * 256:(tf + 1) * 256], ALU.mult,
                         inc="v")
                    p.wait(V, "v")
                    p.op(V, "tensor_tensor",
                         sb["scrC"][:, tf * 256:(tf + 1) * 256],
                         sb["kf"][:, :], sb["fstg"][:, 0:256], ALU.subtract,
                         inc="v")
                    ka("v")
        p.op(V, "memset", sb["scrA"][:, 0:2048:64], 0.0, inc="v")
        p.wait(V, "v")
        for tf in range(8):
            p.op(V, "tensor_tensor_scan",
                 sb["scrD"][:, tf * 256:(tf + 1) * 256],
                 sb["scrA"][:, tf * 256:(tf + 1) * 256],
                 sb["scrB"][:, tf * 256:(tf + 1) * 256], 0.0, ALU.mult,
                 ALU.add, inc="v")
            if tf % 2 == 1:
                ka("v")
        for tf in range(8):
            p.op(V, "tensor_tensor_scan", sb["fstg"][:, 0:256],
                 sb["scrA"][:, tf * 256:(tf + 1) * 256],
                 sb["scrC"][:, tf * 256:(tf + 1) * 256], 0.0, ALU.mult,
                 ALU.add, inc="v")
            p.wait(V, "v")
            p.op(V, "tensor_copy", sb["scrB"][:, tf * 256:(tf + 1) * 256],
                 sb["fstg"][:, 0:256], inc="v")
            p.wait(V, "v")
            ka("v")
        for tf in range(8):
            p.op(V, "tensor_tensor", sb["fstg"][:, 0:256], sb["qf"][:, :],
                 sb["scrB"][:, tf * 256:(tf + 1) * 256], ALU.mult, inc="v")
            p.wait(V, "v")
            p.op(V, "tensor_copy", sb["tmph"][:, 0:256], sb["fstg"][:, 0:256],
                 inc="v")
            p.wait(V, "v")
            tl_ready = p.op(V, "tensor_tensor", sb["tmpl"][:, :],
                            sb["fstg"][:, 0:256], sb["tmph"][:, 0:256],
                            ALU.subtract, inc="v")
            p.wait(Tn, "v", tl_ready)
            if tf == 0:
                p.wait(Tn, "v", evac["pu2"])
            for tti in range(2):
                col = tti * 16 + 2 * tf
                p.op(Tn, "matmul", ps["pu2"][:, col:col + 2],
                     sb["tmph"][:, tti * 128:(tti + 1) * 128],
                     sb["selb"][:, :], start=True, stop=False)
                p.op(Tn, "matmul", ps["pu2"][:, col:col + 2],
                     sb["tmpl"][:, tti * 128:(tti + 1) * 128],
                     sb["selb"][:, :], start=False, stop=True,
                     inc="t" if (tf == 7 and tti == 1) else None)
        p.wait(V, "t")
        p.op(V, "tensor_copy", sb["ilse"][:, :], ps["pu2"][:, 0:32], inc="v")
        p.wait(V, "v")
        ilse_ready = p.op(V, "tensor_scalar", sb["nilse"][:, :],
                          sb["ilse"][:, :], -1.0, None, ALU.mult, inc="v")
        evac["pu2"] = ilse_ready

        ST = lambda i: sb["stat"][:, 32 + i:32 + i + 1]
        for bt in range(4):
            tti, hb = bt // 2, bt % 2
            gt = sub * 2 + tti
            heads = range(8 * hb, 8 * hb + 8)
            prev_sI = unit_hist[-1]["sI"] if unit_hist else 0
            if len(unit_hist) >= 2:
                p.wait(Tn, "v", unit_hist[-2]["evac_v"])
            p.wait(Tn, "v", ilse_ready)
            BQ = ((ps["pu2"], ps["pu3"], ps["ptAf"], ps["ptBf"]),
                  (ps["pm0"], ps["pm1"], ps["pu0"], ps["pu1"]))
            nb = len(unit_hist)
            scb = list(BQ[nb % 2])
            for j, hd in enumerate(heads):
                sc_ps = scb[j // 2][:, (j % 2) * 256:(j % 2) * 256 + 256]
                rq = (hd % 2) * 64
                tfh = hd // 2
                p.op(Tn, "matmul", sc_ps,
                     sb["qb"][rq:rq + 64, tti * 128:(tti + 1) * 128],
                     sb["ckk"][rq:rq + 64, tfh * 256:(tfh + 1) * 256],
                     start=True, stop=False)
                p.op(Tn, "matmul", sc_ps, sb["ub"][:, :],
                     sb["msb"][0:2, gt * 256:(gt + 1) * 256],
                     start=False, stop=True,
                     inc="t" if j == 7 else None)
            wtS = p.cnt["t"]
            p.wait(V, "t", wtS)
            for j in range(8):
                p.op(V, "tensor_copy", sb["scsB"][:, j * 256:(j + 1) * 256],
                     scb[j // 2][:, (j % 2) * 256:(j % 2) * 256 + 256],
                     inc="v" if j in (3, 7) else None)
            ka("v", p.cnt["v"] - 1)
            ka("v")
            p.wait(V, "v")
            m0_8 = sb["stat"][:, 32:40]
            mm8 = sb["stat"][:, 40:48]
            mn8 = sb["stat"][:, 48:56]
            st8 = sb["stat"][:, 56:64]
            den0 = sb["mx8"][:, 0:8]
            den8 = sb["mx8"][:, 32:40]
            rec8 = sb["mx8"][:, 40:48]
            ils = sb["ilse"][:, tti * 16 + 8 * hb:tti * 16 + 8 * hb + 8]
            scv = sb["scsB"][:, :].rearrange("p (j c) -> p j c", c=256)
            p.op(V, "reduce_max", m0_8, scv, axis=AX, inc="v")
            ka("v")
            p.wait(V, "v")
            p.op(V, "tensor_tensor", mm8, m0_8, ils, ALU.max, inc="v")
            p.wait(V, "v")
            vMN = p.op(V, "tensor_scalar", mn8, mm8, -1.0, None, ALU.mult,
                       inc="v")
            p.wait(V, "v", vMN)
            p.wait(V, "s", prev_sI)
            vST = p.op(V, "tensor_tensor", st8, ils, mn8, ALU.add, inc="v")
            ka("v", vST)
            p.wait(V, "v", vST)
            p.wait(Tn, "v", vST)
            wT1 = p.op(Tn, "transpose", scb[0][0:8, 0:128], mn8,
                       sb["idf"][:, :], inc="t")
            p.wait(V, "t", wT1)
            vR1 = p.op(V, "tensor_copy", sb["r8"][:, :], scb[0][0:8, 0:128],
                       inc="v")
            p.wait(Tn, "v", vR1)
            for q4 in range(4):
                p.op(Tn, "matmul", scb[q4][:, 0:512], sb["r8"][:, :],
                     sb["bo8"][0:8, q4 * 512:(q4 + 1) * 512],
                     start=True, stop=True, inc="t" if q4 == 3 else None)
            wT2 = p.cnt["t"]
            p.wait(V, "t", wT2)
            for q4 in range(4):
                p.op(V, "tensor_tensor",
                     sb["scsB"][:, q4 * 512:(q4 + 1) * 512],
                     scb[q4][:, 0:512],
                     sb["scsB"][:, q4 * 512:(q4 + 1) * 512], ALU.add,
                     inc="v" if q4 in (1, 3) else None)
            vS2 = p.cnt["v"]
            ka("v", vS2 - 1)
            ka("v", vS2)
            p.wait(V, "v", vS2)
            p.wait(Sc, "v", vS2)
            sE = p.op(Sc, "activation", sb["wbig"][:, 0:2048],
                      sb["scsB"][:, :], ACT.Exp, bias=0.0, scale=1.0,
                      inc="s")
            ka("s", sE)
            iw8 = sb["mx8"][:, 48:56]
            sI = p.op(Sc, "activation", iw8, st8, ACT.Exp, bias=0.0,
                      scale=1.0, inc="s")
            p.wait(V, "s", sI)
            wv = sb["wbig"][:, 0:2048].rearrange("p (j c) -> p j c", c=256)
            p.op(V, "reduce_sum", den0, wv, axis=AX, inc="v")
            ka("v")
            p.wait(V, "v")
            p.op(V, "tensor_tensor", den8, den0, iw8, ALU.add, inc="v")
            p.wait(V, "v")
            vD = p.op(V, "reciprocal", rec8, den8, inc="v")
            p.wait(V, "v", vD)
            p.wait(Tn, "v", vD)
            p.op(Tn, "transpose", scb[0][0:8, 0:128], iw8, sb["idf"][:, :])
            wT3 = p.op(Tn, "transpose", scb[0][0:8, 128:256], rec8,
                       sb["idf"][:, :], inc="t")
            p.wait(V, "t", wT3)
            p.op(V, "tensor_copy", sb["r8b"][0:8, 0:128],
                 scb[0][0:8, 0:128])
            vR3 = p.op(V, "tensor_copy", sb["r8b"][0:8, 128:256],
                       scb[0][0:8, 128:256], inc="v")
            p.wait(Tn, "v", vR3)
            for tfl in range(4):
                p.op(Tn, "matmul", scb[0][:, tfl * 128:(tfl + 1) * 128],
                     sb["selh"][0:8, tfl * 128:(tfl + 1) * 128],
                     sb["r8b"][0:8, 0:128], start=True, stop=True)
            for tfl in range(4):
                p.op(Tn, "matmul", scb[1][:, tfl * 128:(tfl + 1) * 128],
                     sb["selh"][0:8, tfl * 128:(tfl + 1) * 128],
                     sb["r8b"][0:8, 128:256], start=True, stop=True)
            p.wait(Tn, "v", last_vW[0])
            for j in range(8):
                for ct in range(2):
                    idx = j * 2 + ct
                    pt = ps["ptA"] if idx < 8 else ps["ptB"]
                    p.op(Tn, "transpose",
                         pt[0:128, (idx % 8) * 128:(idx % 8) * 128 + 128],
                         sb["wbig"][:, j * 256 + ct * 128:
                                    j * 256 + ct * 128 + 128],
                         sb["idb"][:, :],
                         inc="t" if idx == 15 else None)
            wT4 = p.cnt["t"]
            p.wait(V, "t", wT4)
            vW0 = p.op(V, "tensor_copy", sb["wbig"][:, 2048:3072],
                       ps["ptA"][0:128, 0:1024], inc="v")
            vW = p.op(V, "tensor_copy", sb["wbig"][:, 3072:4096],
                      ps["ptB"][0:128, 0:1024], inc="v")
            last_vW[0] = vW
            ka("v", vW0)
            p.wait(Tn, "v", vW)
            for j, hd in enumerate(heads):
                rq = (hd % 2) * 64
                tfl = j // 2
                for ct in range(2):
                    idx = j * 2 + ct
                    p.op(Tn, "matmul",
                         scb[2][rq:rq + 64, tfl * 128:(tfl + 1) * 128],
                         sb["ckx"][:, ct * 1024 + hd * 64:
                                   ct * 1024 + hd * 64 + 64],
                         sb["wbig"][:, 2048 + idx * 128:2048 + idx * 128 + 128],
                         start=(ct == 0), stop=(ct == 1),
                         inc="t" if j == 7 and ct == 1 else None)
            wtO = p.cnt["t"]
            p.wait(V, "t", wtO)
            xkv = sb["scrD"][:, :].rearrange("p (tf c) -> p tf c", c=256)[
                :, 4 * hb:4 * hb + 4, tti * 128:(tti + 1) * 128]
            zv = sb["scrE"][:, :].rearrange("p (tf c) -> p tf c", c=256)[
                :, 4 * hb:4 * hb + 4, tti * 128:(tti + 1) * 128]
            azv = sb["azt"][:, :].rearrange("p (tf c) -> p tf c", c=256)[
                :, 4 * hb:4 * hb + 4, tti * 128:(tti + 1) * 128]
            p.op(V, "tensor_tensor", sb["fstg"][:, :], scb[0][:, 0:512],
                 xkv, ALU.mult, inc="v")
            p.wait(V, "v")
            p.op(V, "tensor_tensor", sb["scsB"][:, 0:512],
                 scb[2][:, 0:512], sb["fstg"][:, :], ALU.add, inc="v")
            ka("v")
            p.wait(V, "v")
            p.op(V, "tensor_tensor", sb["fstg"][:, :], sb["scsB"][:, 0:512],
                 scb[1][:, 0:512], ALU.mult, inc="v")
            p.wait(V, "v")
            ev = p.op(V, "tensor_tensor", azv, sb["fstg"][:, :], zv,
                      ALU.mult, inc="v")
            ka("v", ev)
            unit_hist.append({"evac_v": ev, "sI": sI})
            qnames = (("pu2", "pu3", "ptAf", "ptBf"),
                      ("pm0", "pm1", "pu0", "pu1"))[nb % 2]
            for kk in qnames:
                evac[kk] = ev
            evac["ptA"] = ev
            evac["ptB"] = ev
        p.wait(Tn, "v", unit_hist[-1]["evac_v"])
        for o in range(8):
            tok = ring_load(dr["woutt"][o])
            pmname = "pu%d" % (2 + o % 2)
            pp = ps[pmname][:, 0:256]
            p.wait(Tn, "v", evac[pmname])
            wsl, slot = ring_use(tok)
            for k in range(8):
                p.op(Tn, "matmul", pp, wsl[:, k * 128:(k + 1) * 128],
                     sb["azt"][:, k * 256:(k + 1) * 256],
                     start=(k == 0), stop=(k == 7),
                     inc="t" if k == 7 else None)
            ring_done(slot)
            p.wait(V, "t")
            evac[pmname] = p.op(V, "tensor_tensor",
                                sb["sreg"][:, o * 2048 + s0:
                                           o * 2048 + s0 + TS],
                                pp, sb["xnhi"][:, o * 2048 + s0:
                                               o * 2048 + s0 + TS],
                                ALU.add, inc="v")

    p.barrier()
    if core_debug:
        tapout("sreg", (128, 16384), BF16)
        p.wait(Sy, "v")
        p.dma(Sy, taps["sreg"], sb["sreg"][:, :], "dO")
        for _e in ENGS:
            p.wait(_e, "dO", p.cnt["dO"])

    w1_tcnt = [0, 0]
    w2_tcnt = [0, 0]
    evac_s = {}
    QUADS = (("pm0", "pm1", "pu0", "pu1"), ("pu2", "pu3", "ptAf", "ptBf"))
    b1_scnt = [0, 0]
    stg_vcnt = [0, 0]
    db_tok = {0: db0_early}
    for e_i in range(NE):
        bp = e_i % 2
        db = db_tok[e_i]
        if e_i + 1 < NE:
            bpn = (e_i + 1) % 2
            p.wait(Gp, "s", b1_scnt[bpn])
            db_tok[e_i + 1] = p.dma(Gp, sb["b1s"][:, bpn * 32:(bpn + 1) * 32],
                                    dr["b1t"][e_i + 1], "db%d" % bpn)
        for q4 in range(4):
            pmname = "pm%d" % (q4 % 2)
            pp = ps[pmname][:, 0:512]
            p.wait(Tn, "v", evac[pmname])
            p.wait(Tn, "s", evac_s.get(pmname, 0))
            wt = p.op(Tn, "matmul", pp,
                      sb["sel8"][:, e_i * 128:(e_i + 1) * 128],
                      sb["oh"][0:8, q4 * 512:(q4 + 1) * 512],
                      start=True, stop=True, inc="t")
            p.wait(V, "t", wt)
            evac[pmname] = p.op(V, "tensor_copy",
                                sb["scrE"][:, q4 * 512:(q4 + 1) * 512], pp,
                                inc="v")
        mask_ready = p.cnt["v"]
        for q in range(4):
            p.wait(V, "t")
            h_ready_k = []
            for ft in range(8):
                wp = ft % 2
                p.wait(Gp, "t", w1_tcnt[wp])
                c1 = p.dma(Gp, sb["w1buf"][:, wp * 1024:(wp + 1) * 1024],
                           dr["w1t"][e_i, q, ft], "dW1%d" % wp)
                w1sl = sb["w1buf"][:, wp * 1024:(wp + 1) * 1024]
                banks = QUADS[ft % 2]
                for bn in banks:
                    p.wait(Tn, "v", evac[bn])
                    p.wait(Tn, "s", evac_s.get(bn, 0))
                p.wait(Tn, "dW1%d" % wp, c1)
                for k in range(8):
                    for s4 in range(4):
                        p.op(Tn, "matmul", ps[banks[s4]][:, 0:512],
                             w1sl[:, k * 128:(k + 1) * 128],
                             sb["xnhi"][:, k * 2048 + s4 * 512:
                                        k * 2048 + (s4 + 1) * 512],
                             start=(k == 0), stop=(k == 7),
                             inc="t" if (k == 7 and s4 == 3) else None)
                wt = p.cnt["t"]
                bcol = sb["b1s"][:, bp * 32 + q * 8 + ft:
                                 bp * 32 + q * 8 + ft + 1]
                p.wait(Sc, "t", wt)
                p.wait(Sc, "db%d" % bp, db)
                for s4 in range(4):
                    sp = s4 % 2
                    stgsl = sb["stg"][:, sp * 512:(sp + 1) * 512]
                    p.wait(Sc, "v", stg_vcnt[sp])
                    ws = p.op(Sc, "activation", stgsl, ps[banks[s4]][:, 0:512],
                              ACT.Gelu, bias=bcol, scale=1.0, inc="s")
                    evac_s[banks[s4]] = ws
                    evac[banks[s4]] = 0
                    p.wait(V, "s", ws)
                    p.wait(V, "v", mask_ready)
                    hsl = sb["big"][:, ft * 2048 + s4 * 512:
                                    ft * 2048 + (s4 + 1) * 512]
                    stg_vcnt[sp] = p.op(V, "tensor_tensor", hsl, stgsl,
                                        sb["scrE"][:, s4 * 512:(s4 + 1) * 512],
                                        ALU.mult, inc="v")
                w1_tcnt[wp] = p.cnt["t"]
                h_ready_k.append(p.cnt["v"])
            h_ready = p.cnt["v"]
            for d in range(8):
                wp = d % 2
                p.wait(Gp, "t", w2_tcnt[wp])
                c2 = p.dma(Gp, sb["w2buf"][:, wp * 1024:(wp + 1) * 1024],
                           dr["w2t"][e_i, d, q], "dW2%d" % wp)
                w2sl = sb["w2buf"][:, wp * 1024:(wp + 1) * 1024]
                banks = QUADS[d % 2]
                for bn in banks:
                    p.wait(Tn, "v", evac[bn])
                    p.wait(Tn, "s", evac_s.get(bn, 0))
                p.wait(Tn, "dW2%d" % wp, c2)
                if d > 0:
                    p.wait(Tn, "v", h_ready)
                last = (e_i == NE - 1 and q == 3)
                for k in range(8):
                    if d == 0:
                        p.wait(Tn, "v", h_ready_k[k])
                    for s4 in range(4):
                        p.op(Tn, "matmul", ps[banks[s4]][:, 0:512],
                             w2sl[:, k * 128:(k + 1) * 128],
                             sb["big"][:, k * 2048 + s4 * 512:
                                       k * 2048 + (s4 + 1) * 512],
                             start=(k == 0), stop=(k == 7 and not last),
                             inc="t" if (k == 7 and s4 == 3 and not last)
                             else None)
                if last:
                    for s4 in range(4):
                        p.op(Tn, "matmul", ps[banks[s4]][:, 0:512],
                             sb["b2hi"][0:8, d * 128:(d + 1) * 128],
                             sb["oh"][0:8, s4 * 512:(s4 + 1) * 512],
                             start=False, stop=False)
                        p.op(Tn, "matmul", ps[banks[s4]][:, 0:512],
                             sb["b2lo"][0:8, d * 128:(d + 1) * 128],
                             sb["oh"][0:8, s4 * 512:(s4 + 1) * 512],
                             start=False, stop=True,
                             inc="t" if s4 == 3 else None)
                wt = p.cnt["t"]
                p.wait(V, "t", wt)
                for s4 in range(4):
                    osl = sb["offn"][:, d * 2048 + s4 * 512:
                                     d * 2048 + (s4 + 1) * 512]
                    if e_i == 0 and q == 0:
                        ev = p.op(V, "tensor_copy", osl,
                                  ps[banks[s4]][:, 0:512], inc="v")
                    else:
                        ev = p.op(V, "tensor_tensor", osl,
                                  ps[banks[s4]][:, 0:512], osl, ALU.add,
                                  inc="v")
                    evac[banks[s4]] = ev
                w2_tcnt[wp] = p.cnt["t"]
        b1_scnt[bp] = p.cnt["s"]

    p.barrier()
    if core_debug:
        tapout("offn", (128, 16384), BF16)
        p.wait(Sy, "v")
        p.dma(Sy, taps["offn"], sb["offn"][:, :], "dO")
        for _e in ENGS:
            p.wait(_e, "dO", p.cnt["dO"])

    ka_burst()
    xrl_used = [0, 0]
    ot_stored = [0, 0]
    for tt in range(NTT):
        par = tt % 2
        xr = sb["xrl"][:, par * 1024:(par + 1) * 1024]
        ot = sb["otile"][:, par * 1024:(par + 1) * 1024]
        p.wait(Sy, "v", xrl_used[par])
        dx = p.dma(Sy, xr, dr["xs"][tt * 128:(tt + 1) * 128, :], "dX%d" % par)
        p.wait(Tn, "v", evac["pu3"])
        wtp = p.op(Tn, "transpose", ps["pu3"][0:1, 128:256],
                   sb["pmcols"][:, tt:tt + 1], sb["idf"][:, :], inc="t")
        p.wait(V, "t", wtp)
        p.wait(V, "t", rowscr_t[0])
        rsc = p.op(V, "tensor_copy", sb["rowscr"][0:1, 0:128],
                   ps["pu3"][0:1, 128:256], inc="v")
        p.wait(Tn, "v", rsc)
        wt = p.op(Tn, "matmul", ps["pu3"][:, 0:128], sb["onesf"][0:1, 0:128],
                  sb["rowscr"][0:1, 0:128], start=True, stop=True, inc="t")
        rowscr_t[0] = wt
        ptn = "ptA" if par == 0 else "ptB"
        pt = ps[ptn]
        p.wait(Tn, "v", evac[ptn])
        last_tp = 0
        for d in range(8):
            dsl = slice(d * 2048 + tt * 128, d * 2048 + tt * 128 + 128)
            p.wait(V, "t", wt)
            p.op(V, "tensor_tensor", sb["fstg"][:, 0:128],
                 sb["offn"][:, dsl], ps["pu3"][:, 0:128], ALU.mult, inc="v")
            p.wait(Sc, "v")
            ws = p.op(Sc, "activation", sb["stg"][:, 0:128],
                      sb["fstg"][:, 0:128], ACT.Silu, inc="s")
            p.wait(V, "s", ws)
            p.wait(V, "t", last_tp)
            rv = p.op(V, "tensor_tensor", sb["tmpl"][:, 0:128],
                      sb["stg"][:, 0:128], sb["sreg"][:, dsl], ALU.mult,
                      inc="v")
            p.wait(Tn, "v", rv)
            p.op(Tn, "transpose", pt[0:128, d * 128:(d + 1) * 128],
                 sb["tmpl"][:, 0:128], sb["idb"][:, :], inc="t")
            last_tp = p.cnt["t"]
        evac["pu3"] = p.cnt["v"]
        wt2 = p.cnt["t"]
        p.wait(V, "t", wt2)
        p.wait(V, "dX%d" % par, dx)
        addv = p.op(V, "tensor_tensor", ot, xr, pt[0:128, 0:1024], ALU.add,
                    inc="v")
        evac[ptn] = addv
        evac["pu3"] = addv
        xrl_used[par] = addv
        p.wait(Sy, "v", addv)
        ot_stored[par] = p.dma(Sy, y[tt * 128:(tt + 1) * 128, :], ot, "dO")

    p.wait(Sy, "dO", p.cnt["dO"])
    p.build()
    return nc, taps


def _prep_inputs(L0_attn_hidden_states, L0_attn_rms_weight, L0_attn_W_in,
                 L0_attn_gate_bias, L0_attn_W_out, L0_ffn_Wr, L0_ffn_W1,
                 L0_ffn_b1, L0_ffn_W2, L0_ffn_b2):
    x2 = np.ascontiguousarray(
        np.asarray(L0_attn_hidden_states, np.float32).reshape(16384, D))
    W_in = np.asarray(L0_attn_W_in, np.float32)
    W_out = np.asarray(L0_attn_W_out, np.float32)
    Wr = np.asarray(L0_ffn_Wr, np.float32)
    W1 = np.asarray(L0_ffn_W1, np.float32)
    W2 = np.asarray(L0_ffn_W2, np.float32)
    b1 = np.asarray(L0_ffn_b1, np.float32)
    b2 = np.asarray(L0_ffn_b2, np.float32)
    rms = np.asarray(L0_attn_rms_weight, np.float32)
    gb = np.asarray(L0_attn_gate_bias, np.float32)

    com = {}
    com["xsum"] = np.ascontiguousarray(x2[0::CHUNK])
    com["wint"] = np.ascontiguousarray(
        W_in.reshape(8, 128, 26, 128).transpose(2, 1, 0, 3)
        .reshape(26, 128, 1024))
    com["woutt"] = np.ascontiguousarray(
        W_out.reshape(8, 128, 8, 128).transpose(2, 1, 0, 3)
        .reshape(8, 128, 1024))
    com["wrt"] = np.ascontiguousarray(
        Wr.reshape(8, 128, 8).transpose(1, 0, 2).reshape(128, 64))
    com["w1t"] = np.ascontiguousarray(
        W1.reshape(NE, 8, 128, 32, 128).transpose(0, 3, 2, 1, 4)
        .reshape(NE, 4, 8, 128, 1024))
    com["w2t"] = np.ascontiguousarray(
        W2.reshape(NE, 4, 8, 128, 8, 128).transpose(0, 4, 1, 3, 2, 5)
        .reshape(NE, 8, 4, 128, 1024))
    com["b1t"] = np.ascontiguousarray(
        b1.reshape(NE, 32, 128).transpose(0, 2, 1))
    com["b2m"] = b2
    com["rmswt"] = np.ascontiguousarray(rms.reshape(8, 128).T)
    com["gbt"] = np.ascontiguousarray(gb.reshape(8, 128).T)
    com["ubh"] = np.zeros((2, 128), BF)
    com["ubh"][0, 0:64] = 1
    com["ubh"][1, 64:128] = 1
    com["idfh"] = np.eye(128, dtype=np.float32)
    com["idbh"] = np.eye(128, dtype=np.float32).astype(BF)
    com["onesfh"] = np.ones((1, 256), np.float32)
    com["onesbh"] = np.ones((1, 128), BF)
    selb = np.zeros((128, 2), BF)
    selb[0:64, 0] = 1
    selb[64:128, 1] = 1
    com["selbh"] = selb
    sel8 = np.zeros((8, 1024), BF)
    for e in range(8):
        sel8[e, e * 128:(e + 1) * 128] = 1
    com["sel8h"] = sel8
    com["iota8h"] = np.tile(np.arange(8, dtype=np.float32), (128, 1))
    bo8 = np.zeros((8, 2048), np.float32)
    for h in range(8):
        bo8[h, h * 256:(h + 1) * 256] = 1.0
    com["bo8h"] = bo8
    selh = np.zeros((8, 512), np.float32)
    for tfl in range(4):
        for pp_ in range(128):
            selh[2 * tfl + pp_ // 64, tfl * 128 + pp_] = 1.0
    com["selhh"] = selh

    in_maps = []
    for c in range(NCORES):
        m = dict(com)
        m["xs"] = np.ascontiguousarray(x2[c * T:(c + 1) * T])
        msb = np.zeros((2, 16 * 256), np.float32)
        for tt in range(16):
            c0 = (c * T + tt * 128) // CHUNK
            cc = np.arange(NCH)
            msb[0, tt * 256:(tt + 1) * 256] = np.where(cc >= c0, NEG, 0.0)
            msb[1, tt * 256:(tt + 1) * 256] = np.where(cc >= c0 + 1, NEG, 0.0)
        m["msbh"] = msb.astype(BF)
        in_maps.append(m)
    return in_maps


_NC_CACHE = {}


def kernel(**inputs):
    in_maps = _prep_inputs(**inputs)
    if "nc" not in _NC_CACHE:
        _NC_CACHE["nc"] = build(core_debug=False)[0]
    nc = _NC_CACHE["nc"]
    res = run_bass_kernel_spmd(nc, in_maps, core_ids=list(range(NCORES)))
    out = np.concatenate([np.asarray(r["y"]) for r in res.results], axis=0)
    return out.reshape(1, 16384, D)


# version 14
# speedup vs baseline: 1.2639x; 1.0080x over previous
import numpy as np
import ml_dtypes
import concourse.bass as bass
import concourse.mybir as mybir
from concourse.bass_utils import run_bass_kernel_spmd
from contextlib import ExitStack

F32 = mybir.dt.float32
BF16 = mybir.dt.bfloat16
BF = ml_dtypes.bfloat16
AX = mybir.AxisListType.X
ALU = mybir.AluOpType
ACT = mybir.ActivationFunctionType

NCORES = 8
T = 2048
D = 1024
DH = 64
NHEAD = 16
CHUNK = 64
NCH = 256
FH = 4096
NE = 8
NEG = np.float32(-1e30)
TS = 256
NSUB = T // TS
NTT = T // 128

ENGS = ("sync", "gpsimd", "vector", "scalar", "tensor")
V, Sc, Tn, Sy, Gp = "vector", "scalar", "tensor", "sync", "gpsimd"

BASE = 20480
SB = {}
_off = [BASE]
def _reg(name, nbytes, align=64):
    _off[0] = (_off[0] + align - 1) // align * align
    SB[name] = _off[0]
    _off[0] += nbytes

_reg("xnhi", 32768)
_reg("sreg", 32768)
_reg("big", 32768)
_reg("wring", 4096)
_reg("w1buf", 4096)
_reg("w2buf", 4096)
_reg("scrA", 8192)
_reg("scrB", 8192)
_reg("scrC", 8192)
_reg("scrD", 8192)
_reg("scrE", 4096)
_reg("msb", 8192)
_reg("ckk", 4096)
_reg("ckx", 4096)
_reg("oh", 4096)
_reg("rowscr", 1536)
_reg("pmcols", 64)
_reg("sel8", 2048)
_reg("b2hi", 2048)
_reg("b2lo", 2048)
_reg("idf", 512)
_reg("idb", 256)
_reg("onesf", 1024)
_reg("onesb", 256)
_reg("selb", 32)
_reg("ub", 256)
_reg("wrf", 256)
_reg("wrhib", 128)
_reg("wrlob", 128)
_reg("gbs", 32)
_reg("wrmsfm", 32)
_reg("b1s", 256)
_reg("qf", 1024)
_reg("qb", 512)
_reg("kf", 1024)
_reg("ilse", 128)
_reg("nilse", 128)
_reg("stat", 256)
_reg("tmph", 1024)
_reg("tmpl", 512)
_reg("stg", 2048)
_reg("fstg", 2048)
_reg("istg", 512)
_reg("istg2", 512)
_reg("r8", 512)
_reg("lgt", 128)
_reg("ohtm", 64)
_reg("mx8", 256)
_reg("repsb", 1024)
_reg("hsub", 4096)
_reg("rrmscols", 64)
_reg("warmt", 64)
_reg("wbias", 64)
_reg("iota8", 32)
_reg("idxu", 64)
_reg("bo8", 8192)
_reg("selh", 2048)
_reg("r8b", 1024)
_reg("statln", 256)
_reg("mrepsb", 512)
_reg("rrepsb", 512)
_reg("rrrepsb", 512)
assert _off[0] <= 229376, _off[0]


class Prog:

    def __init__(self, nc):
        self.nc = nc
        self.q = {e: [] for e in ENGS}
        self.sem = {}
        self.cnt = {}
        self.ctx = ExitStack()

    def new_sem(self, name):
        self.sem[name] = self.ctx.enter_context(self.nc.semaphore(name))
        self.cnt[name] = 0

    def op(self, eng, meth, *args, inc=None, n=1, **kw):
        if inc is not None:
            self.cnt[inc] += n
        self.q[eng].append((meth, args, kw, inc, n))
        return self.cnt[inc] if inc else None

    def dma(self, eng, out_ap, in_ap, inc):
        return self.op(eng, "dma_start", inc=inc, n=16, out=out_ap, in_=in_ap)

    def wait(self, eng, sem, val=None):
        v = self.cnt[sem] if val is None else val
        if v > 0:
            self.q[eng].append(("__wait__", (sem, v), {}, None, 0))

    def barrier(self, engs=ENGS, sems=("v", "s", "t")):
        vals = [(sm, self.cnt[sm]) for sm in sems]
        for e in engs:
            for sm, v in vals:
                self.wait(e, sm, v)

    def build(self):
        block = self.ctx.enter_context(self.nc.Block())
        for eng in ENGS:
            items = self.q[eng]
            def body(e, items=items):
                for meth, args, kw, inc, n in items:
                    if meth == "__wait__":
                        e.wait_ge(self.sem[args[0]], args[1])
                    else:
                        ins = getattr(e, meth)(*args, **kw)
                        if inc is not None:
                            ins.then_inc(self.sem[inc], n)
            getattr(block, eng)(body)
        self.ctx.close()


def build(core_debug=False):
    nc = bass.Bass("TRN2", target_bir_lowering=False,
                   detect_race_conditions=False)

    dr = {}
    def din(name, shape, dt=F32):
        dr[name] = nc.dram_tensor(name, list(shape), dt,
                                  kind="ExternalInput").ap()
    din("xs", (T, D))
    din("xsum", (NCH, D))
    din("wint", (26, 128, 1024))
    din("woutt", (8, 128, 1024))
    din("wrt", (128, 64))
    din("w1t", (NE, 4, 8, 128, 1024))
    din("w2t", (NE, 8, 4, 128, 1024))
    din("b1t", (NE, 128, 32))
    din("b2m", (NE, D))
    din("rmswt", (128, 8))
    din("gbt", (128, 8))
    din("msbh", (2, 16 * 256), BF16)
    din("ubh", (2, 128), BF16)
    din("idfh", (128, 128))
    din("idbh", (128, 128), BF16)
    din("onesfh", (1, 256))
    din("onesbh", (1, 128), BF16)
    din("selbh", (128, 2), BF16)
    din("sel8h", (8, 1024), BF16)
    din("iota8h", (128, 8))
    din("bo8h", (8, 2048))
    din("selhh", (8, 512))
    y = nc.dram_tensor("y", [T, D], F32, kind="ExternalOutput").ap()

    taps = {}
    def tapout(name, shape, dt=F32):
        taps[name] = nc.dram_tensor("tap_" + name, list(shape), dt,
                                    kind="ExternalOutput").ap()

    sb = {}
    def S_(name, shape, dt, key=None, extra=0):
        sb[name] = nc.alloc_sbuf_tensor_at(
            name, list(shape), dt, offset=SB[key or name] + extra).ap()
    S_("xnhi", (128, 16384), BF16)
    S_("xrl", (128, 2048), F32, "xnhi")
    S_("otile", (128, 2048), F32, "xnhi", 8192)
    S_("sreg", (128, 16384), BF16)
    S_("big", (128, 16384), BF16)
    S_("wring", (128, 2048), BF16)
    S_("w1buf", (128, 2048), BF16)
    S_("w2buf", (128, 2048), BF16)
    S_("scrA", (128, 2048), F32)
    S_("xa", (128, 2048), F32, "scrA")
    S_("azt", (128, 2048), BF16, "scrA")
    S_("scrB", (128, 2048), F32)
    S_("scrC", (128, 2048), F32)
    S_("scrD", (128, 2048), F32)
    S_("cxbf", (128, 2048), BF16, "scrD")
    S_("offn", (128, 16384), BF16, "scrA")
    S_("scrE", (128, 2048), BF16)
    S_("msb", (2, 4096), BF16)
    S_("ckk", (128, 2048), BF16)
    S_("ckx", (128, 2048), BF16)
    S_("rowscr", (1, 384), F32)
    S_("pmcols", (128, 16), F32)
    S_("sel8", (8, 1024), BF16)
    S_("oh", (8, 2048), BF16)
    S_("b2hi", (8, 1024), BF16)
    S_("b2lo", (8, 1024), BF16)
    S_("b2f32", (8, 1024), F32, "scrB")
    S_("idf", (128, 128), F32)
    S_("idb", (128, 128), BF16)
    S_("onesf", (1, 256), F32)
    S_("onesb", (1, 128), BF16)
    S_("selb", (128, 2), BF16)
    S_("ub", (2, 128), BF16)
    S_("wrf", (128, 64), F32)
    S_("wrhib", (128, 64), BF16)
    S_("wrlob", (128, 64), BF16)
    S_("gbs", (128, 8), F32)
    S_("wrmsfm", (128, 8), F32)
    S_("b1s", (128, 64), F32)
    S_("qf", (128, 256), F32)
    S_("qb", (128, 256), BF16)
    S_("kf", (128, 256), F32)
    S_("ilse", (128, 32), F32)
    S_("nilse", (128, 32), F32)
    S_("stat", (128, 64), F32)
    S_("tmph", (128, 512), BF16)
    S_("tmpl", (128, 256), BF16)
    S_("stg", (128, 1024), BF16)
    S_("fstg", (128, 512), F32)
    S_("istg", (128, 128), F32)
    S_("istg2", (128, 128), F32)
    S_("xloT", (128, 128), BF16, "istg")
    S_("r8", (8, 128), F32)
    S_("lgt", (128, 16), F32)
    S_("ohtm", (128, 16), BF16)
    S_("mx8", (128, 64), F32)
    S_("repsb", (128, 256), F32)
    S_("hsub", (128, 2048), BF16)
    S_("rrmscols", (128, 16), F32)
    S_("warmt", (128, 16), F32)
    S_("wbias", (128, 1), F32)
    S_("dbgln", (128, 512), F32, "scrD")
    S_("dbglg", (128, 128), F32, "scrD", 2048)
    S_("iota8", (128, 8), F32)
    S_("idxu", (128, 16), mybir.dt.uint32)
    S_("bo8", (8, 2048), F32)
    S_("selh", (8, 512), F32)
    S_("r8b", (8, 256), F32)
    S_("scsB", (128, 2048), F32, "scrB")
    S_("wbig", (128, 4096), BF16, "scrC")
    S_("statln", (128, 64), F32)
    S_("mrepsb", (128, 128), F32)
    S_("rrepsb", (128, 128), F32)
    S_("rrrepsb", (128, 128), F32)

    p = Prog(nc)
    ps = {}
    for nm, shape, dt in [
        ("pm0", (128, 512), F32), ("pm1", (128, 512), F32),
        ("pu0", (128, 512), F32), ("pu1", (128, 512), F32),
        ("ptA", (128, 1024), BF16), ("ptB", (128, 1024), BF16),
        ("pu2", (128, 512), F32), ("pu3", (128, 512), F32),
    ]:
        ps[nm] = p.ctx.enter_context(nc.psum_tensor(nm, list(shape), dt)).ap()

    for s in ("dL", "dX0", "dX1", "dR0", "dR1", "dW10", "dW11", "dW20",
              "dW21", "db0", "db1", "dO", "v", "s", "t"):
        p.new_sem(s)
    ps["ptAf"] = ps["ptA"].bitcast(F32)
    ps["ptBf"] = ps["ptB"].bitcast(F32)

    p.dma(Sy, sb["msb"][:, :], dr["msbh"], "dL")
    p.dma(Sy, sb["ub"][:, :], dr["ubh"], "dL")
    p.dma(Sy, sb["idf"][:, :], dr["idfh"], "dL")
    p.dma(Sy, sb["idb"][:, :], dr["idbh"], "dL")
    p.dma(Sy, sb["onesf"][:, :], dr["onesfh"], "dL")
    p.dma(Sy, sb["onesb"][:, :], dr["onesbh"], "dL")
    p.dma(Sy, sb["selb"][:, :], dr["selbh"], "dL")
    p.dma(Sy, sb["sel8"][:, :], dr["sel8h"], "dL")
    p.dma(Sy, sb["iota8"][:, :], dr["iota8h"], "dL")
    p.dma(Sy, sb["bo8"][:, :], dr["bo8h"], "dL")
    p.dma(Sy, sb["selh"][:, :], dr["selhh"], "dL")
    p.dma(Sy, sb["wrf"][:, :], dr["wrt"], "dL")
    p.dma(Sy, sb["gbs"][:, :], dr["gbt"], "dL")
    p.dma(Sy, sb["wrmsfm"][:, :], dr["rmswt"], "dL")
    p.dma(Sy, sb["b2f32"][:, :], dr["b2m"], "dL")
    cload = p.cnt["dL"]
    db0_early = p.dma(Gp, sb["b1s"][:, 0:32], dr["b1t"][0], "db0")

    p.wait(V, "dL", cload)
    p.op(V, "memset", sb["wbias"][:, :], 0.25)
    p.op(V, "memset", sb["statln"][:, :], 0.0)
    p.op(V, "memset", sb["mx8"][:, :], 0.0)
    p.op(V, "reduce_max", sb["warmt"][:, 0:1], sb["wbias"][:, :], axis=AX,
         inc="v")
    p.wait(V, "v")
    p.op(V, "tensor_scalar", sb["warmt"][:, 1:2], sb["wbias"][:, :],
         sb["wbias"][:, :], 2.0, ALU.subtract, ALU.mult)
    p.op(V, "scalar_tensor_tensor", sb["warmt"][:, 1:2], sb["wbias"][:, :],
         sb["wbias"][:, :], sb["wbias"][:, :], ALU.mult, ALU.add)
    p.op(V, "tensor_scalar", sb["warmt"][:, 1:2], sb["wbias"][:, :],
         sb["wbias"][:, :], None, ALU.is_equal, inc="v")
    p.wait(Sc, "v")
    for fn in (ACT.Exp, ACT.Gelu, ACT.Sigmoid, ACT.Silu, ACT.Square,
               ACT.Sqrt, ACT.Identity):
        p.op(Sc, "activation", sb["warmt"][:, 2:3], sb["wbias"][:, :], fn,
             bias=sb["wbias"][:, :], scale=1.0, accum_out=sb["warmt"][:, 3:4])
    p.op(Sc, "nop", inc="s")
    p.op(V, "tensor_copy", sb["wrhib"][:, :], sb["wrf"][:, :], inc="v")
    p.wait(V, "v")
    p.op(V, "tensor_tensor", sb["wrlob"][:, :], sb["wrf"][:, :],
         sb["wrhib"][:, :], ALU.subtract)
    p.op(V, "tensor_copy", sb["b2hi"][:, :], sb["b2f32"][:, :], inc="v")
    p.wait(V, "v")
    p.op(V, "tensor_tensor", sb["b2lo"][:, :], sb["b2f32"][:, :],
         sb["b2hi"][:, :], ALU.subtract, inc="v")

    evac = {k: 0 for k in ("pm0", "pm1", "pu0", "pu1", "pu2", "pu3",
                           "ptA", "ptB", "ptAf", "ptBf")}
    xa_done = [[], []]
    rowscr_t = [0]

    def ka(sem="v", at=None):
        vv = p.cnt[sem] if at is None else at
        if vv > 0:
            p.wait(Tn, sem, vv)
        p.op(Tn, "ldweights", sb["idb"][:, :])

    def ka_burst(n=30):
        for i in range(n):
            p.op(Tn, "ldweights", sb["idb"][:, :])
            p.op(Tn, "ldweights", sb["selb"][:, :])

    def ln_stats(src, dump, so):
        sl = lambda i: sb["statln"][:, so + i:so + i + 1]

        def fv(*args, **kw):
            c = p.op(V, *args, inc="v", **kw)
            p.wait(V, "v", c)
            return c

        fv("reduce_sum", sl(0), src, axis=AX)
        p.wait(Sc, "v")
        p.op(Sc, "activation", dump, src, ACT.Square, bias=0.0, scale=1.0,
             inc="s")
        fv("tensor_scalar", sl(2), sl(0), 1.0 / D, None, ALU.mult)
        fv("tensor_tensor", sl(3), sl(2), sl(2), ALU.mult)
        p.wait(V, "s")
        fv("reduce_sum", sl(1), dump, axis=AX)
        fv("tensor_scalar", sl(4), sl(1), 1.0 / D, None, ALU.mult)
        fv("tensor_tensor", sl(5), sl(4), sl(3), ALU.subtract)

        def newton(tin, eps, b):
            fv("tensor_scalar", sl(b), tin, 1.0, eps, ALU.mult, ALU.add)
            p.wait(Sc, "v")
            p.op(Sc, "activation", sl(b + 1), sl(b), ACT.Sqrt, inc="s")
            p.wait(V, "s")
            fv("reciprocal", sl(b + 2), sl(b + 1))
            fv("tensor_tensor", sl(b + 3), sl(b + 2), sl(b + 2), ALU.mult)
            fv("tensor_tensor", sl(b + 4), sl(b), sl(b + 3), ALU.mult)
            fv("tensor_scalar", sl(b + 5), sl(b + 4), -0.5, 1.5, ALU.mult,
               ALU.add)
            fv("tensor_tensor", sl(b + 6), sl(b + 2), sl(b + 5), ALU.mult)
            return sl(b + 6)

        rstd = newton(sl(5), 1e-5, 8)
        fv("tensor_scalar", sl(16), sl(2), -1.0, None, ALU.mult)
        fv("tensor_tensor", sl(17), rstd, rstd, ALU.mult)
        fv("tensor_tensor", sl(18), sl(5), sl(17), ALU.mult)
        rrms = newton(sl(18), 1e-6, 20)
        return sl(16), rstd, rrms

    rowps = lambda: ps["pu2"][0:1, 128:256]
    repps = lambda: ps["pu2"][:, 0:128]

    def make_rep(col, rep_sb):
        p.wait(Tn, "v")
        p.wait(Tn, "v", evac["pu2"])
        wt = p.op(Tn, "transpose", rowps(), col, sb["idf"][:, :], inc="t")
        p.wait(V, "t", wt)
        p.wait(V, "t", rowscr_t[0])
        rs = p.op(V, "tensor_copy", sb["rowscr"][0:1, 0:128], rowps(),
                  inc="v")
        p.wait(Tn, "v", rs)
        wb = p.op(Tn, "matmul", repps(), sb["onesf"][0:1, 0:128],
                  sb["rowscr"][0:1, 0:128], start=True, stop=True, inc="t")
        rowscr_t[0] = wb
        p.wait(V, "t", wb)
        ev = p.op(V, "tensor_copy", rep_sb, repps(), inc="v")
        evac["pu2"] = ev
        return ev

    hs_guard = [0, 0]
    for tt in range(NTT):
        par = tt % 2
        so = 32 * par
        xa_sl = sb["xa"][:, par * 1024:(par + 1) * 1024]
        for semn, vv in xa_done[par]:
            p.wait(Sy, semn, vv)
        dx = p.dma(Sy, xa_sl, dr["xs"][tt * 128:(tt + 1) * 128, :],
                   "dX%d" % par)
        p.wait(V, "dX%d" % par, dx)
        p.wait(Sc, "dX%d" % par, dx)
        dump = sb["scrB"][:, par * 1024:(par + 1) * 1024]
        negm, rstd, rrms = ln_stats(xa_sl, dump, so)
        if core_debug:
            p.op(V, "tensor_copy", sb["dbgln"][:, tt * 32:(tt + 1) * 32],
                 sb["statln"][:, so:so + 32])
        p.op(V, "tensor_copy", sb["rrmscols"][:, tt:tt + 1], rrms, inc="v")
        make_rep(negm, sb["mrepsb"][:, :])
        make_rep(rstd, sb["rrepsb"][:, :])

        for k in range(8):
            pmname = "pm%d" % (k % 2)
            fx = ps[pmname][:, 0:128]
            p.wait(Tn, "v", evac[pmname])
            wt = p.op(Tn, "transpose", fx, xa_sl[:, k * 128:(k + 1) * 128],
                      sb["idf"][:, :], inc="t")
            fsl = sb["fstg"][:, (k % 2) * 128:(k % 2) * 128 + 128]
            p.wait(V, "t", wt)
            p.wait(V, "s", hs_guard[k % 2])
            p.op(V, "tensor_tensor", fsl, fx, sb["mrepsb"][:, :], ALU.add,
                 inc="v")
            evac[pmname] = p.cnt["v"]
            p.wait(V, "v")
            p.op(V, "tensor_tensor", sb["istg2"][:, :], fsl,
                 sb["rrepsb"][:, :], ALU.mult, inc="v")
            p.wait(V, "v")
            hi_sl = sb["xnhi"][:, k * 2048 + tt * 128:k * 2048 + tt * 128 + 128]
            p.op(V, "tensor_copy", hi_sl, sb["istg2"][:, :], inc="v")
            p.wait(V, "v")
            lo_ready = p.op(V, "tensor_tensor", sb["xloT"][:, :],
                            sb["istg2"][:, :], hi_sl, ALU.subtract, inc="v")
            p.wait(Tn, "v", lo_ready)
            if k == 0:
                p.wait(Tn, "v", evac["pu1"])
            p.op(Tn, "matmul", ps["pu1"][0:8, 0:128],
                 sb["wrhib"][:, k * 8:(k + 1) * 8], hi_sl,
                 start=(k == 0), stop=False)
            p.op(Tn, "matmul", ps["pu1"][0:8, 0:128],
                 sb["wrhib"][:, k * 8:(k + 1) * 8], sb["xloT"][:, :],
                 start=False, stop=False)
            rmm = p.op(Tn, "matmul", ps["pu1"][0:8, 0:128],
                       sb["wrlob"][:, k * 8:(k + 1) * 8], hi_sl,
                       start=False, stop=(k == 7), inc="t")
            p.wait(V, "t", rmm)
        xa_done[par] = [("v", p.cnt["v"]), ("s", p.cnt["s"]),
                        ("t", p.cnt["t"])]

        p.wait(V, "t")
        ev = p.op(V, "tensor_copy", sb["r8"][:, :], ps["pu1"][0:8, 0:128],
                  inc="v")
        evac["pu1"] = ev
        p.wait(Tn, "v", ev)
        p.wait(Tn, "v", evac["pu0"])
        wt = p.op(Tn, "transpose", ps["pu0"][0:128, 0:8], sb["r8"][:, :],
                  sb["idf"][0:8, 0:8], inc="t")
        lg = sb["lgt"][:, par * 8:(par + 1) * 8]
        ohc = sb["ohtm"][:, par * 8:(par + 1) * 8]
        m8 = lambda i: sb["mx8"][:, 32 * par + i:32 * par + i + 1]
        vmax8 = sb["mx8"][:, 32 * par + 8:32 * par + 16]
        vidx8 = sb["mx8"][:, 32 * par + 16:32 * par + 24]
        p.wait(V, "t", wt)
        evac["pu0"] = p.op(V, "tensor_copy", lg, ps["pu0"][:, 0:8], inc="v")
        p.wait(V, "v")
        if core_debug:
            p.op(V, "tensor_copy", sb["dbglg"][:, tt * 8:(tt + 1) * 8], lg)
        p.op(V, "max", vmax8, lg, inc="v")
        p.wait(V, "v")
        idxu = sb["idxu"][:, par * 8:(par + 1) * 8]
        p.op(V, "max_index", idxu, vmax8, lg, inc="v")
        p.wait(V, "v")
        mxi = p.op(V, "tensor_copy", vidx8, idxu, inc="v")
        p.wait(V, "v", mxi)
        p.op(V, "tensor_copy", m8(0), vmax8[:, 0:1], inc="v")
        p.wait(Tn, "v", mxi)
        p.wait(Tn, "v", evac["pu3"])
        w1 = p.op(Tn, "transpose", ps["pu3"][0:1, 0:128],
                  vidx8[:, 0:1], sb["idf"][:, :], inc="t")
        p.wait(V, "t", w1)
        p.wait(V, "t", rowscr_t[0])
        rs = p.op(V, "tensor_copy", sb["rowscr"][0:1, 0:128],
                  ps["pu3"][0:1, 0:128], inc="v")
        p.wait(Tn, "v", rs)
        w2 = p.op(Tn, "matmul", ps["pu3"][:, 128:136],
                  sb["rowscr"][0:1, 0:128], sb["onesf"][0:1, 0:8],
                  start=True, stop=True, inc="t")
        rowscr_t[0] = w2
        p.wait(V, "t", w2)
        oh_v = p.op(V, "tensor_tensor", ohc, sb["iota8"][:, :],
                    ps["pu3"][:, 128:136], ALU.is_equal, inc="v")
        evac["pu3"] = oh_v
        p.wait(Sc, "v")
        p.op(Sc, "activation", sb["warmt"][:, 8:16], lg, ACT.Exp, bias=0.0,
             scale=1.0, inc="s")
        ws2 = p.op(Sc, "activation", m8(4), m8(0), ACT.Exp, bias=0.0,
                   scale=1.0, inc="s")
        p.wait(V, "s", ws2)
        p.op(V, "reduce_sum", m8(2), sb["warmt"][:, 8:16], axis=AX, inc="v")
        p.wait(V, "v")
        p.op(V, "reciprocal", m8(5), m8(2), inc="v")
        p.wait(V, "v")
        pm_ready = p.op(V, "tensor_tensor", sb["pmcols"][:, tt:tt + 1],
                        m8(4), m8(5), ALU.mult, inc="v")
        p.wait(Tn, "v", oh_v)
        p.wait(Tn, "v", evac["ptA"])
        wt2 = p.op(Tn, "transpose", ps["ptA"][0:8, 0:128], ohc,
                   sb["idb"][:, :], inc="t")
        p.wait(V, "t", wt2)
        evac["ptA"] = p.op(V, "tensor_copy",
                           sb["oh"][0:8, tt * 128:(tt + 1) * 128],
                           ps["ptA"][0:8, 0:128], inc="v")

    p.barrier()
    if core_debug:
        tapout("xnhi", (128, 16384), BF16)
        tapout("pmcols", (128, 16))
        tapout("oh", (8, 2048), BF16)
        tapout("dbgln", (128, 512))
        tapout("dbglg", (128, 128))
        p.wait(Sy, "v")
        p.dma(Sy, taps["dbgln"], sb["dbgln"][:, :], "dO")
        p.dma(Sy, taps["dbglg"], sb["dbglg"][:, :], "dO")
        p.dma(Sy, taps["xnhi"], sb["xnhi"][:, :], "dO")
        p.dma(Sy, taps["pmcols"], sb["pmcols"][:, :], "dO")
        p.dma(Sy, taps["oh"], sb["oh"][:, :], "dO")
        for _e in ENGS:
            p.wait(_e, "dO", p.cnt["dO"])

    ring_state = {"n": 0, "tcnt": [0, 0]}

    def ring_load(src_ap):
        j = ring_state["n"]
        slot = j % 2
        p.wait(Gp, "t", ring_state["tcnt"][slot])
        c = p.dma(Gp, sb["wring"][:, slot * 1024:(slot + 1) * 1024], src_ap,
                  "dR%d" % slot)
        ring_state["n"] = j + 1
        return ("dR%d" % slot, c, slot)

    def ring_use(tok):
        semn, c, slot = tok
        p.wait(Tn, semn, c)
        return sb["wring"][:, slot * 1024:(slot + 1) * 1024], slot

    def ring_done(slot):
        ring_state["tcnt"][slot] = p.cnt["t"]

    for st2 in range(2):
        par = st2
        so = 32 * par
        xa_sl = sb["xa"][:, par * 1024:(par + 1) * 1024]
        for semn, vv in xa_done[par]:
            p.wait(Sy, semn, vv)
        dx = p.dma(Sy, xa_sl, dr["xsum"][st2 * 128:(st2 + 1) * 128, :],
                   "dX%d" % par)
        p.wait(V, "dX%d" % par, dx)
        p.wait(Sc, "dX%d" % par, dx)
        dump = sb["scrB"][:, par * 1024:(par + 1) * 1024]
        negm, rstd, rrms = ln_stats(xa_sl, dump, so)
        make_rep(negm, sb["mrepsb"][:, :])
        make_rep(rstd, sb["rrepsb"][:, :])
        make_rep(rrms, sb["rrrepsb"][:, :])
        p.wait(V, "v")
        for k in range(8):
            pmname = "pm%d" % (k % 2)
            fx = ps[pmname][:, 0:128]
            p.wait(Tn, "v", evac[pmname])
            wt = p.op(Tn, "transpose", fx, xa_sl[:, k * 128:(k + 1) * 128],
                      sb["idf"][:, :], inc="t")
            fsl = sb["fstg"][:, (k % 2) * 128:(k % 2) * 128 + 128]
            p.wait(V, "t", wt)
            p.wait(V, "s", hs_guard[k % 2])
            p.op(V, "tensor_tensor", fsl, fx, sb["mrepsb"][:, :], ALU.add,
                 inc="v")
            evac[pmname] = p.cnt["v"]
            p.op(V, "tensor_tensor", sb["istg2"][:, :], fsl,
                 sb["rrepsb"][:, :], ALU.mult)
            tv = p.op(V, "tensor_tensor", fsl, sb["istg2"][:, :],
                      sb["rrrepsb"][:, :], ALU.mult, inc="v")
            p.wait(Sc, "v", tv)
            hs_sl = sb["scrE"][:, k * 256 + st2 * 128:k * 256 + st2 * 128 + 128]
            hs_guard[k % 2] = p.op(Sc, "activation", hs_sl, fsl, ACT.Identity,
                                   bias=0.0, scale=sb["wrmsfm"][:, k:k + 1],
                                   inc="s")
        xa_done[par] = [("v", p.cnt["v"]), ("s", p.cnt["s"]),
                        ("t", p.cnt["t"])]

    hs_ready = p.cnt["v"]
    gsum_s = {}
    kf_ready = 0
    for oi, o in enumerate(list(range(16, 24)) + list(range(8, 16)) + [25]):
        tok = ring_load(dr["wint"][o])
        pmname = "pm%d" % (oi % 2)
        pp = ps[pmname][:, 0:256]
        p.wait(Tn, "v", evac[pmname])
        p.wait(Tn, "v", hs_ready)
        wsl, slot = ring_use(tok)
        for k in range(8):
            p.op(Tn, "matmul", pp, wsl[:, k * 128:(k + 1) * 128],
                 sb["scrE"][:, k * 256:(k + 1) * 256],
                 start=(k == 0), stop=(k == 7), inc="t" if k == 7 else None)
        ring_done(slot)
        wt = p.cnt["t"]
        if 16 <= o < 24:
            tf = o - 16
            p.wait(Sc, "t", wt)
            ws = p.op(Sc, "activation", sb["scrC"][:, tf * 256:(tf + 1) * 256],
                      pp, ACT.Sigmoid, bias=sb["gbs"][:, tf:tf + 1], scale=1.0,
                      inc="s")
            gsum_s[tf] = ws
            p.wait(Tn, "s", ws)
            evac[pmname] = 0
        elif 8 <= o < 16:
            tf = o - 8
            p.wait(V, "t", wt)
            p.wait(V, "s", gsum_s[tf])
            p.op(V, "tensor_tensor", sb["fstg"][:, 0:256], pp,
                 sb["scrC"][:, tf * 256:(tf + 1) * 256], ALU.mult, inc="v")
            p.wait(V, "v")
            evac[pmname] = p.op(V, "tensor_tensor",
                                sb["cxbf"][:, tf * 256:(tf + 1) * 256], pp,
                                sb["fstg"][:, 0:256], ALU.subtract, inc="v")
        else:
            p.wait(V, "t", wt)
            kf_ready = p.op(V, "tensor_copy", sb["kf"][:, :], pp, inc="v")
            evac[pmname] = kf_ready
    for tf in range(8):
        p.wait(V, "v", kf_ready)
        p.wait(V, "s", gsum_s[tf])
        p.op(V, "tensor_tensor", sb["fstg"][:, 0:256], sb["kf"][:, :],
             sb["scrC"][:, tf * 256:(tf + 1) * 256], ALU.mult, inc="v")
        p.wait(V, "v")
        p.op(V, "tensor_tensor", sb["ckk"][:, tf * 256:(tf + 1) * 256],
             sb["kf"][:, :], sb["fstg"][:, 0:256], ALU.subtract, inc="v")
    p.wait(Tn, "v")
    for tf in range(8):
        for ct in range(2):
            ptn = "ptA" if (tf * 2 + ct) % 2 == 0 else "ptB"
            p.wait(Tn, "v", evac[ptn])
            wt = p.op(Tn, "transpose",
                      ps[ptn][0:128, tf * 128:(tf + 1) * 128],
                      sb["cxbf"][:, tf * 256 + ct * 128:
                                 tf * 256 + ct * 128 + 128],
                      sb["idb"][:, :], inc="t")
            p.wait(V, "t", wt)
            evac[ptn] = p.op(V, "tensor_copy",
                             sb["ckx"][:, ct * 1024 + tf * 128:
                                       ct * 1024 + (tf + 1) * 128],
                             ps[ptn][0:128, tf * 128:(tf + 1) * 128], inc="v")

    p.barrier()
    if core_debug:
        tapout("ckk", (128, 2048), BF16)
        tapout("ckx", (128, 2048), BF16)
        p.wait(Sy, "v")
        p.dma(Sy, taps["ckk"], sb["ckk"][:, :], "dO")
        p.dma(Sy, taps["ckx"], sb["ckx"][:, :], "dO")
        for _e in ENGS:
            p.wait(_e, "dO", p.cnt["dO"])

    unit_hist = []
    hsub_s = [0, 0]
    last_vW = [0]
    for sub in range(NSUB):
        s0 = sub * TS
        p.wait(Tn, "v", evac["ptBf"])
        for i2 in range(2):
            tt16 = sub * 2 + i2
            wt = p.op(Tn, "transpose", ps["ptBf"][0:1, 256:384],
                      sb["rrmscols"][:, tt16:tt16 + 1], sb["idf"][:, :],
                      inc="t")
            p.wait(V, "t", wt)
            p.wait(V, "t", rowscr_t[0])
            rs = p.op(V, "tensor_copy",
                      sb["rowscr"][0:1, i2 * 128:(i2 + 1) * 128],
                      ps["ptBf"][0:1, 256:384], inc="v")
            p.wait(Tn, "v", rs)
        wb = p.op(Tn, "matmul", ps["ptBf"][:, 0:256], sb["onesf"][0:1, 0:128],
                  sb["rowscr"][0:1, 0:256], start=True, stop=True, inc="t")
        rowscr_t[0] = wb
        p.wait(V, "t", wb)
        rep_r = p.op(V, "tensor_copy", sb["repsb"][:, 0:256],
                     ps["ptBf"][:, 0:256], inc="v")
        evac["ptBf"] = rep_r
        for k in range(8):
            fsl2 = sb["fstg"][:, (k % 2) * 256:(k % 2) * 256 + 256]
            p.wait(V, "v", rep_r)
            p.wait(V, "s", hsub_s[k % 2])
            hv = p.op(V, "tensor_tensor", fsl2,
                      sb["xnhi"][:, k * 2048 + s0:k * 2048 + s0 + TS],
                      sb["repsb"][:, 0:256], ALU.mult, inc="v")
            p.wait(Sc, "v", hv)
            hsub_s[k % 2] = p.op(Sc, "activation",
                                 sb["hsub"][:, k * 256:(k + 1) * 256], fsl2,
                                 ACT.Identity, bias=0.0,
                                 scale=sb["wrmsfm"][:, k:k + 1], inc="s")
        hsub_ready = p.cnt["s"]
        o_order = (list(range(16, 24)) + list(range(0, 8)) +
                   list(range(8, 16)) + [24, 25])
        g_evac = {}
        for oi, o in enumerate(o_order):
            tok = ring_load(dr["wint"][o])
            pmname = "pu%d" % (2 + oi % 2)
            pp = ps[pmname][:, 0:256]
            p.wait(Tn, "v", evac[pmname])
            if oi == 0:
                p.wait(Tn, "s", hsub_ready)
            wsl, slot = ring_use(tok)
            for k in range(8):
                p.op(Tn, "matmul", pp, wsl[:, k * 128:(k + 1) * 128],
                     sb["hsub"][:, k * 256:(k + 1) * 256],
                     start=(k == 0), stop=(k == 7),
                     inc="t" if k == 7 else None)
            ring_done(slot)
            wt = p.cnt["t"]
            if 16 <= o < 24:
                tf = o - 16
                p.wait(Sc, "t", wt)
                ws = p.op(Sc, "activation",
                          sb["scrA"][:, tf * 256:(tf + 1) * 256], pp,
                          ACT.Sigmoid, bias=sb["gbs"][:, tf:tf + 1],
                          scale=1.0, inc="s")
                g_evac[tf] = ws
                p.wait(Tn, "s", ws)
                evac[pmname] = 0
            elif o < 8:
                tf = o
                p.wait(Sc, "t", wt)
                ws = p.op(Sc, "activation",
                          sb["scrE"][:, tf * 256:(tf + 1) * 256], pp,
                          ACT.Sigmoid, inc="s")
                p.wait(Tn, "s", ws)
                evac[pmname] = 0
            elif o < 16:
                tf = o - 8
                p.wait(V, "t", wt)
                p.wait(V, "s", g_evac[tf])
                p.op(V, "tensor_tensor",
                     sb["scrB"][:, tf * 256:(tf + 1) * 256], pp,
                     sb["scrA"][:, tf * 256:(tf + 1) * 256], ALU.mult,
                     inc="v")
                p.wait(V, "v")
                evac[pmname] = p.op(V, "tensor_tensor",
                                    sb["scrB"][:, tf * 256:(tf + 1) * 256],
                                    pp, sb["scrB"][:, tf * 256:(tf + 1) * 256],
                                    ALU.subtract, inc="v")
            elif o == 24:
                p.wait(V, "t", wt)
                p.op(V, "tensor_scalar", sb["qf"][:, :], pp, 0.125, None,
                     ALU.mult, inc="v")
                evac[pmname] = p.op(V, "tensor_scalar", sb["qb"][:, :], pp,
                                    0.125, None, ALU.mult, inc="v")
            else:
                p.wait(V, "t", wt)
                evac[pmname] = p.op(V, "tensor_copy", sb["kf"][:, :], pp,
                                    inc="v")
                for tf in range(8):
                    p.wait(V, "s", g_evac[tf])
                    p.op(V, "tensor_tensor", sb["fstg"][:, 0:256],
                         sb["kf"][:, :],
                         sb["scrA"][:, tf * 256:(tf + 1) * 256], ALU.mult,
                         inc="v")
                    p.wait(V, "v")
                    p.op(V, "tensor_tensor",
                         sb["scrC"][:, tf * 256:(tf + 1) * 256],
                         sb["kf"][:, :], sb["fstg"][:, 0:256], ALU.subtract,
                         inc="v")
        p.op(V, "memset", sb["scrA"][:, 0:2048:64], 0.0, inc="v")
        p.wait(V, "v")
        for tf in range(8):
            p.op(V, "tensor_tensor_scan",
                 sb["scrD"][:, tf * 256:(tf + 1) * 256],
                 sb["scrA"][:, tf * 256:(tf + 1) * 256],
                 sb["scrB"][:, tf * 256:(tf + 1) * 256], 0.0, ALU.mult,
                 ALU.add, inc="v")
        for tf in range(8):
            p.op(V, "tensor_tensor_scan", sb["fstg"][:, 0:256],
                 sb["scrA"][:, tf * 256:(tf + 1) * 256],
                 sb["scrC"][:, tf * 256:(tf + 1) * 256], 0.0, ALU.mult,
                 ALU.add, inc="v")
            p.wait(V, "v")
            p.op(V, "tensor_copy", sb["scrB"][:, tf * 256:(tf + 1) * 256],
                 sb["fstg"][:, 0:256], inc="v")
            p.wait(V, "v")
        for tf in range(8):
            p.op(V, "tensor_tensor", sb["fstg"][:, 0:256], sb["qf"][:, :],
                 sb["scrB"][:, tf * 256:(tf + 1) * 256], ALU.mult, inc="v")
            p.wait(V, "v")
            p.op(V, "tensor_copy", sb["tmph"][:, 0:256], sb["fstg"][:, 0:256],
                 inc="v")
            p.wait(V, "v")
            tl_ready = p.op(V, "tensor_tensor", sb["tmpl"][:, :],
                            sb["fstg"][:, 0:256], sb["tmph"][:, 0:256],
                            ALU.subtract, inc="v")
            p.wait(Tn, "v", tl_ready)
            if tf == 0:
                p.wait(Tn, "v", evac["pu2"])
            for tti in range(2):
                col = tti * 16 + 2 * tf
                p.op(Tn, "matmul", ps["pu2"][:, col:col + 2],
                     sb["tmph"][:, tti * 128:(tti + 1) * 128],
                     sb["selb"][:, :], start=True, stop=False)
                p.op(Tn, "matmul", ps["pu2"][:, col:col + 2],
                     sb["tmpl"][:, tti * 128:(tti + 1) * 128],
                     sb["selb"][:, :], start=False, stop=True,
                     inc="t" if (tf == 7 and tti == 1) else None)
        p.wait(V, "t")
        p.op(V, "tensor_copy", sb["ilse"][:, :], ps["pu2"][:, 0:32], inc="v")
        p.wait(V, "v")
        ilse_ready = p.op(V, "tensor_scalar", sb["nilse"][:, :],
                          sb["ilse"][:, :], -1.0, None, ALU.mult, inc="v")
        evac["pu2"] = ilse_ready

        ST = lambda i: sb["stat"][:, 32 + i:32 + i + 1]
        for bt in range(4):
            tti, hb = bt // 2, bt % 2
            gt = sub * 2 + tti
            heads = range(8 * hb, 8 * hb + 8)
            prev_sI = unit_hist[-1]["sI"] if unit_hist else 0
            if len(unit_hist) >= 2:
                p.wait(Tn, "v", unit_hist[-2]["evac_v"])
            p.wait(Tn, "v", ilse_ready)
            BQ = ((ps["pu2"], ps["pu3"], ps["ptAf"], ps["ptBf"]),
                  (ps["pm0"], ps["pm1"], ps["pu0"], ps["pu1"]))
            nb = len(unit_hist)
            scb = list(BQ[nb % 2])
            for j, hd in enumerate(heads):
                sc_ps = scb[j // 2][:, (j % 2) * 256:(j % 2) * 256 + 256]
                rq = (hd % 2) * 64
                tfh = hd // 2
                p.op(Tn, "matmul", sc_ps,
                     sb["qb"][rq:rq + 64, tti * 128:(tti + 1) * 128],
                     sb["ckk"][rq:rq + 64, tfh * 256:(tfh + 1) * 256],
                     start=True, stop=False)
                p.op(Tn, "matmul", sc_ps, sb["ub"][:, :],
                     sb["msb"][0:2, gt * 256:(gt + 1) * 256],
                     start=False, stop=True,
                     inc="t" if j == 7 else None)
            wtS = p.cnt["t"]
            p.wait(Sc, "t", wtS)
            for j in range(8):
                p.op(Sc, "activation",
                     sb["scsB"][:, j * 256:(j + 1) * 256],
                     scb[j // 2][:, (j % 2) * 256:(j % 2) * 256 + 256],
                     ACT.Identity, bias=0.0, scale=1.0,
                     inc="s" if j == 7 else None)
            sCP = p.cnt["s"]
            p.wait(V, "s", sCP)
            m0_8 = sb["stat"][:, 32:40]
            mm8 = sb["stat"][:, 40:48]
            mn8 = sb["stat"][:, 48:56]
            st8 = sb["stat"][:, 56:64]
            den0 = sb["mx8"][:, 0:8]
            den8 = sb["mx8"][:, 32:40]
            rec8 = sb["mx8"][:, 40:48]
            ils = sb["ilse"][:, tti * 16 + 8 * hb:tti * 16 + 8 * hb + 8]
            scv = sb["scsB"][:, :].rearrange("p (j c) -> p j c", c=256)
            p.op(V, "reduce_max", m0_8, scv, axis=AX, inc="v")
            p.wait(V, "v")
            p.op(V, "tensor_tensor", mm8, m0_8, ils, ALU.max, inc="v")
            p.wait(V, "v")
            vMN = p.op(V, "tensor_scalar", mn8, mm8, -1.0, None, ALU.mult,
                       inc="v")
            p.wait(V, "v", vMN)
            p.wait(V, "s", prev_sI)
            vST = p.op(V, "tensor_tensor", st8, ils, mn8, ALU.add, inc="v")
            p.wait(V, "v", vST)
            p.wait(Tn, "v", vST)
            wT1 = p.op(Tn, "transpose", scb[0][0:8, 0:128], mn8,
                       sb["idf"][:, :], inc="t")
            p.wait(V, "t", wT1)
            vR1 = p.op(V, "tensor_copy", sb["r8"][:, :], scb[0][0:8, 0:128],
                       inc="v")
            p.wait(Tn, "v", vR1)
            for q4 in range(4):
                p.op(Tn, "matmul", scb[q4][:, 0:512], sb["r8"][:, :],
                     sb["bo8"][0:8, q4 * 512:(q4 + 1) * 512],
                     start=True, stop=True, inc="t" if q4 == 3 else None)
            wT2 = p.cnt["t"]
            p.wait(V, "t", wT2)
            for q4 in range(4):
                p.op(V, "tensor_tensor",
                     sb["scsB"][:, q4 * 512:(q4 + 1) * 512],
                     scb[q4][:, 0:512],
                     sb["scsB"][:, q4 * 512:(q4 + 1) * 512], ALU.add,
                     inc="v" if q4 in (1, 3) else None)
            vS2 = p.cnt["v"]
            p.wait(V, "v", vS2)
            p.wait(Sc, "v", vS2)
            sE = p.op(Sc, "activation", sb["wbig"][:, 0:2048],
                      sb["scsB"][:, :], ACT.Exp, bias=0.0, scale=1.0,
                      inc="s")
            iw8 = sb["mx8"][:, 48:56]
            sI = p.op(Sc, "activation", iw8, st8, ACT.Exp, bias=0.0,
                      scale=1.0, inc="s")
            p.wait(V, "s", sI)
            wv = sb["wbig"][:, 0:2048].rearrange("p (j c) -> p j c", c=256)
            p.op(V, "reduce_sum", den0, wv, axis=AX, inc="v")
            p.wait(V, "v")
            p.op(V, "tensor_tensor", den8, den0, iw8, ALU.add, inc="v")
            p.wait(V, "v")
            vD = p.op(V, "reciprocal", rec8, den8, inc="v")
            p.wait(V, "v", vD)
            p.wait(Tn, "v", vD)
            p.op(Tn, "transpose", scb[0][0:8, 0:128], iw8, sb["idf"][:, :])
            wT3 = p.op(Tn, "transpose", scb[0][0:8, 128:256], rec8,
                       sb["idf"][:, :], inc="t")
            p.wait(V, "t", wT3)
            p.op(V, "tensor_copy", sb["r8b"][0:8, 0:128],
                 scb[0][0:8, 0:128])
            vR3 = p.op(V, "tensor_copy", sb["r8b"][0:8, 128:256],
                       scb[0][0:8, 128:256], inc="v")
            p.wait(Tn, "v", vR3)
            for tfl in range(4):
                p.op(Tn, "matmul", scb[0][:, tfl * 128:(tfl + 1) * 128],
                     sb["selh"][0:8, tfl * 128:(tfl + 1) * 128],
                     sb["r8b"][0:8, 0:128], start=True, stop=True)
            for tfl in range(4):
                p.op(Tn, "matmul", scb[1][:, tfl * 128:(tfl + 1) * 128],
                     sb["selh"][0:8, tfl * 128:(tfl + 1) * 128],
                     sb["r8b"][0:8, 128:256], start=True, stop=True)
            p.wait(Tn, "v", last_vW[0])
            for j in range(8):
                for ct in range(2):
                    idx = j * 2 + ct
                    pt = ps["ptA"] if idx < 8 else ps["ptB"]
                    p.op(Tn, "transpose",
                         pt[0:128, (idx % 8) * 128:(idx % 8) * 128 + 128],
                         sb["wbig"][:, j * 256 + ct * 128:
                                    j * 256 + ct * 128 + 128],
                         sb["idb"][:, :],
                         inc="t" if idx == 15 else None)
            wT4 = p.cnt["t"]
            p.wait(V, "t", wT4)
            p.op(V, "tensor_copy", sb["wbig"][:, 2048:3072],
                 ps["ptA"][0:128, 0:1024])
            vW = p.op(V, "tensor_copy", sb["wbig"][:, 3072:4096],
                      ps["ptB"][0:128, 0:1024], inc="v")
            last_vW[0] = vW
            p.wait(Tn, "v", vW)
            for j, hd in enumerate(heads):
                rq = (hd % 2) * 64
                tfl = j // 2
                for ct in range(2):
                    idx = j * 2 + ct
                    p.op(Tn, "matmul",
                         scb[2][rq:rq + 64, tfl * 128:(tfl + 1) * 128],
                         sb["ckx"][:, ct * 1024 + hd * 64:
                                   ct * 1024 + hd * 64 + 64],
                         sb["wbig"][:, 2048 + idx * 128:2048 + idx * 128 + 128],
                         start=(ct == 0), stop=(ct == 1),
                         inc="t" if j == 7 and ct == 1 else None)
            wtO = p.cnt["t"]
            p.wait(V, "t", wtO)
            xkv = sb["scrD"][:, :].rearrange("p (tf c) -> p tf c", c=256)[
                :, 4 * hb:4 * hb + 4, tti * 128:(tti + 1) * 128]
            zv = sb["scrE"][:, :].rearrange("p (tf c) -> p tf c", c=256)[
                :, 4 * hb:4 * hb + 4, tti * 128:(tti + 1) * 128]
            azv = sb["azt"][:, :].rearrange("p (tf c) -> p tf c", c=256)[
                :, 4 * hb:4 * hb + 4, tti * 128:(tti + 1) * 128]
            p.op(V, "tensor_tensor", sb["fstg"][:, :], scb[0][:, 0:512],
                 xkv, ALU.mult, inc="v")
            p.wait(V, "v")
            p.op(V, "tensor_tensor", sb["scsB"][:, 0:512],
                 scb[2][:, 0:512], sb["fstg"][:, :], ALU.add, inc="v")
            p.wait(V, "v")
            p.op(V, "tensor_tensor", sb["fstg"][:, :], sb["scsB"][:, 0:512],
                 scb[1][:, 0:512], ALU.mult, inc="v")
            p.wait(V, "v")
            ev = p.op(V, "tensor_tensor", azv, sb["fstg"][:, :], zv,
                      ALU.mult, inc="v")
            unit_hist.append({"evac_v": ev, "sI": sI})
            qnames = (("pu2", "pu3", "ptAf", "ptBf"),
                      ("pm0", "pm1", "pu0", "pu1"))[nb % 2]
            for kk in qnames:
                evac[kk] = ev
            evac["ptA"] = ev
            evac["ptB"] = ev
        p.wait(Tn, "v", unit_hist[-1]["evac_v"])
        for o in range(8):
            tok = ring_load(dr["woutt"][o])
            pmname = "pu%d" % (2 + o % 2)
            pp = ps[pmname][:, 0:256]
            p.wait(Tn, "v", evac[pmname])
            wsl, slot = ring_use(tok)
            for k in range(8):
                p.op(Tn, "matmul", pp, wsl[:, k * 128:(k + 1) * 128],
                     sb["azt"][:, k * 256:(k + 1) * 256],
                     start=(k == 0), stop=(k == 7),
                     inc="t" if k == 7 else None)
            ring_done(slot)
            p.wait(V, "t")
            evac[pmname] = p.op(V, "tensor_tensor",
                                sb["sreg"][:, o * 2048 + s0:
                                           o * 2048 + s0 + TS],
                                pp, sb["xnhi"][:, o * 2048 + s0:
                                               o * 2048 + s0 + TS],
                                ALU.add, inc="v")

    p.barrier()
    if core_debug:
        tapout("sreg", (128, 16384), BF16)
        p.wait(Sy, "v")
        p.dma(Sy, taps["sreg"], sb["sreg"][:, :], "dO")
        for _e in ENGS:
            p.wait(_e, "dO", p.cnt["dO"])

    w1_tcnt = [0, 0]
    w2_tcnt = [0, 0]
    evac_s = {}
    QUADS = (("pm0", "pm1", "pu0", "pu1"), ("pu2", "pu3", "ptAf", "ptBf"))
    b1_scnt = [0, 0]
    stg_vcnt = [0, 0]
    db_tok = {0: db0_early}
    for e_i in range(NE):
        bp = e_i % 2
        db = db_tok[e_i]
        if e_i + 1 < NE:
            bpn = (e_i + 1) % 2
            p.wait(Gp, "s", b1_scnt[bpn])
            db_tok[e_i + 1] = p.dma(Gp, sb["b1s"][:, bpn * 32:(bpn + 1) * 32],
                                    dr["b1t"][e_i + 1], "db%d" % bpn)
        for q4 in range(4):
            pmname = "pm%d" % (q4 % 2)
            pp = ps[pmname][:, 0:512]
            p.wait(Tn, "v", evac[pmname])
            p.wait(Tn, "s", evac_s.get(pmname, 0))
            wt = p.op(Tn, "matmul", pp,
                      sb["sel8"][:, e_i * 128:(e_i + 1) * 128],
                      sb["oh"][0:8, q4 * 512:(q4 + 1) * 512],
                      start=True, stop=True, inc="t")
            p.wait(V, "t", wt)
            evac[pmname] = p.op(V, "tensor_copy",
                                sb["scrE"][:, q4 * 512:(q4 + 1) * 512], pp,
                                inc="v")
        mask_ready = p.cnt["v"]
        for q in range(4):
            p.wait(V, "t")
            h_ready_k = []
            for ft in range(8):
                wp = ft % 2
                p.wait(Gp, "t", w1_tcnt[wp])
                c1 = p.dma(Gp, sb["w1buf"][:, wp * 1024:(wp + 1) * 1024],
                           dr["w1t"][e_i, q, ft], "dW1%d" % wp)
                w1sl = sb["w1buf"][:, wp * 1024:(wp + 1) * 1024]
                banks = QUADS[ft % 2]
                for bn in banks:
                    p.wait(Tn, "v", evac[bn])
                    p.wait(Tn, "s", evac_s.get(bn, 0))
                p.wait(Tn, "dW1%d" % wp, c1)
                for k in range(8):
                    for s4 in range(4):
                        p.op(Tn, "matmul", ps[banks[s4]][:, 0:512],
                             w1sl[:, k * 128:(k + 1) * 128],
                             sb["xnhi"][:, k * 2048 + s4 * 512:
                                        k * 2048 + (s4 + 1) * 512],
                             start=(k == 0), stop=(k == 7),
                             inc="t" if (k == 7 and s4 == 3) else None)
                wt = p.cnt["t"]
                bcol = sb["b1s"][:, bp * 32 + q * 8 + ft:
                                 bp * 32 + q * 8 + ft + 1]
                p.wait(Sc, "t", wt)
                p.wait(Sc, "db%d" % bp, db)
                for s4 in range(4):
                    sp = s4 % 2
                    stgsl = sb["stg"][:, sp * 512:(sp + 1) * 512]
                    p.wait(Sc, "v", stg_vcnt[sp])
                    ws = p.op(Sc, "activation", stgsl, ps[banks[s4]][:, 0:512],
                              ACT.Gelu, bias=bcol, scale=1.0, inc="s")
                    evac_s[banks[s4]] = ws
                    evac[banks[s4]] = 0
                    p.wait(V, "s", ws)
                    p.wait(V, "v", mask_ready)
                    hsl = sb["big"][:, ft * 2048 + s4 * 512:
                                    ft * 2048 + (s4 + 1) * 512]
                    stg_vcnt[sp] = p.op(V, "tensor_tensor", hsl, stgsl,
                                        sb["scrE"][:, s4 * 512:(s4 + 1) * 512],
                                        ALU.mult, inc="v")
                w1_tcnt[wp] = p.cnt["t"]
                h_ready_k.append(p.cnt["v"])
            h_ready = p.cnt["v"]
            for d in range(8):
                wp = d % 2
                p.wait(Gp, "t", w2_tcnt[wp])
                c2 = p.dma(Gp, sb["w2buf"][:, wp * 1024:(wp + 1) * 1024],
                           dr["w2t"][e_i, d, q], "dW2%d" % wp)
                w2sl = sb["w2buf"][:, wp * 1024:(wp + 1) * 1024]
                banks = QUADS[d % 2]
                for bn in banks:
                    p.wait(Tn, "v", evac[bn])
                    p.wait(Tn, "s", evac_s.get(bn, 0))
                p.wait(Tn, "dW2%d" % wp, c2)
                if d > 0:
                    p.wait(Tn, "v", h_ready)
                last = (e_i == NE - 1 and q == 3)
                for k in range(8):
                    if d == 0:
                        p.wait(Tn, "v", h_ready_k[k])
                    for s4 in range(4):
                        p.op(Tn, "matmul", ps[banks[s4]][:, 0:512],
                             w2sl[:, k * 128:(k + 1) * 128],
                             sb["big"][:, k * 2048 + s4 * 512:
                                       k * 2048 + (s4 + 1) * 512],
                             start=(k == 0), stop=(k == 7 and not last),
                             inc="t" if (k == 7 and s4 == 3 and not last)
                             else None)
                if last:
                    for s4 in range(4):
                        p.op(Tn, "matmul", ps[banks[s4]][:, 0:512],
                             sb["b2hi"][0:8, d * 128:(d + 1) * 128],
                             sb["oh"][0:8, s4 * 512:(s4 + 1) * 512],
                             start=False, stop=False)
                        p.op(Tn, "matmul", ps[banks[s4]][:, 0:512],
                             sb["b2lo"][0:8, d * 128:(d + 1) * 128],
                             sb["oh"][0:8, s4 * 512:(s4 + 1) * 512],
                             start=False, stop=True,
                             inc="t" if s4 == 3 else None)
                wt = p.cnt["t"]
                p.wait(V, "t", wt)
                for s4 in range(4):
                    osl = sb["offn"][:, d * 2048 + s4 * 512:
                                     d * 2048 + (s4 + 1) * 512]
                    if e_i == 0 and q == 0:
                        ev = p.op(V, "tensor_copy", osl,
                                  ps[banks[s4]][:, 0:512], inc="v")
                    else:
                        ev = p.op(V, "tensor_tensor", osl,
                                  ps[banks[s4]][:, 0:512], osl, ALU.add,
                                  inc="v")
                    evac[banks[s4]] = ev
                w2_tcnt[wp] = p.cnt["t"]
        b1_scnt[bp] = p.cnt["s"]

    p.barrier()
    if core_debug:
        tapout("offn", (128, 16384), BF16)
        p.wait(Sy, "v")
        p.dma(Sy, taps["offn"], sb["offn"][:, :], "dO")
        for _e in ENGS:
            p.wait(_e, "dO", p.cnt["dO"])

    xrl_used = [0, 0]
    ot_stored = [0, 0]
    for tt in range(NTT):
        par = tt % 2
        xr = sb["xrl"][:, par * 1024:(par + 1) * 1024]
        ot = sb["otile"][:, par * 1024:(par + 1) * 1024]
        p.wait(Sy, "v", xrl_used[par])
        dx = p.dma(Sy, xr, dr["xs"][tt * 128:(tt + 1) * 128, :], "dX%d" % par)
        p.wait(Tn, "v", evac["pu3"])
        wtp = p.op(Tn, "transpose", ps["pu3"][0:1, 128:256],
                   sb["pmcols"][:, tt:tt + 1], sb["idf"][:, :], inc="t")
        p.wait(V, "t", wtp)
        p.wait(V, "t", rowscr_t[0])
        rsc = p.op(V, "tensor_copy", sb["rowscr"][0:1, 0:128],
                   ps["pu3"][0:1, 128:256], inc="v")
        p.wait(Tn, "v", rsc)
        wt = p.op(Tn, "matmul", ps["pu3"][:, 0:128], sb["onesf"][0:1, 0:128],
                  sb["rowscr"][0:1, 0:128], start=True, stop=True, inc="t")
        rowscr_t[0] = wt
        ptn = "ptA" if par == 0 else "ptB"
        pt = ps[ptn]
        p.wait(Tn, "v", evac[ptn])
        last_tp = 0
        for d in range(8):
            dsl = slice(d * 2048 + tt * 128, d * 2048 + tt * 128 + 128)
            p.wait(V, "t", wt)
            p.op(V, "tensor_tensor", sb["fstg"][:, 0:128],
                 sb["offn"][:, dsl], ps["pu3"][:, 0:128], ALU.mult, inc="v")
            p.wait(Sc, "v")
            ws = p.op(Sc, "activation", sb["stg"][:, 0:128],
                      sb["fstg"][:, 0:128], ACT.Silu, inc="s")
            p.wait(V, "s", ws)
            p.wait(V, "t", last_tp)
            rv = p.op(V, "tensor_tensor", sb["tmpl"][:, 0:128],
                      sb["stg"][:, 0:128], sb["sreg"][:, dsl], ALU.mult,
                      inc="v")
            p.wait(Tn, "v", rv)
            p.op(Tn, "transpose", pt[0:128, d * 128:(d + 1) * 128],
                 sb["tmpl"][:, 0:128], sb["idb"][:, :], inc="t")
            last_tp = p.cnt["t"]
        evac["pu3"] = p.cnt["v"]
        wt2 = p.cnt["t"]
        p.wait(V, "t", wt2)
        p.wait(V, "dX%d" % par, dx)
        addv = p.op(V, "tensor_tensor", ot, xr, pt[0:128, 0:1024], ALU.add,
                    inc="v")
        evac[ptn] = addv
        evac["pu3"] = addv
        xrl_used[par] = addv
        p.wait(Sy, "v", addv)
        ot_stored[par] = p.dma(Sy, y[tt * 128:(tt + 1) * 128, :], ot, "dO")

    p.wait(Sy, "dO", p.cnt["dO"])
    p.build()
    return nc, taps


def _prep_inputs(L0_attn_hidden_states, L0_attn_rms_weight, L0_attn_W_in,
                 L0_attn_gate_bias, L0_attn_W_out, L0_ffn_Wr, L0_ffn_W1,
                 L0_ffn_b1, L0_ffn_W2, L0_ffn_b2):
    x2 = np.ascontiguousarray(
        np.asarray(L0_attn_hidden_states, np.float32).reshape(16384, D))
    W_in = np.asarray(L0_attn_W_in, np.float32)
    W_out = np.asarray(L0_attn_W_out, np.float32)
    Wr = np.asarray(L0_ffn_Wr, np.float32)
    W1 = np.asarray(L0_ffn_W1, np.float32)
    W2 = np.asarray(L0_ffn_W2, np.float32)
    b1 = np.asarray(L0_ffn_b1, np.float32)
    b2 = np.asarray(L0_ffn_b2, np.float32)
    rms = np.asarray(L0_attn_rms_weight, np.float32)
    gb = np.asarray(L0_attn_gate_bias, np.float32)

    com = {}
    com["xsum"] = np.ascontiguousarray(x2[0::CHUNK])
    com["wint"] = np.ascontiguousarray(
        W_in.reshape(8, 128, 26, 128).transpose(2, 1, 0, 3)
        .reshape(26, 128, 1024))
    com["woutt"] = np.ascontiguousarray(
        W_out.reshape(8, 128, 8, 128).transpose(2, 1, 0, 3)
        .reshape(8, 128, 1024))
    com["wrt"] = np.ascontiguousarray(
        Wr.reshape(8, 128, 8).transpose(1, 0, 2).reshape(128, 64))
    com["w1t"] = np.ascontiguousarray(
        W1.reshape(NE, 8, 128, 32, 128).transpose(0, 3, 2, 1, 4)
        .reshape(NE, 4, 8, 128, 1024))
    com["w2t"] = np.ascontiguousarray(
        W2.reshape(NE, 4, 8, 128, 8, 128).transpose(0, 4, 1, 3, 2, 5)
        .reshape(NE, 8, 4, 128, 1024))
    com["b1t"] = np.ascontiguousarray(
        b1.reshape(NE, 32, 128).transpose(0, 2, 1))
    com["b2m"] = b2
    com["rmswt"] = np.ascontiguousarray(rms.reshape(8, 128).T)
    com["gbt"] = np.ascontiguousarray(gb.reshape(8, 128).T)
    com["ubh"] = np.zeros((2, 128), BF)
    com["ubh"][0, 0:64] = 1
    com["ubh"][1, 64:128] = 1
    com["idfh"] = np.eye(128, dtype=np.float32)
    com["idbh"] = np.eye(128, dtype=np.float32).astype(BF)
    com["onesfh"] = np.ones((1, 256), np.float32)
    com["onesbh"] = np.ones((1, 128), BF)
    selb = np.zeros((128, 2), BF)
    selb[0:64, 0] = 1
    selb[64:128, 1] = 1
    com["selbh"] = selb
    sel8 = np.zeros((8, 1024), BF)
    for e in range(8):
        sel8[e, e * 128:(e + 1) * 128] = 1
    com["sel8h"] = sel8
    com["iota8h"] = np.tile(np.arange(8, dtype=np.float32), (128, 1))
    bo8 = np.zeros((8, 2048), np.float32)
    for h in range(8):
        bo8[h, h * 256:(h + 1) * 256] = 1.0
    com["bo8h"] = bo8
    selh = np.zeros((8, 512), np.float32)
    for tfl in range(4):
        for pp_ in range(128):
            selh[2 * tfl + pp_ // 64, tfl * 128 + pp_] = 1.0
    com["selhh"] = selh

    in_maps = []
    for c in range(NCORES):
        m = dict(com)
        m["xs"] = np.ascontiguousarray(x2[c * T:(c + 1) * T])
        msb = np.zeros((2, 16 * 256), np.float32)
        for tt in range(16):
            c0 = (c * T + tt * 128) // CHUNK
            cc = np.arange(NCH)
            msb[0, tt * 256:(tt + 1) * 256] = np.where(cc >= c0, NEG, 0.0)
            msb[1, tt * 256:(tt + 1) * 256] = np.where(cc >= c0 + 1, NEG, 0.0)
        m["msbh"] = msb.astype(BF)
        in_maps.append(m)
    return in_maps


_NC_CACHE = {}


def kernel(**inputs):
    in_maps = _prep_inputs(**inputs)
    if "nc" not in _NC_CACHE:
        _NC_CACHE["nc"] = build(core_debug=False)[0]
    nc = _NC_CACHE["nc"]
    res = run_bass_kernel_spmd(nc, in_maps, core_ids=list(range(NCORES)))
    out = np.concatenate([np.asarray(r["y"]) for r in res.results], axis=0)
    return out.reshape(1, 16384, D)


# version 15
# speedup vs baseline: 1.2754x; 1.0091x over previous
import numpy as np
import ml_dtypes
import concourse.bass as bass
import concourse.mybir as mybir
from concourse.bass_utils import run_bass_kernel_spmd
from contextlib import ExitStack

F32 = mybir.dt.float32
BF16 = mybir.dt.bfloat16
BF = ml_dtypes.bfloat16
AX = mybir.AxisListType.X
ALU = mybir.AluOpType
ACT = mybir.ActivationFunctionType

NCORES = 8
T = 2048
D = 1024
DH = 64
NHEAD = 16
CHUNK = 64
NCH = 256
FH = 4096
NE = 8
NEG = np.float32(-1e30)
TS = 256
NSUB = T // TS
NTT = T // 128

ENGS = ("sync", "gpsimd", "vector", "scalar", "tensor")
V, Sc, Tn, Sy, Gp = "vector", "scalar", "tensor", "sync", "gpsimd"

BASE = 20480
SB = {}
_off = [BASE]
def _reg(name, nbytes, align=64):
    _off[0] = (_off[0] + align - 1) // align * align
    SB[name] = _off[0]
    _off[0] += nbytes

_reg("xnhi", 32768)
_reg("sreg", 32768)
_reg("big", 32768)
_reg("wring", 4096)
_reg("w1buf", 4096)
_reg("w2buf", 4096)
_reg("scrA", 8192)
_reg("scrB", 8192)
_reg("scrC", 8192)
_reg("scrD", 8192)
_reg("scrE", 4096)
_reg("msb", 8192)
_reg("ckk", 4096)
_reg("ckx", 4096)
_reg("oh", 4096)
_reg("rowscr", 1536)
_reg("pmcols", 64)
_reg("sel8", 2048)
_reg("b2hi", 2048)
_reg("b2lo", 2048)
_reg("idf", 512)
_reg("idb", 256)
_reg("onesf", 1024)
_reg("onesb", 256)
_reg("selb", 32)
_reg("ub", 256)
_reg("wrf", 256)
_reg("wrhib", 128)
_reg("wrlob", 128)
_reg("gbs", 32)
_reg("wrmsfm", 32)
_reg("b1s", 256)
_reg("qf", 1024)
_reg("qb", 512)
_reg("kf", 1024)
_reg("ilse", 128)
_reg("nilse", 128)
_reg("stat", 256)
_reg("tmph", 1024)
_reg("tmpl", 512)
_reg("stg", 2048)
_reg("fstg", 2048)
_reg("istg", 512)
_reg("istg2", 512)
_reg("r8", 512)
_reg("lgt", 128)
_reg("ohtm", 64)
_reg("mx8", 256)
_reg("repsb", 1024)
_reg("hsub", 4096)
_reg("rrmscols", 64)
_reg("warmt", 64)
_reg("wbias", 64)
_reg("iota8", 32)
_reg("idxu", 64)
_reg("bo8", 8192)
_reg("selh", 2048)
_reg("r8b", 1024)
_reg("statln", 256)
_reg("mrepsb", 512)
_reg("rrepsb", 512)
_reg("rrrepsb", 512)
assert _off[0] <= 229376, _off[0]


class Prog:

    def __init__(self, nc):
        self.nc = nc
        self.q = {e: [] for e in ENGS}
        self.sem = {}
        self.cnt = {}
        self.ctx = ExitStack()

    def new_sem(self, name):
        self.sem[name] = self.ctx.enter_context(self.nc.semaphore(name))
        self.cnt[name] = 0

    def op(self, eng, meth, *args, inc=None, n=1, **kw):
        if inc is not None:
            self.cnt[inc] += n
        self.q[eng].append((meth, args, kw, inc, n))
        return self.cnt[inc] if inc else None

    def dma(self, eng, out_ap, in_ap, inc):
        return self.op(eng, "dma_start", inc=inc, n=16, out=out_ap, in_=in_ap)

    def wait(self, eng, sem, val=None):
        v = self.cnt[sem] if val is None else val
        if v > 0:
            self.q[eng].append(("__wait__", (sem, v), {}, None, 0))

    def barrier(self, engs=ENGS, sems=("v", "s", "t")):
        vals = [(sm, self.cnt[sm]) for sm in sems]
        for e in engs:
            for sm, v in vals:
                self.wait(e, sm, v)

    def build(self):
        block = self.ctx.enter_context(self.nc.Block())
        for eng in ENGS:
            items = self.q[eng]
            def body(e, items=items):
                for meth, args, kw, inc, n in items:
                    if meth == "__wait__":
                        e.wait_ge(self.sem[args[0]], args[1])
                    else:
                        ins = getattr(e, meth)(*args, **kw)
                        if inc is not None:
                            ins.then_inc(self.sem[inc], n)
            getattr(block, eng)(body)
        self.ctx.close()


def build(core_debug=False):
    nc = bass.Bass("TRN2", target_bir_lowering=False,
                   detect_race_conditions=False)

    dr = {}
    def din(name, shape, dt=F32):
        dr[name] = nc.dram_tensor(name, list(shape), dt,
                                  kind="ExternalInput").ap()
    din("xs", (T, D))
    din("xsum", (NCH, D))
    din("wint", (26, 128, 1024))
    din("woutt", (8, 128, 1024))
    din("wrt", (128, 64))
    din("w1t", (NE, 4, 8, 128, 1024))
    din("w2t", (NE, 8, 4, 128, 1024))
    din("b1t", (NE, 128, 32))
    din("b2m", (NE, D))
    din("rmswt", (128, 8))
    din("gbt", (128, 8))
    din("msbh", (2, 16 * 256), BF16)
    din("ubh", (2, 128), BF16)
    din("idfh", (128, 128))
    din("idbh", (128, 128), BF16)
    din("onesfh", (1, 256))
    din("onesbh", (1, 128), BF16)
    din("selbh", (128, 2), BF16)
    din("sel8h", (8, 1024), BF16)
    din("iota8h", (128, 8))
    din("bo8h", (8, 2048))
    din("selhh", (8, 512))
    y = nc.dram_tensor("y", [T, D], F32, kind="ExternalOutput").ap()

    taps = {}
    def tapout(name, shape, dt=F32):
        taps[name] = nc.dram_tensor("tap_" + name, list(shape), dt,
                                    kind="ExternalOutput").ap()

    sb = {}
    def S_(name, shape, dt, key=None, extra=0):
        sb[name] = nc.alloc_sbuf_tensor_at(
            name, list(shape), dt, offset=SB[key or name] + extra).ap()
    S_("xnhi", (128, 16384), BF16)
    S_("xrl", (128, 2048), F32, "xnhi")
    S_("otile", (128, 2048), F32, "xnhi", 8192)
    S_("sreg", (128, 16384), BF16)
    S_("big", (128, 16384), BF16)
    S_("wring", (128, 2048), BF16)
    S_("w1buf", (128, 2048), BF16)
    S_("w2buf", (128, 2048), BF16)
    S_("scrA", (128, 2048), F32)
    S_("xa", (128, 2048), F32, "scrA")
    S_("azt", (128, 2048), BF16, "scrA")
    S_("scrB", (128, 2048), F32)
    S_("scrC", (128, 2048), F32)
    S_("scrD", (128, 2048), F32)
    S_("cxbf", (128, 2048), BF16, "scrD")
    S_("offn", (128, 16384), BF16, "scrA")
    S_("scrE", (128, 2048), BF16)
    S_("msb", (2, 4096), BF16)
    S_("ckk", (128, 2048), BF16)
    S_("ckx", (128, 2048), BF16)
    S_("rowscr", (1, 384), F32)
    S_("pmcols", (128, 16), F32)
    S_("sel8", (8, 1024), BF16)
    S_("oh", (8, 2048), BF16)
    S_("b2hi", (8, 1024), BF16)
    S_("b2lo", (8, 1024), BF16)
    S_("b2f32", (8, 1024), F32, "scrB")
    S_("idf", (128, 128), F32)
    S_("idb", (128, 128), BF16)
    S_("onesf", (1, 256), F32)
    S_("onesb", (1, 128), BF16)
    S_("selb", (128, 2), BF16)
    S_("ub", (2, 128), BF16)
    S_("wrf", (128, 64), F32)
    S_("wrhib", (128, 64), BF16)
    S_("wrlob", (128, 64), BF16)
    S_("gbs", (128, 8), F32)
    S_("wrmsfm", (128, 8), F32)
    S_("b1s", (128, 64), F32)
    S_("qf", (128, 256), F32)
    S_("qb", (128, 256), BF16)
    S_("kf", (128, 256), F32)
    S_("ilse", (128, 32), F32)
    S_("nilse", (128, 32), F32)
    S_("stat", (128, 64), F32)
    S_("tmph", (128, 512), BF16)
    S_("tmpl", (128, 256), BF16)
    S_("stg", (128, 1024), BF16)
    S_("fstg", (128, 512), F32)
    S_("istg", (128, 128), F32)
    S_("istg2", (128, 128), F32)
    S_("xloT", (128, 128), BF16, "istg")
    S_("r8", (8, 128), F32)
    S_("lgt", (128, 16), F32)
    S_("ohtm", (128, 16), BF16)
    S_("mx8", (128, 64), F32)
    S_("repsb", (128, 256), F32)
    S_("hsub", (128, 2048), BF16)
    S_("rrmscols", (128, 16), F32)
    S_("warmt", (128, 16), F32)
    S_("wbias", (128, 1), F32)
    S_("dbgln", (128, 512), F32, "scrD")
    S_("dbglg", (128, 128), F32, "scrD", 2048)
    S_("iota8", (128, 8), F32)
    S_("idxu", (128, 16), mybir.dt.uint32)
    S_("bo8", (8, 2048), F32)
    S_("selh", (8, 512), F32)
    S_("r8b", (8, 256), F32)
    S_("scsB", (128, 2048), F32, "scrB")
    S_("wbig", (128, 4096), BF16, "scrC")
    S_("statln", (128, 64), F32)
    S_("mrepsb", (128, 128), F32)
    S_("rrepsb", (128, 128), F32)
    S_("rrrepsb", (128, 128), F32)

    p = Prog(nc)
    ps = {}
    for nm, shape, dt in [
        ("pm0", (128, 512), F32), ("pm1", (128, 512), F32),
        ("pu0", (128, 512), F32), ("pu1", (128, 512), F32),
        ("ptA", (128, 1024), BF16), ("ptB", (128, 1024), BF16),
        ("pu2", (128, 512), F32), ("pu3", (128, 512), F32),
    ]:
        ps[nm] = p.ctx.enter_context(nc.psum_tensor(nm, list(shape), dt)).ap()

    for s in ("dL", "dX0", "dX1", "dR0", "dR1", "dW10", "dW11", "dW20",
              "dW21", "db0", "db1", "dO", "v", "s", "t"):
        p.new_sem(s)
    ps["ptAf"] = ps["ptA"].bitcast(F32)
    ps["ptBf"] = ps["ptB"].bitcast(F32)

    p.dma(Sy, sb["msb"][:, :], dr["msbh"], "dL")
    p.dma(Sy, sb["ub"][:, :], dr["ubh"], "dL")
    p.dma(Sy, sb["idf"][:, :], dr["idfh"], "dL")
    p.dma(Sy, sb["idb"][:, :], dr["idbh"], "dL")
    p.dma(Sy, sb["onesf"][:, :], dr["onesfh"], "dL")
    p.dma(Sy, sb["onesb"][:, :], dr["onesbh"], "dL")
    p.dma(Sy, sb["selb"][:, :], dr["selbh"], "dL")
    p.dma(Sy, sb["sel8"][:, :], dr["sel8h"], "dL")
    p.dma(Sy, sb["iota8"][:, :], dr["iota8h"], "dL")
    p.dma(Sy, sb["bo8"][:, :], dr["bo8h"], "dL")
    p.dma(Sy, sb["selh"][:, :], dr["selhh"], "dL")
    p.dma(Sy, sb["wrf"][:, :], dr["wrt"], "dL")
    p.dma(Sy, sb["gbs"][:, :], dr["gbt"], "dL")
    p.dma(Sy, sb["wrmsfm"][:, :], dr["rmswt"], "dL")
    p.dma(Sy, sb["b2f32"][:, :], dr["b2m"], "dL")
    cload = p.cnt["dL"]
    db0_early = p.dma(Gp, sb["b1s"][:, 0:32], dr["b1t"][0], "db0")

    p.wait(V, "dL", cload)
    p.op(V, "memset", sb["wbias"][:, :], 0.25)
    p.op(V, "memset", sb["statln"][:, :], 0.0)
    p.op(V, "memset", sb["mx8"][:, :], 0.0)
    p.op(V, "reduce_max", sb["warmt"][:, 0:1], sb["wbias"][:, :], axis=AX,
         inc="v")
    p.wait(V, "v")
    p.op(V, "tensor_scalar", sb["warmt"][:, 1:2], sb["wbias"][:, :],
         sb["wbias"][:, :], 2.0, ALU.subtract, ALU.mult)
    p.op(V, "scalar_tensor_tensor", sb["warmt"][:, 1:2], sb["wbias"][:, :],
         sb["wbias"][:, :], sb["wbias"][:, :], ALU.mult, ALU.add)
    p.op(V, "tensor_scalar", sb["warmt"][:, 1:2], sb["wbias"][:, :],
         sb["wbias"][:, :], None, ALU.is_equal, inc="v")
    p.wait(Sc, "v")
    for fn in (ACT.Exp, ACT.Gelu, ACT.Sigmoid, ACT.Silu, ACT.Square,
               ACT.Sqrt, ACT.Identity):
        p.op(Sc, "activation", sb["warmt"][:, 2:3], sb["wbias"][:, :], fn,
             bias=sb["wbias"][:, :], scale=1.0, accum_out=sb["warmt"][:, 3:4])
    p.op(Sc, "nop", inc="s")
    p.op(V, "tensor_copy", sb["wrhib"][:, :], sb["wrf"][:, :], inc="v")
    p.wait(V, "v")
    p.op(V, "tensor_tensor", sb["wrlob"][:, :], sb["wrf"][:, :],
         sb["wrhib"][:, :], ALU.subtract)
    p.op(V, "tensor_copy", sb["b2hi"][:, :], sb["b2f32"][:, :], inc="v")
    p.wait(V, "v")
    p.op(V, "tensor_tensor", sb["b2lo"][:, :], sb["b2f32"][:, :],
         sb["b2hi"][:, :], ALU.subtract, inc="v")

    evac = {k: 0 for k in ("pm0", "pm1", "pu0", "pu1", "pu2", "pu3",
                           "ptA", "ptB", "ptAf", "ptBf")}
    xa_done = [[], []]
    rowscr_t = [0]

    def ka(sem="v", at=None):
        vv = p.cnt[sem] if at is None else at
        if vv > 0:
            p.wait(Tn, sem, vv)
        p.op(Tn, "ldweights", sb["idb"][:, :])

    def ka_burst(n=30):
        for i in range(n):
            p.op(Tn, "ldweights", sb["idb"][:, :])
            p.op(Tn, "ldweights", sb["selb"][:, :])

    def ln_stats(src, dump, so):
        sl = lambda i: sb["statln"][:, so + i:so + i + 1]

        def fv(*args, **kw):
            c = p.op(V, *args, inc="v", **kw)
            p.wait(V, "v", c)
            return c

        fv("reduce_sum", sl(0), src, axis=AX)
        p.wait(Sc, "v")
        p.op(Sc, "activation", dump, src, ACT.Square, bias=0.0, scale=1.0,
             inc="s")
        fv("tensor_scalar", sl(2), sl(0), 1.0 / D, None, ALU.mult)
        fv("tensor_tensor", sl(3), sl(2), sl(2), ALU.mult)
        p.wait(V, "s")
        fv("reduce_sum", sl(1), dump, axis=AX)
        fv("tensor_scalar", sl(4), sl(1), 1.0 / D, None, ALU.mult)
        fv("tensor_tensor", sl(5), sl(4), sl(3), ALU.subtract)

        def newton(tin, eps, b):
            fv("tensor_scalar", sl(b), tin, 1.0, eps, ALU.mult, ALU.add)
            p.wait(Sc, "v")
            p.op(Sc, "activation", sl(b + 1), sl(b), ACT.Sqrt, inc="s")
            p.wait(V, "s")
            fv("reciprocal", sl(b + 2), sl(b + 1))
            fv("tensor_tensor", sl(b + 3), sl(b + 2), sl(b + 2), ALU.mult)
            fv("tensor_tensor", sl(b + 4), sl(b), sl(b + 3), ALU.mult)
            fv("tensor_scalar", sl(b + 5), sl(b + 4), -0.5, 1.5, ALU.mult,
               ALU.add)
            fv("tensor_tensor", sl(b + 6), sl(b + 2), sl(b + 5), ALU.mult)
            return sl(b + 6)

        rstd = newton(sl(5), 1e-5, 8)
        fv("tensor_scalar", sl(16), sl(2), -1.0, None, ALU.mult)
        fv("tensor_tensor", sl(17), rstd, rstd, ALU.mult)
        fv("tensor_tensor", sl(18), sl(5), sl(17), ALU.mult)
        rrms = newton(sl(18), 1e-6, 20)
        return sl(16), rstd, rrms

    rowps = lambda: ps["pu2"][0:1, 128:256]
    repps = lambda: ps["pu2"][:, 0:128]

    def make_rep(col, rep_sb):
        p.wait(Tn, "v")
        p.wait(Tn, "v", evac["pu2"])
        wt = p.op(Tn, "transpose", rowps(), col, sb["idf"][:, :], inc="t")
        p.wait(V, "t", wt)
        p.wait(V, "t", rowscr_t[0])
        rs = p.op(V, "tensor_copy", sb["rowscr"][0:1, 0:128], rowps(),
                  inc="v")
        p.wait(Tn, "v", rs)
        wb = p.op(Tn, "matmul", repps(), sb["onesf"][0:1, 0:128],
                  sb["rowscr"][0:1, 0:128], start=True, stop=True, inc="t")
        rowscr_t[0] = wb
        p.wait(V, "t", wb)
        ev = p.op(V, "tensor_copy", rep_sb, repps(), inc="v")
        evac["pu2"] = ev
        return ev

    hs_guard = [0, 0]
    for tt in range(NTT):
        par = tt % 2
        so = 32 * par
        xa_sl = sb["xa"][:, par * 1024:(par + 1) * 1024]
        for semn, vv in xa_done[par]:
            p.wait(Sy, semn, vv)
        dx = p.dma(Sy, xa_sl, dr["xs"][tt * 128:(tt + 1) * 128, :],
                   "dX%d" % par)
        p.wait(V, "dX%d" % par, dx)
        p.wait(Sc, "dX%d" % par, dx)
        dump = sb["scrB"][:, par * 1024:(par + 1) * 1024]
        negm, rstd, rrms = ln_stats(xa_sl, dump, so)
        if core_debug:
            p.op(V, "tensor_copy", sb["dbgln"][:, tt * 32:(tt + 1) * 32],
                 sb["statln"][:, so:so + 32])
        p.op(V, "tensor_copy", sb["rrmscols"][:, tt:tt + 1], rrms, inc="v")
        make_rep(negm, sb["mrepsb"][:, :])
        make_rep(rstd, sb["rrepsb"][:, :])

        for k in range(8):
            pmname = "pm%d" % (k % 2)
            fx = ps[pmname][:, 0:128]
            p.wait(Tn, "v", evac[pmname])
            wt = p.op(Tn, "transpose", fx, xa_sl[:, k * 128:(k + 1) * 128],
                      sb["idf"][:, :], inc="t")
            fsl = sb["fstg"][:, (k % 2) * 128:(k % 2) * 128 + 128]
            p.wait(V, "t", wt)
            p.wait(V, "s", hs_guard[k % 2])
            p.op(V, "tensor_tensor", fsl, fx, sb["mrepsb"][:, :], ALU.add,
                 inc="v")
            evac[pmname] = p.cnt["v"]
            p.wait(V, "v")
            p.op(V, "tensor_tensor", sb["istg2"][:, :], fsl,
                 sb["rrepsb"][:, :], ALU.mult, inc="v")
            p.wait(V, "v")
            hi_sl = sb["xnhi"][:, k * 2048 + tt * 128:k * 2048 + tt * 128 + 128]
            p.op(V, "tensor_copy", hi_sl, sb["istg2"][:, :], inc="v")
            p.wait(V, "v")
            lo_ready = p.op(V, "tensor_tensor", sb["xloT"][:, :],
                            sb["istg2"][:, :], hi_sl, ALU.subtract, inc="v")
            p.wait(Tn, "v", lo_ready)
            if k == 0:
                p.wait(Tn, "v", evac["pu1"])
            p.op(Tn, "matmul", ps["pu1"][0:8, 0:128],
                 sb["wrhib"][:, k * 8:(k + 1) * 8], hi_sl,
                 start=(k == 0), stop=False)
            p.op(Tn, "matmul", ps["pu1"][0:8, 0:128],
                 sb["wrhib"][:, k * 8:(k + 1) * 8], sb["xloT"][:, :],
                 start=False, stop=False)
            rmm = p.op(Tn, "matmul", ps["pu1"][0:8, 0:128],
                       sb["wrlob"][:, k * 8:(k + 1) * 8], hi_sl,
                       start=False, stop=(k == 7), inc="t")
            p.wait(V, "t", rmm)
        xa_done[par] = [("v", p.cnt["v"]), ("s", p.cnt["s"]),
                        ("t", p.cnt["t"])]

        p.wait(V, "t")
        ev = p.op(V, "tensor_copy", sb["r8"][:, :], ps["pu1"][0:8, 0:128],
                  inc="v")
        evac["pu1"] = ev
        p.wait(Tn, "v", ev)
        p.wait(Tn, "v", evac["pu0"])
        wt = p.op(Tn, "transpose", ps["pu0"][0:128, 0:8], sb["r8"][:, :],
                  sb["idf"][0:8, 0:8], inc="t")
        lg = sb["lgt"][:, par * 8:(par + 1) * 8]
        ohc = sb["ohtm"][:, par * 8:(par + 1) * 8]
        m8 = lambda i: sb["mx8"][:, 32 * par + i:32 * par + i + 1]
        vmax8 = sb["mx8"][:, 32 * par + 8:32 * par + 16]
        vidx8 = sb["mx8"][:, 32 * par + 16:32 * par + 24]
        p.wait(V, "t", wt)
        evac["pu0"] = p.op(V, "tensor_copy", lg, ps["pu0"][:, 0:8], inc="v")
        p.wait(V, "v")
        if core_debug:
            p.op(V, "tensor_copy", sb["dbglg"][:, tt * 8:(tt + 1) * 8], lg)
        p.op(V, "max", vmax8, lg, inc="v")
        p.wait(V, "v")
        idxu = sb["idxu"][:, par * 8:(par + 1) * 8]
        p.op(V, "max_index", idxu, vmax8, lg, inc="v")
        p.wait(V, "v")
        mxi = p.op(V, "tensor_copy", vidx8, idxu, inc="v")
        p.wait(V, "v", mxi)
        p.op(V, "tensor_copy", m8(0), vmax8[:, 0:1], inc="v")
        p.wait(Tn, "v", mxi)
        p.wait(Tn, "v", evac["pu3"])
        w1 = p.op(Tn, "transpose", ps["pu3"][0:1, 0:128],
                  vidx8[:, 0:1], sb["idf"][:, :], inc="t")
        p.wait(V, "t", w1)
        p.wait(V, "t", rowscr_t[0])
        rs = p.op(V, "tensor_copy", sb["rowscr"][0:1, 0:128],
                  ps["pu3"][0:1, 0:128], inc="v")
        p.wait(Tn, "v", rs)
        w2 = p.op(Tn, "matmul", ps["pu3"][:, 128:136],
                  sb["rowscr"][0:1, 0:128], sb["onesf"][0:1, 0:8],
                  start=True, stop=True, inc="t")
        rowscr_t[0] = w2
        p.wait(V, "t", w2)
        oh_v = p.op(V, "tensor_tensor", ohc, sb["iota8"][:, :],
                    ps["pu3"][:, 128:136], ALU.is_equal, inc="v")
        evac["pu3"] = oh_v
        p.wait(Sc, "v")
        p.op(Sc, "activation", sb["warmt"][:, 8:16], lg, ACT.Exp, bias=0.0,
             scale=1.0, inc="s")
        ws2 = p.op(Sc, "activation", m8(4), m8(0), ACT.Exp, bias=0.0,
                   scale=1.0, inc="s")
        p.wait(V, "s", ws2)
        p.op(V, "reduce_sum", m8(2), sb["warmt"][:, 8:16], axis=AX, inc="v")
        p.wait(V, "v")
        p.op(V, "reciprocal", m8(5), m8(2), inc="v")
        p.wait(V, "v")
        pm_ready = p.op(V, "tensor_tensor", sb["pmcols"][:, tt:tt + 1],
                        m8(4), m8(5), ALU.mult, inc="v")
        p.wait(Tn, "v", oh_v)
        p.wait(Tn, "v", evac["ptA"])
        wt2 = p.op(Tn, "transpose", ps["ptA"][0:8, 0:128], ohc,
                   sb["idb"][:, :], inc="t")
        p.wait(V, "t", wt2)
        evac["ptA"] = p.op(V, "tensor_copy",
                           sb["oh"][0:8, tt * 128:(tt + 1) * 128],
                           ps["ptA"][0:8, 0:128], inc="v")

    p.barrier()
    if core_debug:
        tapout("xnhi", (128, 16384), BF16)
        tapout("pmcols", (128, 16))
        tapout("oh", (8, 2048), BF16)
        tapout("dbgln", (128, 512))
        tapout("dbglg", (128, 128))
        p.wait(Sy, "v")
        p.dma(Sy, taps["dbgln"], sb["dbgln"][:, :], "dO")
        p.dma(Sy, taps["dbglg"], sb["dbglg"][:, :], "dO")
        p.dma(Sy, taps["xnhi"], sb["xnhi"][:, :], "dO")
        p.dma(Sy, taps["pmcols"], sb["pmcols"][:, :], "dO")
        p.dma(Sy, taps["oh"], sb["oh"][:, :], "dO")
        for _e in ENGS:
            p.wait(_e, "dO", p.cnt["dO"])

    ring_state = {"n": 0, "tcnt": [0, 0]}

    def ring_load(src_ap):
        j = ring_state["n"]
        slot = j % 2
        p.wait(Gp, "t", ring_state["tcnt"][slot])
        c = p.dma(Gp, sb["wring"][:, slot * 1024:(slot + 1) * 1024], src_ap,
                  "dR%d" % slot)
        ring_state["n"] = j + 1
        return ("dR%d" % slot, c, slot)

    def ring_use(tok):
        semn, c, slot = tok
        p.wait(Tn, semn, c)
        return sb["wring"][:, slot * 1024:(slot + 1) * 1024], slot

    def ring_done(slot):
        ring_state["tcnt"][slot] = p.cnt["t"]

    for st2 in range(2):
        par = st2
        so = 32 * par
        xa_sl = sb["xa"][:, par * 1024:(par + 1) * 1024]
        for semn, vv in xa_done[par]:
            p.wait(Sy, semn, vv)
        dx = p.dma(Sy, xa_sl, dr["xsum"][st2 * 128:(st2 + 1) * 128, :],
                   "dX%d" % par)
        p.wait(V, "dX%d" % par, dx)
        p.wait(Sc, "dX%d" % par, dx)
        dump = sb["scrB"][:, par * 1024:(par + 1) * 1024]
        negm, rstd, rrms = ln_stats(xa_sl, dump, so)
        make_rep(negm, sb["mrepsb"][:, :])
        make_rep(rstd, sb["rrepsb"][:, :])
        make_rep(rrms, sb["rrrepsb"][:, :])
        p.wait(V, "v")
        for k in range(8):
            pmname = "pm%d" % (k % 2)
            fx = ps[pmname][:, 0:128]
            p.wait(Tn, "v", evac[pmname])
            wt = p.op(Tn, "transpose", fx, xa_sl[:, k * 128:(k + 1) * 128],
                      sb["idf"][:, :], inc="t")
            fsl = sb["fstg"][:, (k % 2) * 128:(k % 2) * 128 + 128]
            p.wait(V, "t", wt)
            p.wait(V, "s", hs_guard[k % 2])
            p.op(V, "tensor_tensor", fsl, fx, sb["mrepsb"][:, :], ALU.add,
                 inc="v")
            evac[pmname] = p.cnt["v"]
            p.op(V, "tensor_tensor", sb["istg2"][:, :], fsl,
                 sb["rrepsb"][:, :], ALU.mult)
            tv = p.op(V, "tensor_tensor", fsl, sb["istg2"][:, :],
                      sb["rrrepsb"][:, :], ALU.mult, inc="v")
            p.wait(Sc, "v", tv)
            hs_sl = sb["scrE"][:, k * 256 + st2 * 128:k * 256 + st2 * 128 + 128]
            hs_guard[k % 2] = p.op(Sc, "activation", hs_sl, fsl, ACT.Identity,
                                   bias=0.0, scale=sb["wrmsfm"][:, k:k + 1],
                                   inc="s")
        xa_done[par] = [("v", p.cnt["v"]), ("s", p.cnt["s"]),
                        ("t", p.cnt["t"])]

    hs_ready = p.cnt["v"]
    gsum_s = {}
    kf_ready = 0
    for oi, o in enumerate(list(range(16, 24)) + list(range(8, 16)) + [25]):
        tok = ring_load(dr["wint"][o])
        pmname = "pm%d" % (oi % 2)
        pp = ps[pmname][:, 0:256]
        p.wait(Tn, "v", evac[pmname])
        p.wait(Tn, "v", hs_ready)
        wsl, slot = ring_use(tok)
        for k in range(8):
            p.op(Tn, "matmul", pp, wsl[:, k * 128:(k + 1) * 128],
                 sb["scrE"][:, k * 256:(k + 1) * 256],
                 start=(k == 0), stop=(k == 7), inc="t" if k == 7 else None)
        ring_done(slot)
        wt = p.cnt["t"]
        if 16 <= o < 24:
            tf = o - 16
            p.wait(Sc, "t", wt)
            ws = p.op(Sc, "activation", sb["scrC"][:, tf * 256:(tf + 1) * 256],
                      pp, ACT.Sigmoid, bias=sb["gbs"][:, tf:tf + 1], scale=1.0,
                      inc="s")
            gsum_s[tf] = ws
            p.wait(Tn, "s", ws)
            evac[pmname] = 0
        elif 8 <= o < 16:
            tf = o - 8
            p.wait(V, "t", wt)
            p.wait(V, "s", gsum_s[tf])
            p.op(V, "tensor_tensor", sb["fstg"][:, 0:256], pp,
                 sb["scrC"][:, tf * 256:(tf + 1) * 256], ALU.mult, inc="v")
            p.wait(V, "v")
            evac[pmname] = p.op(V, "tensor_tensor",
                                sb["cxbf"][:, tf * 256:(tf + 1) * 256], pp,
                                sb["fstg"][:, 0:256], ALU.subtract, inc="v")
        else:
            p.wait(V, "t", wt)
            kf_ready = p.op(V, "tensor_copy", sb["kf"][:, :], pp, inc="v")
            evac[pmname] = kf_ready
    for tf in range(8):
        p.wait(V, "v", kf_ready)
        p.wait(V, "s", gsum_s[tf])
        p.op(V, "tensor_tensor", sb["fstg"][:, 0:256], sb["kf"][:, :],
             sb["scrC"][:, tf * 256:(tf + 1) * 256], ALU.mult, inc="v")
        p.wait(V, "v")
        p.op(V, "tensor_tensor", sb["ckk"][:, tf * 256:(tf + 1) * 256],
             sb["kf"][:, :], sb["fstg"][:, 0:256], ALU.subtract, inc="v")
    p.wait(Tn, "v")
    for tf in range(8):
        for ct in range(2):
            ptn = "ptA" if (tf * 2 + ct) % 2 == 0 else "ptB"
            p.wait(Tn, "v", evac[ptn])
            wt = p.op(Tn, "transpose",
                      ps[ptn][0:128, tf * 128:(tf + 1) * 128],
                      sb["cxbf"][:, tf * 256 + ct * 128:
                                 tf * 256 + ct * 128 + 128],
                      sb["idb"][:, :], inc="t")
            p.wait(V, "t", wt)
            evac[ptn] = p.op(V, "tensor_copy",
                             sb["ckx"][:, ct * 1024 + tf * 128:
                                       ct * 1024 + (tf + 1) * 128],
                             ps[ptn][0:128, tf * 128:(tf + 1) * 128], inc="v")

    p.barrier()
    if core_debug:
        tapout("ckk", (128, 2048), BF16)
        tapout("ckx", (128, 2048), BF16)
        p.wait(Sy, "v")
        p.dma(Sy, taps["ckk"], sb["ckk"][:, :], "dO")
        p.dma(Sy, taps["ckx"], sb["ckx"][:, :], "dO")
        for _e in ENGS:
            p.wait(_e, "dO", p.cnt["dO"])

    unit_hist = []
    hsub_s = [0, 0]
    last_vW = [0]
    for sub in range(NSUB):
        s0 = sub * TS
        p.wait(Tn, "v", evac["ptBf"])
        for i2 in range(2):
            tt16 = sub * 2 + i2
            wt = p.op(Tn, "transpose", ps["ptBf"][0:1, 256:384],
                      sb["rrmscols"][:, tt16:tt16 + 1], sb["idf"][:, :],
                      inc="t")
            p.wait(V, "t", wt)
            p.wait(V, "t", rowscr_t[0])
            rs = p.op(V, "tensor_copy",
                      sb["rowscr"][0:1, i2 * 128:(i2 + 1) * 128],
                      ps["ptBf"][0:1, 256:384], inc="v")
            p.wait(Tn, "v", rs)
        wb = p.op(Tn, "matmul", ps["ptBf"][:, 0:256], sb["onesf"][0:1, 0:128],
                  sb["rowscr"][0:1, 0:256], start=True, stop=True, inc="t")
        rowscr_t[0] = wb
        p.wait(V, "t", wb)
        rep_r = p.op(V, "tensor_copy", sb["repsb"][:, 0:256],
                     ps["ptBf"][:, 0:256], inc="v")
        evac["ptBf"] = rep_r
        for k in range(8):
            fsl2 = sb["fstg"][:, (k % 2) * 256:(k % 2) * 256 + 256]
            p.wait(V, "v", rep_r)
            p.wait(V, "s", hsub_s[k % 2])
            hv = p.op(V, "tensor_tensor", fsl2,
                      sb["xnhi"][:, k * 2048 + s0:k * 2048 + s0 + TS],
                      sb["repsb"][:, 0:256], ALU.mult, inc="v")
            p.wait(Sc, "v", hv)
            hsub_s[k % 2] = p.op(Sc, "activation",
                                 sb["hsub"][:, k * 256:(k + 1) * 256], fsl2,
                                 ACT.Identity, bias=0.0,
                                 scale=sb["wrmsfm"][:, k:k + 1], inc="s")
        hsub_ready = p.cnt["s"]
        o_order = (list(range(16, 24)) + list(range(0, 8)) +
                   list(range(8, 16)) + [24, 25])
        g_evac = {}
        for oi, o in enumerate(o_order):
            tok = ring_load(dr["wint"][o])
            pmname = "pu%d" % (2 + oi % 2)
            pp = ps[pmname][:, 0:256]
            p.wait(Tn, "v", evac[pmname])
            if oi == 0:
                p.wait(Tn, "s", hsub_ready)
            wsl, slot = ring_use(tok)
            for k in range(8):
                p.op(Tn, "matmul", pp, wsl[:, k * 128:(k + 1) * 128],
                     sb["hsub"][:, k * 256:(k + 1) * 256],
                     start=(k == 0), stop=(k == 7),
                     inc="t" if k == 7 else None)
            ring_done(slot)
            wt = p.cnt["t"]
            if 16 <= o < 24:
                tf = o - 16
                p.wait(Sc, "t", wt)
                ws = p.op(Sc, "activation",
                          sb["scrA"][:, tf * 256:(tf + 1) * 256], pp,
                          ACT.Sigmoid, bias=sb["gbs"][:, tf:tf + 1],
                          scale=1.0, inc="s")
                g_evac[tf] = ws
                p.wait(Tn, "s", ws)
                evac[pmname] = 0
            elif o < 8:
                tf = o
                p.wait(Sc, "t", wt)
                ws = p.op(Sc, "activation",
                          sb["scrE"][:, tf * 256:(tf + 1) * 256], pp,
                          ACT.Sigmoid, inc="s")
                p.wait(Tn, "s", ws)
                evac[pmname] = 0
            elif o < 16:
                tf = o - 8
                p.wait(V, "t", wt)
                p.wait(V, "s", g_evac[tf])
                p.op(V, "tensor_tensor",
                     sb["scrB"][:, tf * 256:(tf + 1) * 256], pp,
                     sb["scrA"][:, tf * 256:(tf + 1) * 256], ALU.mult,
                     inc="v")
                p.wait(V, "v")
                evac[pmname] = p.op(V, "tensor_tensor",
                                    sb["scrB"][:, tf * 256:(tf + 1) * 256],
                                    pp, sb["scrB"][:, tf * 256:(tf + 1) * 256],
                                    ALU.subtract, inc="v")
            elif o == 24:
                p.wait(V, "t", wt)
                p.op(V, "tensor_scalar", sb["qf"][:, :], pp, 0.125, None,
                     ALU.mult, inc="v")
                evac[pmname] = p.op(V, "tensor_scalar", sb["qb"][:, :], pp,
                                    0.125, None, ALU.mult, inc="v")
            else:
                p.wait(V, "t", wt)
                evac[pmname] = p.op(V, "tensor_copy", sb["kf"][:, :], pp,
                                    inc="v")
                for tf in range(8):
                    p.wait(V, "s", g_evac[tf])
                    p.op(V, "tensor_tensor", sb["fstg"][:, 0:256],
                         sb["kf"][:, :],
                         sb["scrA"][:, tf * 256:(tf + 1) * 256], ALU.mult,
                         inc="v")
                    p.wait(V, "v")
                    p.op(V, "tensor_tensor",
                         sb["scrC"][:, tf * 256:(tf + 1) * 256],
                         sb["kf"][:, :], sb["fstg"][:, 0:256], ALU.subtract,
                         inc="v")
        p.op(V, "memset", sb["scrA"][:, 0:2048:64], 0.0, inc="v")
        p.wait(V, "v")
        for tf in range(8):
            p.op(V, "tensor_tensor_scan",
                 sb["scrD"][:, tf * 256:(tf + 1) * 256],
                 sb["scrA"][:, tf * 256:(tf + 1) * 256],
                 sb["scrB"][:, tf * 256:(tf + 1) * 256], 0.0, ALU.mult,
                 ALU.add, inc="v")
        for tf in range(8):
            p.op(V, "tensor_tensor_scan", sb["fstg"][:, 0:256],
                 sb["scrA"][:, tf * 256:(tf + 1) * 256],
                 sb["scrC"][:, tf * 256:(tf + 1) * 256], 0.0, ALU.mult,
                 ALU.add, inc="v")
            p.wait(V, "v")
            p.op(V, "tensor_copy", sb["scrB"][:, tf * 256:(tf + 1) * 256],
                 sb["fstg"][:, 0:256], inc="v")
            p.wait(V, "v")
        for tf in range(8):
            p.op(V, "tensor_tensor", sb["fstg"][:, 0:256], sb["qf"][:, :],
                 sb["scrB"][:, tf * 256:(tf + 1) * 256], ALU.mult, inc="v")
            p.wait(V, "v")
            p.op(V, "tensor_copy", sb["tmph"][:, 0:256], sb["fstg"][:, 0:256],
                 inc="v")
            p.wait(V, "v")
            tl_ready = p.op(V, "tensor_tensor", sb["tmpl"][:, :],
                            sb["fstg"][:, 0:256], sb["tmph"][:, 0:256],
                            ALU.subtract, inc="v")
            p.wait(Tn, "v", tl_ready)
            if tf == 0:
                p.wait(Tn, "v", evac["pu2"])
            for tti in range(2):
                col = tti * 16 + 2 * tf
                p.op(Tn, "matmul", ps["pu2"][:, col:col + 2],
                     sb["tmph"][:, tti * 128:(tti + 1) * 128],
                     sb["selb"][:, :], start=True, stop=False)
                p.op(Tn, "matmul", ps["pu2"][:, col:col + 2],
                     sb["tmpl"][:, tti * 128:(tti + 1) * 128],
                     sb["selb"][:, :], start=False, stop=True,
                     inc="t" if (tf == 7 and tti == 1) else None)
        p.wait(V, "t")
        p.op(V, "tensor_copy", sb["ilse"][:, :], ps["pu2"][:, 0:32], inc="v")
        p.wait(V, "v")
        ilse_ready = p.op(V, "tensor_scalar", sb["nilse"][:, :],
                          sb["ilse"][:, :], -1.0, None, ALU.mult, inc="v")
        evac["pu2"] = ilse_ready

        ST = lambda i: sb["stat"][:, 32 + i:32 + i + 1]
        for bt in range(4):
            tti, hb = bt // 2, bt % 2
            gt = sub * 2 + tti
            heads = range(8 * hb, 8 * hb + 8)
            prev_sI = unit_hist[-1]["sI"] if unit_hist else 0
            if len(unit_hist) >= 2:
                p.wait(Tn, "v", unit_hist[-2]["evac_v"])
            p.wait(Tn, "v", ilse_ready)
            BQ = ((ps["pu2"], ps["pu3"], ps["ptAf"], ps["ptBf"]),
                  (ps["pm0"], ps["pm1"], ps["pu0"], ps["pu1"]))
            nb = len(unit_hist)
            scb = list(BQ[nb % 2])
            for j, hd in enumerate(heads):
                sc_ps = scb[j // 2][:, (j % 2) * 256:(j % 2) * 256 + 256]
                rq = (hd % 2) * 64
                tfh = hd // 2
                p.op(Tn, "matmul", sc_ps,
                     sb["qb"][rq:rq + 64, tti * 128:(tti + 1) * 128],
                     sb["ckk"][rq:rq + 64, tfh * 256:(tfh + 1) * 256],
                     start=True, stop=False)
                p.op(Tn, "matmul", sc_ps, sb["ub"][:, :],
                     sb["msb"][0:2, gt * 256:(gt + 1) * 256],
                     start=False, stop=True,
                     inc="t" if j == 7 else None)
            wtS = p.cnt["t"]
            p.wait(Sc, "t", wtS)
            for j in range(8):
                p.op(Sc, "activation",
                     sb["scsB"][:, j * 256:(j + 1) * 256],
                     scb[j // 2][:, (j % 2) * 256:(j % 2) * 256 + 256],
                     ACT.Identity, bias=0.0, scale=1.0,
                     inc="s" if j == 7 else None)
            sCP = p.cnt["s"]
            p.wait(V, "s", sCP)
            m0_8 = sb["stat"][:, 32:40]
            mm8 = sb["stat"][:, 40:48]
            mn8 = sb["stat"][:, 48:56]
            st8 = sb["stat"][:, 56:64]
            den0 = sb["mx8"][:, 0:8]
            den8 = sb["mx8"][:, 32:40]
            rec8 = sb["mx8"][:, 40:48]
            ils = sb["ilse"][:, tti * 16 + 8 * hb:tti * 16 + 8 * hb + 8]
            scv = sb["scsB"][:, :].rearrange("p (j c) -> p j c", c=256)
            p.op(V, "reduce_max", m0_8, scv, axis=AX, inc="v")
            p.wait(V, "v")
            p.op(V, "tensor_tensor", mm8, m0_8, ils, ALU.max, inc="v")
            p.wait(V, "v")
            vMN = p.op(V, "tensor_scalar", mn8, mm8, -1.0, None, ALU.mult,
                       inc="v")
            p.wait(V, "v", vMN)
            p.wait(V, "s", prev_sI)
            vST = p.op(V, "tensor_tensor", st8, ils, mn8, ALU.add, inc="v")
            p.wait(V, "v", vST)
            p.wait(Tn, "v", vST)
            wT1 = p.op(Tn, "transpose", scb[0][0:8, 0:128], mn8,
                       sb["idf"][:, :], inc="t")
            p.wait(V, "t", wT1)
            vR1 = p.op(V, "tensor_copy", sb["r8"][:, :], scb[0][0:8, 0:128],
                       inc="v")
            p.wait(Tn, "v", vR1)
            for q4 in range(4):
                p.op(Tn, "matmul", scb[q4][:, 0:512], sb["r8"][:, :],
                     sb["bo8"][0:8, q4 * 512:(q4 + 1) * 512],
                     start=True, stop=True, inc="t" if q4 == 3 else None)
            wT2 = p.cnt["t"]
            p.wait(V, "t", wT2)
            for q4 in range(4):
                p.op(V, "tensor_tensor",
                     sb["scsB"][:, q4 * 512:(q4 + 1) * 512],
                     scb[q4][:, 0:512],
                     sb["scsB"][:, q4 * 512:(q4 + 1) * 512], ALU.add,
                     inc="v" if q4 in (1, 3) else None)
            vS2 = p.cnt["v"]
            p.wait(V, "v", vS2)
            p.wait(Sc, "v", vS2)
            sE = p.op(Sc, "activation", sb["wbig"][:, 0:2048],
                      sb["scsB"][:, :], ACT.Exp, bias=0.0, scale=1.0,
                      inc="s")
            iw8 = sb["mx8"][:, 48:56]
            sI = p.op(Sc, "activation", iw8, st8, ACT.Exp, bias=0.0,
                      scale=1.0, inc="s")
            p.wait(V, "s", sI)
            wv = sb["wbig"][:, 0:2048].rearrange("p (j c) -> p j c", c=256)
            p.op(V, "reduce_sum", den0, wv, axis=AX, inc="v")
            p.wait(V, "v")
            p.op(V, "tensor_tensor", den8, den0, iw8, ALU.add, inc="v")
            p.wait(V, "v")
            vD = p.op(V, "reciprocal", rec8, den8, inc="v")
            p.wait(V, "v", vD)
            p.wait(Tn, "v", vD)
            p.op(Tn, "transpose", scb[0][0:8, 0:128], iw8, sb["idf"][:, :])
            wT3 = p.op(Tn, "transpose", scb[0][0:8, 128:256], rec8,
                       sb["idf"][:, :], inc="t")
            p.wait(V, "t", wT3)
            p.op(V, "tensor_copy", sb["r8b"][0:8, 0:128],
                 scb[0][0:8, 0:128])
            vR3 = p.op(V, "tensor_copy", sb["r8b"][0:8, 128:256],
                       scb[0][0:8, 128:256], inc="v")
            p.wait(Tn, "v", vR3)
            for tfl in range(4):
                p.op(Tn, "matmul", scb[0][:, tfl * 128:(tfl + 1) * 128],
                     sb["selh"][0:8, tfl * 128:(tfl + 1) * 128],
                     sb["r8b"][0:8, 0:128], start=True, stop=True)
            for tfl in range(4):
                p.op(Tn, "matmul", scb[1][:, tfl * 128:(tfl + 1) * 128],
                     sb["selh"][0:8, tfl * 128:(tfl + 1) * 128],
                     sb["r8b"][0:8, 128:256], start=True, stop=True)
            p.wait(Tn, "v", last_vW[0])
            for j in range(8):
                for ct in range(2):
                    idx = j * 2 + ct
                    pt = ps["ptA"] if idx < 8 else ps["ptB"]
                    p.op(Tn, "transpose",
                         pt[0:128, (idx % 8) * 128:(idx % 8) * 128 + 128],
                         sb["wbig"][:, j * 256 + ct * 128:
                                    j * 256 + ct * 128 + 128],
                         sb["idb"][:, :],
                         inc="t" if idx == 15 else None)
            wT4 = p.cnt["t"]
            p.wait(V, "t", wT4)
            p.op(V, "tensor_copy", sb["wbig"][:, 2048:3072],
                 ps["ptA"][0:128, 0:1024])
            vW = p.op(V, "tensor_copy", sb["wbig"][:, 3072:4096],
                      ps["ptB"][0:128, 0:1024], inc="v")
            last_vW[0] = vW
            p.wait(Tn, "v", vW)
            for j, hd in enumerate(heads):
                rq = (hd % 2) * 64
                tfl = j // 2
                for ct in range(2):
                    idx = j * 2 + ct
                    p.op(Tn, "matmul",
                         scb[2][rq:rq + 64, tfl * 128:(tfl + 1) * 128],
                         sb["ckx"][:, ct * 1024 + hd * 64:
                                   ct * 1024 + hd * 64 + 64],
                         sb["wbig"][:, 2048 + idx * 128:2048 + idx * 128 + 128],
                         start=(ct == 0), stop=(ct == 1),
                         inc="t" if j == 7 and ct == 1 else None)
            wtO = p.cnt["t"]
            p.wait(V, "t", wtO)
            xkv = sb["scrD"][:, :].rearrange("p (tf c) -> p tf c", c=256)[
                :, 4 * hb:4 * hb + 4, tti * 128:(tti + 1) * 128]
            zv = sb["scrE"][:, :].rearrange("p (tf c) -> p tf c", c=256)[
                :, 4 * hb:4 * hb + 4, tti * 128:(tti + 1) * 128]
            azv = sb["azt"][:, :].rearrange("p (tf c) -> p tf c", c=256)[
                :, 4 * hb:4 * hb + 4, tti * 128:(tti + 1) * 128]
            p.op(V, "tensor_tensor", sb["fstg"][:, :], scb[0][:, 0:512],
                 xkv, ALU.mult, inc="v")
            p.wait(V, "v")
            p.op(V, "tensor_tensor", sb["scsB"][:, 0:512],
                 scb[2][:, 0:512], sb["fstg"][:, :], ALU.add, inc="v")
            p.wait(V, "v")
            p.op(V, "tensor_tensor", sb["fstg"][:, :], sb["scsB"][:, 0:512],
                 scb[1][:, 0:512], ALU.mult, inc="v")
            p.wait(V, "v")
            ev = p.op(V, "tensor_tensor", azv, sb["fstg"][:, :], zv,
                      ALU.mult, inc="v")
            unit_hist.append({"evac_v": ev, "sI": sI})
            qnames = (("pu2", "pu3", "ptAf", "ptBf"),
                      ("pm0", "pm1", "pu0", "pu1"))[nb % 2]
            for kk in qnames:
                evac[kk] = ev
            evac["ptA"] = ev
            evac["ptB"] = ev
        p.wait(Tn, "v", unit_hist[-1]["evac_v"])
        for o in range(8):
            tok = ring_load(dr["woutt"][o])
            pmname = "pu%d" % (2 + o % 2)
            pp = ps[pmname][:, 0:256]
            p.wait(Tn, "v", evac[pmname])
            wsl, slot = ring_use(tok)
            for k in range(8):
                p.op(Tn, "matmul", pp, wsl[:, k * 128:(k + 1) * 128],
                     sb["azt"][:, k * 256:(k + 1) * 256],
                     start=(k == 0), stop=(k == 7),
                     inc="t" if k == 7 else None)
            ring_done(slot)
            p.wait(V, "t")
            evac[pmname] = p.op(V, "tensor_tensor",
                                sb["sreg"][:, o * 2048 + s0:
                                           o * 2048 + s0 + TS],
                                pp, sb["xnhi"][:, o * 2048 + s0:
                                               o * 2048 + s0 + TS],
                                ALU.add, inc="v")

    p.barrier()
    if core_debug:
        tapout("sreg", (128, 16384), BF16)
        p.wait(Sy, "v")
        p.dma(Sy, taps["sreg"], sb["sreg"][:, :], "dO")
        for _e in ENGS:
            p.wait(_e, "dO", p.cnt["dO"])

    w1_tcnt = [0, 0]
    w2_tcnt = [0, 0]
    evac_s = {}
    QUADS = (("pm0", "pm1", "pu0", "pu1"), ("pu2", "pu3", "ptAf", "ptBf"))
    b1_scnt = [0, 0]
    stg_vcnt = [0, 0]
    db_tok = {0: db0_early}
    for e_i in range(NE):
        bp = e_i % 2
        db = db_tok[e_i]
        if e_i + 1 < NE:
            bpn = (e_i + 1) % 2
            p.wait(Gp, "s", b1_scnt[bpn])
            db_tok[e_i + 1] = p.dma(Gp, sb["b1s"][:, bpn * 32:(bpn + 1) * 32],
                                    dr["b1t"][e_i + 1], "db%d" % bpn)
        for q4 in range(4):
            pmname = "pm%d" % (q4 % 2)
            pp = ps[pmname][:, 0:512]
            p.wait(Tn, "v", evac[pmname])
            p.wait(Tn, "s", evac_s.get(pmname, 0))
            wt = p.op(Tn, "matmul", pp,
                      sb["sel8"][:, e_i * 128:(e_i + 1) * 128],
                      sb["oh"][0:8, q4 * 512:(q4 + 1) * 512],
                      start=True, stop=True, inc="t")
            p.wait(V, "t", wt)
            evac[pmname] = p.op(V, "tensor_copy",
                                sb["scrE"][:, q4 * 512:(q4 + 1) * 512], pp,
                                inc="v")
        mask_ready = p.cnt["v"]
        for q in range(4):
            p.wait(V, "t")
            h_ready_k = []
            for ft in range(8):
                wp = ft % 2
                p.wait(Gp, "t", w1_tcnt[wp])
                c1 = p.dma(Gp, sb["w1buf"][:, wp * 1024:(wp + 1) * 1024],
                           dr["w1t"][e_i, q, ft], "dW1%d" % wp)
                w1sl = sb["w1buf"][:, wp * 1024:(wp + 1) * 1024]
                banks = QUADS[ft % 2]
                for bn in banks:
                    p.wait(Tn, "v", evac[bn])
                    p.wait(Tn, "s", evac_s.get(bn, 0))
                p.wait(Tn, "dW1%d" % wp, c1)
                for k in range(8):
                    for s4 in range(4):
                        p.op(Tn, "matmul", ps[banks[s4]][:, 0:512],
                             w1sl[:, k * 128:(k + 1) * 128],
                             sb["xnhi"][:, k * 2048 + s4 * 512:
                                        k * 2048 + (s4 + 1) * 512],
                             start=(k == 0), stop=(k == 7),
                             inc="t" if (k == 7 and s4 == 3) else None)
                wt = p.cnt["t"]
                bcol = sb["b1s"][:, bp * 32 + q * 8 + ft:
                                 bp * 32 + q * 8 + ft + 1]
                p.wait(Sc, "t", wt)
                p.wait(Sc, "db%d" % bp, db)
                for s4 in range(4):
                    sp = s4 % 2
                    stgsl = sb["stg"][:, sp * 512:(sp + 1) * 512]
                    p.wait(Sc, "v", stg_vcnt[sp])
                    ws = p.op(Sc, "activation", stgsl, ps[banks[s4]][:, 0:512],
                              ACT.Gelu, bias=bcol, scale=1.0, inc="s")
                    evac_s[banks[s4]] = ws
                    evac[banks[s4]] = 0
                    p.wait(V, "s", ws)
                    p.wait(V, "v", mask_ready)
                    hsl = sb["big"][:, ft * 2048 + s4 * 512:
                                    ft * 2048 + (s4 + 1) * 512]
                    stg_vcnt[sp] = p.op(V, "tensor_tensor", hsl, stgsl,
                                        sb["scrE"][:, s4 * 512:(s4 + 1) * 512],
                                        ALU.mult, inc="v")
                w1_tcnt[wp] = p.cnt["t"]
                h_ready_k.append(p.cnt["v"])
            h_ready = p.cnt["v"]
            for d in range(8):
                wp = d % 2
                p.wait(Gp, "t", w2_tcnt[wp])
                c2 = p.dma(Gp, sb["w2buf"][:, wp * 1024:(wp + 1) * 1024],
                           dr["w2t"][e_i, d, q], "dW2%d" % wp)
                w2sl = sb["w2buf"][:, wp * 1024:(wp + 1) * 1024]
                banks = QUADS[d % 2]
                for bn in banks:
                    p.wait(Tn, "v", evac[bn])
                    p.wait(Tn, "s", evac_s.get(bn, 0))
                p.wait(Tn, "dW2%d" % wp, c2)
                if d > 0:
                    p.wait(Tn, "v", h_ready)
                last = (e_i == NE - 1 and q == 3)
                for k in range(8):
                    if d == 0:
                        p.wait(Tn, "v", h_ready_k[k])
                    for s4 in range(4):
                        p.op(Tn, "matmul", ps[banks[s4]][:, 0:512],
                             w2sl[:, k * 128:(k + 1) * 128],
                             sb["big"][:, k * 2048 + s4 * 512:
                                       k * 2048 + (s4 + 1) * 512],
                             start=(k == 0), stop=(k == 7 and not last),
                             inc="t" if (k == 7 and s4 == 3 and not last)
                             else None)
                if last:
                    for s4 in range(4):
                        p.op(Tn, "matmul", ps[banks[s4]][:, 0:512],
                             sb["b2hi"][0:8, d * 128:(d + 1) * 128],
                             sb["oh"][0:8, s4 * 512:(s4 + 1) * 512],
                             start=False, stop=False)
                        p.op(Tn, "matmul", ps[banks[s4]][:, 0:512],
                             sb["b2lo"][0:8, d * 128:(d + 1) * 128],
                             sb["oh"][0:8, s4 * 512:(s4 + 1) * 512],
                             start=False, stop=True,
                             inc="t" if s4 == 3 else None)
                wt = p.cnt["t"]
                p.wait(V, "t", wt)
                for s4 in range(4):
                    osl = sb["offn"][:, d * 2048 + s4 * 512:
                                     d * 2048 + (s4 + 1) * 512]
                    if e_i == 0 and q == 0:
                        ev = p.op(V, "tensor_copy", osl,
                                  ps[banks[s4]][:, 0:512], inc="v")
                    else:
                        ev = p.op(V, "tensor_tensor", osl,
                                  ps[banks[s4]][:, 0:512], osl, ALU.add,
                                  inc="v")
                    evac[banks[s4]] = ev
                w2_tcnt[wp] = p.cnt["t"]
        b1_scnt[bp] = p.cnt["s"]

    p.barrier()
    if core_debug:
        tapout("offn", (128, 16384), BF16)
        p.wait(Sy, "v")
        p.dma(Sy, taps["offn"], sb["offn"][:, :], "dO")
        for _e in ENGS:
            p.wait(_e, "dO", p.cnt["dO"])

    xrl_used = [0, 0]
    ot_stored = [0, 0]
    last_tpF = [0]
    for tt in range(NTT):
        par = tt % 2
        xr = sb["xrl"][:, par * 1024:(par + 1) * 1024]
        ot = sb["otile"][:, par * 1024:(par + 1) * 1024]
        p.wait(Sy, "v", xrl_used[par])
        dx = p.dma(Sy, xr, dr["xs"][tt * 128:(tt + 1) * 128, :], "dX%d" % par)
        p.wait(Tn, "v", evac["pu3"])
        wtp = p.op(Tn, "transpose", ps["pu3"][0:1, 128:256],
                   sb["pmcols"][:, tt:tt + 1], sb["idf"][:, :], inc="t")
        p.wait(V, "t", wtp)
        p.wait(V, "t", rowscr_t[0])
        rsc = p.op(V, "tensor_copy", sb["rowscr"][0:1, 0:128],
                   ps["pu3"][0:1, 128:256], inc="v")
        p.wait(Tn, "v", rsc)
        wt = p.op(Tn, "matmul", ps["pu3"][:, 0:128], sb["onesf"][0:1, 0:128],
                  sb["rowscr"][0:1, 0:128], start=True, stop=True, inc="t")
        rowscr_t[0] = wt
        ptn = "ptA" if par == 0 else "ptB"
        pt = ps[ptn]
        p.wait(Tn, "v", evac[ptn])
        p.wait(V, "t", wt)
        p.wait(V, "t", last_tpF[0])
        for d in range(8):
            dsl = slice(d * 2048 + tt * 128, d * 2048 + tt * 128 + 128)
            p.op(V, "tensor_tensor", sb["scrE"][:, d * 128:(d + 1) * 128],
                 sb["offn"][:, dsl], ps["pu3"][:, 0:128], ALU.mult,
                 inc="v" if d == 7 else None)
        vb = p.cnt["v"]
        p.wait(Sc, "v", vb)
        ws = p.op(Sc, "activation", sb["stg"][:, 0:1024],
                  sb["scrE"][:, 0:1024], ACT.Silu, inc="s")
        p.wait(V, "s", ws)
        srv = sb["sreg"][:, :].rearrange("p (d c) -> p d c", c=2048)[
            :, :, tt * 128:tt * 128 + 128]
        rv = p.op(V, "tensor_tensor",
                  sb["scrE"][:, 0:1024].rearrange("p (d c) -> p d c", c=128),
                  sb["stg"][:, 0:1024].rearrange("p (d c) -> p d c", c=128),
                  srv, ALU.mult, inc="v")
        p.wait(Tn, "v", rv)
        for d in range(8):
            p.op(Tn, "transpose", pt[0:128, d * 128:(d + 1) * 128],
                 sb["scrE"][:, d * 128:(d + 1) * 128], sb["idb"][:, :],
                 inc="t" if d == 7 else None)
        last_tpF[0] = p.cnt["t"]
        evac["pu3"] = p.cnt["v"]
        wt2 = p.cnt["t"]
        p.wait(V, "t", wt2)
        p.wait(V, "dX%d" % par, dx)
        addv = p.op(V, "tensor_tensor", ot, xr, pt[0:128, 0:1024], ALU.add,
                    inc="v")
        evac[ptn] = addv
        evac["pu3"] = addv
        xrl_used[par] = addv
        p.wait(Sy, "v", addv)
        ot_stored[par] = p.dma(Sy, y[tt * 128:(tt + 1) * 128, :], ot, "dO")

    p.wait(Sy, "dO", p.cnt["dO"])
    p.build()
    return nc, taps


def _prep_inputs(L0_attn_hidden_states, L0_attn_rms_weight, L0_attn_W_in,
                 L0_attn_gate_bias, L0_attn_W_out, L0_ffn_Wr, L0_ffn_W1,
                 L0_ffn_b1, L0_ffn_W2, L0_ffn_b2):
    x2 = np.ascontiguousarray(
        np.asarray(L0_attn_hidden_states, np.float32).reshape(16384, D))
    W_in = np.asarray(L0_attn_W_in, np.float32)
    W_out = np.asarray(L0_attn_W_out, np.float32)
    Wr = np.asarray(L0_ffn_Wr, np.float32)
    W1 = np.asarray(L0_ffn_W1, np.float32)
    W2 = np.asarray(L0_ffn_W2, np.float32)
    b1 = np.asarray(L0_ffn_b1, np.float32)
    b2 = np.asarray(L0_ffn_b2, np.float32)
    rms = np.asarray(L0_attn_rms_weight, np.float32)
    gb = np.asarray(L0_attn_gate_bias, np.float32)

    com = {}
    com["xsum"] = np.ascontiguousarray(x2[0::CHUNK])
    com["wint"] = np.ascontiguousarray(
        W_in.reshape(8, 128, 26, 128).transpose(2, 1, 0, 3)
        .reshape(26, 128, 1024))
    com["woutt"] = np.ascontiguousarray(
        W_out.reshape(8, 128, 8, 128).transpose(2, 1, 0, 3)
        .reshape(8, 128, 1024))
    com["wrt"] = np.ascontiguousarray(
        Wr.reshape(8, 128, 8).transpose(1, 0, 2).reshape(128, 64))
    com["w1t"] = np.ascontiguousarray(
        W1.reshape(NE, 8, 128, 32, 128).transpose(0, 3, 2, 1, 4)
        .reshape(NE, 4, 8, 128, 1024))
    com["w2t"] = np.ascontiguousarray(
        W2.reshape(NE, 4, 8, 128, 8, 128).transpose(0, 4, 1, 3, 2, 5)
        .reshape(NE, 8, 4, 128, 1024))
    com["b1t"] = np.ascontiguousarray(
        b1.reshape(NE, 32, 128).transpose(0, 2, 1))
    com["b2m"] = b2
    com["rmswt"] = np.ascontiguousarray(rms.reshape(8, 128).T)
    com["gbt"] = np.ascontiguousarray(gb.reshape(8, 128).T)
    com["ubh"] = np.zeros((2, 128), BF)
    com["ubh"][0, 0:64] = 1
    com["ubh"][1, 64:128] = 1
    com["idfh"] = np.eye(128, dtype=np.float32)
    com["idbh"] = np.eye(128, dtype=np.float32).astype(BF)
    com["onesfh"] = np.ones((1, 256), np.float32)
    com["onesbh"] = np.ones((1, 128), BF)
    selb = np.zeros((128, 2), BF)
    selb[0:64, 0] = 1
    selb[64:128, 1] = 1
    com["selbh"] = selb
    sel8 = np.zeros((8, 1024), BF)
    for e in range(8):
        sel8[e, e * 128:(e + 1) * 128] = 1
    com["sel8h"] = sel8
    com["iota8h"] = np.tile(np.arange(8, dtype=np.float32), (128, 1))
    bo8 = np.zeros((8, 2048), np.float32)
    for h in range(8):
        bo8[h, h * 256:(h + 1) * 256] = 1.0
    com["bo8h"] = bo8
    selh = np.zeros((8, 512), np.float32)
    for tfl in range(4):
        for pp_ in range(128):
            selh[2 * tfl + pp_ // 64, tfl * 128 + pp_] = 1.0
    com["selhh"] = selh

    in_maps = []
    for c in range(NCORES):
        m = dict(com)
        m["xs"] = np.ascontiguousarray(x2[c * T:(c + 1) * T])
        msb = np.zeros((2, 16 * 256), np.float32)
        for tt in range(16):
            c0 = (c * T + tt * 128) // CHUNK
            cc = np.arange(NCH)
            msb[0, tt * 256:(tt + 1) * 256] = np.where(cc >= c0, NEG, 0.0)
            msb[1, tt * 256:(tt + 1) * 256] = np.where(cc >= c0 + 1, NEG, 0.0)
        m["msbh"] = msb.astype(BF)
        in_maps.append(m)
    return in_maps


_NC_CACHE = {}


def kernel(**inputs):
    in_maps = _prep_inputs(**inputs)
    if "nc" not in _NC_CACHE:
        _NC_CACHE["nc"] = build(core_debug=False)[0]
    nc = _NC_CACHE["nc"]
    res = run_bass_kernel_spmd(nc, in_maps, core_ids=list(range(NCORES)))
    out = np.concatenate([np.asarray(r["y"]) for r in res.results], axis=0)
    return out.reshape(1, 16384, D)


# version 17 (final)
# speedup vs baseline: 1.2880x; 1.0099x over previous
import numpy as np
import ml_dtypes
import concourse.bass as bass
import concourse.mybir as mybir
from concourse.bass_utils import run_bass_kernel_spmd
from contextlib import ExitStack

F32 = mybir.dt.float32
BF16 = mybir.dt.bfloat16
BF = ml_dtypes.bfloat16
AX = mybir.AxisListType.X
ALU = mybir.AluOpType
ACT = mybir.ActivationFunctionType

NCORES = 8
T = 2048
D = 1024
DH = 64
NHEAD = 16
CHUNK = 64
NCH = 256
FH = 4096
NE = 8
NEG = np.float32(-1e30)
TS = 256
NSUB = T // TS
NTT = T // 128

ENGS = ("sync", "gpsimd", "vector", "scalar", "tensor")
V, Sc, Tn, Sy, Gp = "vector", "scalar", "tensor", "sync", "gpsimd"

BASE = 20480
SB = {}
_off = [BASE]
def _reg(name, nbytes, align=64):
    _off[0] = (_off[0] + align - 1) // align * align
    SB[name] = _off[0]
    _off[0] += nbytes

_reg("xnhi", 32768)
_reg("sreg", 32768)
_reg("big", 32768)
_reg("wring", 4096)
_reg("w1buf", 4096)
_reg("w2buf", 4096)
_reg("scrA", 8192)
_reg("scrB", 8192)
_reg("scrC", 8192)
_reg("scrD", 8192)
_reg("scrE", 4096)
_reg("msb", 8192)
_reg("ckk", 4096)
_reg("ckx", 4096)
_reg("oh", 4096)
_reg("rowscr", 1536)
_reg("pmcols", 64)
_reg("sel8", 2048)
_reg("b2hi", 2048)
_reg("b2lo", 2048)
_reg("idf", 512)
_reg("idb", 256)
_reg("onesf", 1024)
_reg("onesb", 256)
_reg("selb", 32)
_reg("ub", 256)
_reg("wrf", 256)
_reg("wrhib", 128)
_reg("wrlob", 128)
_reg("gbs", 32)
_reg("wrmsfm", 32)
_reg("b1s", 256)
_reg("qf", 1024)
_reg("qb", 512)
_reg("kf", 1024)
_reg("ilse", 128)
_reg("nilse", 128)
_reg("stat", 256)
_reg("tmph", 1024)
_reg("tmpl", 512)
_reg("stg", 2048)
_reg("fstg", 2048)
_reg("istg", 512)
_reg("istg2", 512)
_reg("r8", 512)
_reg("lgt", 128)
_reg("ohtm", 64)
_reg("mx8", 256)
_reg("repsb", 1024)
_reg("hsub", 4096)
_reg("rrmscols", 64)
_reg("warmt", 64)
_reg("wbias", 64)
_reg("iota8", 32)
_reg("idxu", 64)
_reg("bo8", 8192)
_reg("selh", 2048)
_reg("r8b", 1024)
_reg("statln", 256)
_reg("mrepsb", 512)
_reg("rrepsb", 512)
_reg("rrrepsb", 512)
assert _off[0] <= 229376, _off[0]


class Prog:

    def __init__(self, nc):
        self.nc = nc
        self.q = {e: [] for e in ENGS}
        self.sem = {}
        self.cnt = {}
        self.ctx = ExitStack()

    def new_sem(self, name):
        self.sem[name] = self.ctx.enter_context(self.nc.semaphore(name))
        self.cnt[name] = 0

    def op(self, eng, meth, *args, inc=None, n=1, **kw):
        if inc is not None:
            self.cnt[inc] += n
        self.q[eng].append((meth, args, kw, inc, n))
        return self.cnt[inc] if inc else None

    def dma(self, eng, out_ap, in_ap, inc):
        return self.op(eng, "dma_start", inc=inc, n=16, out=out_ap, in_=in_ap)

    def wait(self, eng, sem, val=None):
        v = self.cnt[sem] if val is None else val
        if v > 0:
            self.q[eng].append(("__wait__", (sem, v), {}, None, 0))

    def barrier(self, engs=ENGS, sems=("v", "s", "t")):
        vals = [(sm, self.cnt[sm]) for sm in sems]
        for e in engs:
            for sm, v in vals:
                self.wait(e, sm, v)

    def build(self):
        block = self.ctx.enter_context(self.nc.Block())
        for eng in ENGS:
            items = self.q[eng]
            def body(e, items=items):
                for meth, args, kw, inc, n in items:
                    if meth == "__wait__":
                        e.wait_ge(self.sem[args[0]], args[1])
                    else:
                        ins = getattr(e, meth)(*args, **kw)
                        if inc is not None:
                            ins.then_inc(self.sem[inc], n)
            getattr(block, eng)(body)
        self.ctx.close()


def build(core_debug=False):
    nc = bass.Bass("TRN2", target_bir_lowering=False,
                   detect_race_conditions=False)

    dr = {}
    def din(name, shape, dt=F32):
        dr[name] = nc.dram_tensor(name, list(shape), dt,
                                  kind="ExternalInput").ap()
    din("xs", (T, D))
    din("xsum", (NCH, D))
    din("wint", (26, 128, 1024))
    din("woutt", (8, 128, 1024))
    din("wrt", (128, 64))
    din("w1t", (NE, 4, 8, 128, 1024))
    din("w2t", (NE, 8, 4, 128, 1024))
    din("b1t", (NE, 128, 32))
    din("b2m", (NE, D))
    din("rmswt", (128, 8))
    din("gbt", (128, 8))
    din("msbh", (2, 16 * 256), BF16)
    din("ubh", (2, 128), BF16)
    din("idfh", (128, 128))
    din("idbh", (128, 128), BF16)
    din("onesfh", (1, 256))
    din("onesbh", (1, 128), BF16)
    din("selbh", (128, 2), BF16)
    din("sel8h", (8, 1024), BF16)
    din("iota8h", (128, 8))
    din("bo8h", (8, 2048))
    din("selhh", (8, 512))
    y = nc.dram_tensor("y", [T, D], F32, kind="ExternalOutput").ap()

    taps = {}
    def tapout(name, shape, dt=F32):
        taps[name] = nc.dram_tensor("tap_" + name, list(shape), dt,
                                    kind="ExternalOutput").ap()

    sb = {}
    def S_(name, shape, dt, key=None, extra=0):
        sb[name] = nc.alloc_sbuf_tensor_at(
            name, list(shape), dt, offset=SB[key or name] + extra).ap()
    S_("xnhi", (128, 16384), BF16)
    S_("xrl", (128, 2048), F32, "xnhi")
    S_("otile", (128, 2048), F32, "xnhi", 8192)
    S_("sreg", (128, 16384), BF16)
    S_("big", (128, 16384), BF16)
    S_("wring", (128, 2048), BF16)
    S_("w1buf", (128, 2048), BF16)
    S_("w2buf", (128, 2048), BF16)
    S_("scrA", (128, 2048), F32)
    S_("xa", (128, 2048), F32, "scrA")
    S_("azt", (128, 2048), BF16, "scrA")
    S_("scrB", (128, 2048), F32)
    S_("scrC", (128, 2048), F32)
    S_("scrD", (128, 2048), F32)
    S_("cxbf", (128, 2048), BF16, "scrD")
    S_("offn", (128, 16384), BF16, "scrA")
    S_("scrE", (128, 2048), BF16)
    S_("msb", (2, 4096), BF16)
    S_("ckk", (128, 2048), BF16)
    S_("ckx", (128, 2048), BF16)
    S_("rowscr", (1, 384), F32)
    S_("pmcols", (128, 16), F32)
    S_("sel8", (8, 1024), BF16)
    S_("oh", (8, 2048), BF16)
    S_("b2hi", (8, 1024), BF16)
    S_("b2lo", (8, 1024), BF16)
    S_("b2f32", (8, 1024), F32, "scrB")
    S_("idf", (128, 128), F32)
    S_("idb", (128, 128), BF16)
    S_("onesf", (1, 256), F32)
    S_("onesb", (1, 128), BF16)
    S_("selb", (128, 2), BF16)
    S_("ub", (2, 128), BF16)
    S_("wrf", (128, 64), F32)
    S_("wrhib", (128, 64), BF16)
    S_("wrlob", (128, 64), BF16)
    S_("gbs", (128, 8), F32)
    S_("wrmsfm", (128, 8), F32)
    S_("b1s", (128, 64), F32)
    S_("qf", (128, 256), F32)
    S_("qb", (128, 256), BF16)
    S_("kf", (128, 256), F32)
    S_("ilse", (128, 32), F32)
    S_("nilse", (128, 32), F32)
    S_("stat", (128, 64), F32)
    S_("tmph", (128, 512), BF16)
    S_("tmpl", (128, 256), BF16)
    S_("stg", (128, 1024), BF16)
    S_("fstg", (128, 512), F32)
    S_("istg", (128, 128), F32)
    S_("istg2", (128, 128), F32)
    S_("xloT", (128, 128), BF16, "istg")
    S_("r8", (8, 128), F32)
    S_("lgt", (128, 16), F32)
    S_("ohtm", (128, 16), BF16)
    S_("mx8", (128, 64), F32)
    S_("repsb", (128, 256), F32)
    S_("hsub", (128, 2048), BF16)
    S_("rrmscols", (128, 16), F32)
    S_("warmt", (128, 16), F32)
    S_("wbias", (128, 1), F32)
    S_("dbgln", (128, 512), F32, "scrD")
    S_("dbglg", (128, 128), F32, "scrD", 2048)
    S_("iota8", (128, 8), F32)
    S_("idxu", (128, 16), mybir.dt.uint32)
    S_("bo8", (8, 2048), F32)
    S_("selh", (8, 512), F32)
    S_("r8b", (8, 256), F32)
    S_("scsB", (128, 2048), F32, "scrB")
    S_("wbig", (128, 4096), BF16, "scrC")
    S_("statln", (128, 64), F32)
    S_("mrepsb", (128, 128), F32)
    S_("rrepsb", (128, 128), F32)
    S_("rrrepsb", (128, 128), F32)

    p = Prog(nc)
    ps = {}
    for nm, shape, dt in [
        ("pm0", (128, 512), F32), ("pm1", (128, 512), F32),
        ("pu0", (128, 512), F32), ("pu1", (128, 512), F32),
        ("ptA", (128, 1024), BF16), ("ptB", (128, 1024), BF16),
        ("pu2", (128, 512), F32), ("pu3", (128, 512), F32),
    ]:
        ps[nm] = p.ctx.enter_context(nc.psum_tensor(nm, list(shape), dt)).ap()

    for s in ("dL", "dX0", "dX1", "dR0", "dR1", "dW10", "dW11", "dW20",
              "dW21", "db0", "db1", "dO", "v", "s", "t"):
        p.new_sem(s)
    ps["ptAf"] = ps["ptA"].bitcast(F32)
    ps["ptBf"] = ps["ptB"].bitcast(F32)

    p.dma(Sy, sb["msb"][:, :], dr["msbh"], "dL")
    p.dma(Sy, sb["ub"][:, :], dr["ubh"], "dL")
    p.dma(Sy, sb["idf"][:, :], dr["idfh"], "dL")
    p.dma(Sy, sb["idb"][:, :], dr["idbh"], "dL")
    p.dma(Sy, sb["onesf"][:, :], dr["onesfh"], "dL")
    p.dma(Sy, sb["onesb"][:, :], dr["onesbh"], "dL")
    p.dma(Sy, sb["selb"][:, :], dr["selbh"], "dL")
    p.dma(Sy, sb["sel8"][:, :], dr["sel8h"], "dL")
    p.dma(Sy, sb["iota8"][:, :], dr["iota8h"], "dL")
    p.dma(Sy, sb["bo8"][:, :], dr["bo8h"], "dL")
    p.dma(Sy, sb["selh"][:, :], dr["selhh"], "dL")
    p.dma(Sy, sb["wrf"][:, :], dr["wrt"], "dL")
    p.dma(Sy, sb["gbs"][:, :], dr["gbt"], "dL")
    p.dma(Sy, sb["wrmsfm"][:, :], dr["rmswt"], "dL")
    p.dma(Sy, sb["b2f32"][:, :], dr["b2m"], "dL")
    cload = p.cnt["dL"]
    db0_early = p.dma(Gp, sb["b1s"][:, 0:32], dr["b1t"][0], "db0")

    p.wait(V, "dL", cload)
    p.op(V, "memset", sb["wbias"][:, :], 0.25)
    p.op(V, "memset", sb["statln"][:, :], 0.0)
    p.op(V, "memset", sb["mx8"][:, :], 0.0)
    p.op(V, "reduce_max", sb["warmt"][:, 0:1], sb["wbias"][:, :], axis=AX,
         inc="v")
    p.wait(V, "v")
    p.op(V, "tensor_scalar", sb["warmt"][:, 1:2], sb["wbias"][:, :],
         sb["wbias"][:, :], 2.0, ALU.subtract, ALU.mult)
    p.op(V, "scalar_tensor_tensor", sb["warmt"][:, 1:2], sb["wbias"][:, :],
         sb["wbias"][:, :], sb["wbias"][:, :], ALU.mult, ALU.add)
    p.op(V, "tensor_scalar", sb["warmt"][:, 1:2], sb["wbias"][:, :],
         sb["wbias"][:, :], None, ALU.is_equal, inc="v")
    p.wait(Sc, "v")
    for fn in (ACT.Exp, ACT.Gelu, ACT.Sigmoid, ACT.Silu, ACT.Square,
               ACT.Sqrt, ACT.Identity):
        p.op(Sc, "activation", sb["warmt"][:, 2:3], sb["wbias"][:, :], fn,
             bias=sb["wbias"][:, :], scale=1.0, accum_out=sb["warmt"][:, 3:4])
    p.op(Sc, "nop", inc="s")
    p.op(V, "tensor_copy", sb["wrhib"][:, :], sb["wrf"][:, :], inc="v")
    p.wait(V, "v")
    p.op(V, "tensor_tensor", sb["wrlob"][:, :], sb["wrf"][:, :],
         sb["wrhib"][:, :], ALU.subtract)
    p.op(V, "tensor_copy", sb["b2hi"][:, :], sb["b2f32"][:, :], inc="v")
    p.wait(V, "v")
    p.op(V, "tensor_tensor", sb["b2lo"][:, :], sb["b2f32"][:, :],
         sb["b2hi"][:, :], ALU.subtract, inc="v")

    evac = {k: 0 for k in ("pm0", "pm1", "pu0", "pu1", "pu2", "pu3",
                           "ptA", "ptB", "ptAf", "ptBf")}
    xa_done = [[], []]
    rowscr_t = [0]

    def ka(sem="v", at=None):
        vv = p.cnt[sem] if at is None else at
        if vv > 0:
            p.wait(Tn, sem, vv)
        p.op(Tn, "ldweights", sb["idb"][:, :])

    def ka_burst(n=30):
        for i in range(n):
            p.op(Tn, "ldweights", sb["idb"][:, :])
            p.op(Tn, "ldweights", sb["selb"][:, :])

    def ka_mm(sem=None, at=None, n=2, bank="pm0"):
        if sem is not None:
            vv = p.cnt[sem] if at is None else at
            if vv > 0:
                p.wait(Tn, sem, vv)
        for _ in range(n):
            p.op(Tn, "matmul", ps[bank][:, 0:512], sb["idb"][:, :],
                 sb["xnhi"][:, 0:512], start=True, stop=True)

    def ln_stats(src, dump, so):
        sl = lambda i: sb["statln"][:, so + i:so + i + 1]

        def fv(*args, **kw):
            c = p.op(V, *args, inc="v", **kw)
            p.wait(V, "v", c)
            return c

        fv("reduce_sum", sl(0), src, axis=AX)
        p.wait(Sc, "v")
        p.op(Sc, "activation", dump, src, ACT.Square, bias=0.0, scale=1.0,
             inc="s")
        fv("tensor_scalar", sl(2), sl(0), 1.0 / D, None, ALU.mult)
        fv("tensor_tensor", sl(3), sl(2), sl(2), ALU.mult)
        p.wait(V, "s")
        fv("reduce_sum", sl(1), dump, axis=AX)
        fv("tensor_scalar", sl(4), sl(1), 1.0 / D, None, ALU.mult)
        fv("tensor_tensor", sl(5), sl(4), sl(3), ALU.subtract)

        def newton(tin, eps, b):
            fv("tensor_scalar", sl(b), tin, 1.0, eps, ALU.mult, ALU.add)
            p.wait(Sc, "v")
            p.op(Sc, "activation", sl(b + 1), sl(b), ACT.Sqrt, inc="s")
            p.wait(V, "s")
            fv("reciprocal", sl(b + 2), sl(b + 1))
            fv("tensor_tensor", sl(b + 3), sl(b + 2), sl(b + 2), ALU.mult)
            fv("tensor_tensor", sl(b + 4), sl(b), sl(b + 3), ALU.mult)
            fv("tensor_scalar", sl(b + 5), sl(b + 4), -0.5, 1.5, ALU.mult,
               ALU.add)
            fv("tensor_tensor", sl(b + 6), sl(b + 2), sl(b + 5), ALU.mult)
            return sl(b + 6)

        rstd = newton(sl(5), 1e-5, 8)
        fv("tensor_scalar", sl(16), sl(2), -1.0, None, ALU.mult)
        fv("tensor_tensor", sl(17), rstd, rstd, ALU.mult)
        fv("tensor_tensor", sl(18), sl(5), sl(17), ALU.mult)
        rrms = newton(sl(18), 1e-6, 20)
        return sl(16), rstd, rrms

    rowps = lambda: ps["pu2"][0:1, 128:256]
    repps = lambda: ps["pu2"][:, 0:128]

    def make_rep(col, rep_sb):
        p.wait(Tn, "v")
        p.wait(Tn, "v", evac["pu2"])
        wt = p.op(Tn, "transpose", rowps(), col, sb["idf"][:, :], inc="t")
        p.wait(V, "t", wt)
        p.wait(V, "t", rowscr_t[0])
        rs = p.op(V, "tensor_copy", sb["rowscr"][0:1, 0:128], rowps(),
                  inc="v")
        p.wait(Tn, "v", rs)
        wb = p.op(Tn, "matmul", repps(), sb["onesf"][0:1, 0:128],
                  sb["rowscr"][0:1, 0:128], start=True, stop=True, inc="t")
        rowscr_t[0] = wb
        p.wait(V, "t", wb)
        ev = p.op(V, "tensor_copy", rep_sb, repps(), inc="v")
        evac["pu2"] = ev
        return ev

    hs_guard = [0, 0]
    for tt in range(NTT):
        par = tt % 2
        so = 32 * par
        xa_sl = sb["xa"][:, par * 1024:(par + 1) * 1024]
        for semn, vv in xa_done[par]:
            p.wait(Sy, semn, vv)
        dx = p.dma(Sy, xa_sl, dr["xs"][tt * 128:(tt + 1) * 128, :],
                   "dX%d" % par)
        p.wait(V, "dX%d" % par, dx)
        p.wait(Sc, "dX%d" % par, dx)
        dump = sb["scrB"][:, par * 1024:(par + 1) * 1024]
        negm, rstd, rrms = ln_stats(xa_sl, dump, so)
        if core_debug:
            p.op(V, "tensor_copy", sb["dbgln"][:, tt * 32:(tt + 1) * 32],
                 sb["statln"][:, so:so + 32])
        p.op(V, "tensor_copy", sb["rrmscols"][:, tt:tt + 1], rrms, inc="v")
        make_rep(negm, sb["mrepsb"][:, :])
        make_rep(rstd, sb["rrepsb"][:, :])

        for k in range(8):
            pmname = "pm%d" % (k % 2)
            fx = ps[pmname][:, 0:128]
            p.wait(Tn, "v", evac[pmname])
            wt = p.op(Tn, "transpose", fx, xa_sl[:, k * 128:(k + 1) * 128],
                      sb["idf"][:, :], inc="t")
            fsl = sb["fstg"][:, (k % 2) * 128:(k % 2) * 128 + 128]
            p.wait(V, "t", wt)
            p.wait(V, "s", hs_guard[k % 2])
            p.op(V, "tensor_tensor", fsl, fx, sb["mrepsb"][:, :], ALU.add,
                 inc="v")
            evac[pmname] = p.cnt["v"]
            p.wait(V, "v")
            p.op(V, "tensor_tensor", sb["istg2"][:, :], fsl,
                 sb["rrepsb"][:, :], ALU.mult, inc="v")
            p.wait(V, "v")
            hi_sl = sb["xnhi"][:, k * 2048 + tt * 128:k * 2048 + tt * 128 + 128]
            p.op(V, "tensor_copy", hi_sl, sb["istg2"][:, :], inc="v")
            p.wait(V, "v")
            lo_ready = p.op(V, "tensor_tensor", sb["xloT"][:, :],
                            sb["istg2"][:, :], hi_sl, ALU.subtract, inc="v")
            p.wait(Tn, "v", lo_ready)
            if k == 0:
                p.wait(Tn, "v", evac["pu1"])
            p.op(Tn, "matmul", ps["pu1"][0:8, 0:128],
                 sb["wrhib"][:, k * 8:(k + 1) * 8], hi_sl,
                 start=(k == 0), stop=False)
            p.op(Tn, "matmul", ps["pu1"][0:8, 0:128],
                 sb["wrhib"][:, k * 8:(k + 1) * 8], sb["xloT"][:, :],
                 start=False, stop=False)
            rmm = p.op(Tn, "matmul", ps["pu1"][0:8, 0:128],
                       sb["wrlob"][:, k * 8:(k + 1) * 8], hi_sl,
                       start=False, stop=(k == 7), inc="t")
            p.wait(V, "t", rmm)
        xa_done[par] = [("v", p.cnt["v"]), ("s", p.cnt["s"]),
                        ("t", p.cnt["t"])]

        p.wait(V, "t")
        ev = p.op(V, "tensor_copy", sb["r8"][:, :], ps["pu1"][0:8, 0:128],
                  inc="v")
        evac["pu1"] = ev
        p.wait(Tn, "v", ev)
        p.wait(Tn, "v", evac["pu0"])
        wt = p.op(Tn, "transpose", ps["pu0"][0:128, 0:8], sb["r8"][:, :],
                  sb["idf"][0:8, 0:8], inc="t")
        lg = sb["lgt"][:, par * 8:(par + 1) * 8]
        ohc = sb["ohtm"][:, par * 8:(par + 1) * 8]
        m8 = lambda i: sb["mx8"][:, 32 * par + i:32 * par + i + 1]
        vmax8 = sb["mx8"][:, 32 * par + 8:32 * par + 16]
        vidx8 = sb["mx8"][:, 32 * par + 16:32 * par + 24]
        p.wait(V, "t", wt)
        evac["pu0"] = p.op(V, "tensor_copy", lg, ps["pu0"][:, 0:8], inc="v")
        p.wait(V, "v")
        if core_debug:
            p.op(V, "tensor_copy", sb["dbglg"][:, tt * 8:(tt + 1) * 8], lg)
        p.op(V, "max", vmax8, lg, inc="v")
        p.wait(V, "v")
        idxu = sb["idxu"][:, par * 8:(par + 1) * 8]
        p.op(V, "max_index", idxu, vmax8, lg, inc="v")
        p.wait(V, "v")
        mxi = p.op(V, "tensor_copy", vidx8, idxu, inc="v")
        p.wait(V, "v", mxi)
        p.op(V, "tensor_copy", m8(0), vmax8[:, 0:1], inc="v")
        p.wait(Tn, "v", mxi)
        p.wait(Tn, "v", evac["pu3"])
        w1 = p.op(Tn, "transpose", ps["pu3"][0:1, 0:128],
                  vidx8[:, 0:1], sb["idf"][:, :], inc="t")
        p.wait(V, "t", w1)
        p.wait(V, "t", rowscr_t[0])
        rs = p.op(V, "tensor_copy", sb["rowscr"][0:1, 0:128],
                  ps["pu3"][0:1, 0:128], inc="v")
        p.wait(Tn, "v", rs)
        w2 = p.op(Tn, "matmul", ps["pu3"][:, 128:136],
                  sb["rowscr"][0:1, 0:128], sb["onesf"][0:1, 0:8],
                  start=True, stop=True, inc="t")
        rowscr_t[0] = w2
        p.wait(V, "t", w2)
        oh_v = p.op(V, "tensor_tensor", ohc, sb["iota8"][:, :],
                    ps["pu3"][:, 128:136], ALU.is_equal, inc="v")
        evac["pu3"] = oh_v
        p.wait(Sc, "v")
        p.op(Sc, "activation", sb["warmt"][:, 8:16], lg, ACT.Exp, bias=0.0,
             scale=1.0, inc="s")
        ws2 = p.op(Sc, "activation", m8(4), m8(0), ACT.Exp, bias=0.0,
                   scale=1.0, inc="s")
        p.wait(V, "s", ws2)
        p.op(V, "reduce_sum", m8(2), sb["warmt"][:, 8:16], axis=AX, inc="v")
        p.wait(V, "v")
        p.op(V, "reciprocal", m8(5), m8(2), inc="v")
        p.wait(V, "v")
        pm_ready = p.op(V, "tensor_tensor", sb["pmcols"][:, tt:tt + 1],
                        m8(4), m8(5), ALU.mult, inc="v")
        p.wait(Tn, "v", oh_v)
        p.wait(Tn, "v", evac["ptA"])
        wt2 = p.op(Tn, "transpose", ps["ptA"][0:8, 0:128], ohc,
                   sb["idb"][:, :], inc="t")
        p.wait(V, "t", wt2)
        evac["ptA"] = p.op(V, "tensor_copy",
                           sb["oh"][0:8, tt * 128:(tt + 1) * 128],
                           ps["ptA"][0:8, 0:128], inc="v")

    p.barrier()
    if core_debug:
        tapout("xnhi", (128, 16384), BF16)
        tapout("pmcols", (128, 16))
        tapout("oh", (8, 2048), BF16)
        tapout("dbgln", (128, 512))
        tapout("dbglg", (128, 128))
        p.wait(Sy, "v")
        p.dma(Sy, taps["dbgln"], sb["dbgln"][:, :], "dO")
        p.dma(Sy, taps["dbglg"], sb["dbglg"][:, :], "dO")
        p.dma(Sy, taps["xnhi"], sb["xnhi"][:, :], "dO")
        p.dma(Sy, taps["pmcols"], sb["pmcols"][:, :], "dO")
        p.dma(Sy, taps["oh"], sb["oh"][:, :], "dO")
        for _e in ENGS:
            p.wait(_e, "dO", p.cnt["dO"])

    ring_state = {"n": 0, "tcnt": [0, 0]}

    def ring_load(src_ap):
        j = ring_state["n"]
        slot = j % 2
        p.wait(Gp, "t", ring_state["tcnt"][slot])
        c = p.dma(Gp, sb["wring"][:, slot * 1024:(slot + 1) * 1024], src_ap,
                  "dR%d" % slot)
        ring_state["n"] = j + 1
        return ("dR%d" % slot, c, slot)

    def ring_use(tok):
        semn, c, slot = tok
        p.wait(Tn, semn, c)
        return sb["wring"][:, slot * 1024:(slot + 1) * 1024], slot

    def ring_done(slot):
        ring_state["tcnt"][slot] = p.cnt["t"]

    for st2 in range(2):
        par = st2
        so = 32 * par
        xa_sl = sb["xa"][:, par * 1024:(par + 1) * 1024]
        for semn, vv in xa_done[par]:
            p.wait(Sy, semn, vv)
        dx = p.dma(Sy, xa_sl, dr["xsum"][st2 * 128:(st2 + 1) * 128, :],
                   "dX%d" % par)
        p.wait(V, "dX%d" % par, dx)
        p.wait(Sc, "dX%d" % par, dx)
        dump = sb["scrB"][:, par * 1024:(par + 1) * 1024]
        negm, rstd, rrms = ln_stats(xa_sl, dump, so)
        make_rep(negm, sb["mrepsb"][:, :])
        make_rep(rstd, sb["rrepsb"][:, :])
        make_rep(rrms, sb["rrrepsb"][:, :])
        p.wait(V, "v")
        for k in range(8):
            pmname = "pm%d" % (k % 2)
            fx = ps[pmname][:, 0:128]
            p.wait(Tn, "v", evac[pmname])
            wt = p.op(Tn, "transpose", fx, xa_sl[:, k * 128:(k + 1) * 128],
                      sb["idf"][:, :], inc="t")
            fsl = sb["fstg"][:, (k % 2) * 128:(k % 2) * 128 + 128]
            p.wait(V, "t", wt)
            p.wait(V, "s", hs_guard[k % 2])
            p.op(V, "tensor_tensor", fsl, fx, sb["mrepsb"][:, :], ALU.add,
                 inc="v")
            evac[pmname] = p.cnt["v"]
            p.op(V, "tensor_tensor", sb["istg2"][:, :], fsl,
                 sb["rrepsb"][:, :], ALU.mult)
            tv = p.op(V, "tensor_tensor", fsl, sb["istg2"][:, :],
                      sb["rrrepsb"][:, :], ALU.mult, inc="v")
            p.wait(Sc, "v", tv)
            hs_sl = sb["scrE"][:, k * 256 + st2 * 128:k * 256 + st2 * 128 + 128]
            hs_guard[k % 2] = p.op(Sc, "activation", hs_sl, fsl, ACT.Identity,
                                   bias=0.0, scale=sb["wrmsfm"][:, k:k + 1],
                                   inc="s")
        xa_done[par] = [("v", p.cnt["v"]), ("s", p.cnt["s"]),
                        ("t", p.cnt["t"])]

    hs_ready = p.cnt["v"]
    gsum_s = {}
    kf_ready = 0
    for oi, o in enumerate(list(range(16, 24)) + list(range(8, 16)) + [25]):
        tok = ring_load(dr["wint"][o])
        pmname = "pm%d" % (oi % 2)
        pp = ps[pmname][:, 0:256]
        p.wait(Tn, "v", evac[pmname])
        p.wait(Tn, "v", hs_ready)
        wsl, slot = ring_use(tok)
        for k in range(8):
            p.op(Tn, "matmul", pp, wsl[:, k * 128:(k + 1) * 128],
                 sb["scrE"][:, k * 256:(k + 1) * 256],
                 start=(k == 0), stop=(k == 7), inc="t" if k == 7 else None)
        ring_done(slot)
        wt = p.cnt["t"]
        if 16 <= o < 24:
            tf = o - 16
            p.wait(Sc, "t", wt)
            ws = p.op(Sc, "activation", sb["scrC"][:, tf * 256:(tf + 1) * 256],
                      pp, ACT.Sigmoid, bias=sb["gbs"][:, tf:tf + 1], scale=1.0,
                      inc="s")
            gsum_s[tf] = ws
            p.wait(Tn, "s", ws)
            evac[pmname] = 0
        elif 8 <= o < 16:
            tf = o - 8
            p.wait(V, "t", wt)
            p.wait(V, "s", gsum_s[tf])
            p.op(V, "tensor_tensor", sb["fstg"][:, 0:256], pp,
                 sb["scrC"][:, tf * 256:(tf + 1) * 256], ALU.mult, inc="v")
            p.wait(V, "v")
            evac[pmname] = p.op(V, "tensor_tensor",
                                sb["cxbf"][:, tf * 256:(tf + 1) * 256], pp,
                                sb["fstg"][:, 0:256], ALU.subtract, inc="v")
        else:
            p.wait(V, "t", wt)
            kf_ready = p.op(V, "tensor_copy", sb["kf"][:, :], pp, inc="v")
            evac[pmname] = kf_ready
    for tf in range(8):
        p.wait(V, "v", kf_ready)
        p.wait(V, "s", gsum_s[tf])
        p.op(V, "tensor_tensor", sb["fstg"][:, 0:256], sb["kf"][:, :],
             sb["scrC"][:, tf * 256:(tf + 1) * 256], ALU.mult, inc="v")
        p.wait(V, "v")
        p.op(V, "tensor_tensor", sb["ckk"][:, tf * 256:(tf + 1) * 256],
             sb["kf"][:, :], sb["fstg"][:, 0:256], ALU.subtract, inc="v")
    p.wait(Tn, "v")
    for tf in range(8):
        for ct in range(2):
            ptn = "ptA" if (tf * 2 + ct) % 2 == 0 else "ptB"
            p.wait(Tn, "v", evac[ptn])
            wt = p.op(Tn, "transpose",
                      ps[ptn][0:128, tf * 128:(tf + 1) * 128],
                      sb["cxbf"][:, tf * 256 + ct * 128:
                                 tf * 256 + ct * 128 + 128],
                      sb["idb"][:, :], inc="t")
            p.wait(V, "t", wt)
            evac[ptn] = p.op(V, "tensor_copy",
                             sb["ckx"][:, ct * 1024 + tf * 128:
                                       ct * 1024 + (tf + 1) * 128],
                             ps[ptn][0:128, tf * 128:(tf + 1) * 128], inc="v")

    p.barrier()
    if core_debug:
        tapout("ckk", (128, 2048), BF16)
        tapout("ckx", (128, 2048), BF16)
        p.wait(Sy, "v")
        p.dma(Sy, taps["ckk"], sb["ckk"][:, :], "dO")
        p.dma(Sy, taps["ckx"], sb["ckx"][:, :], "dO")
        for _e in ENGS:
            p.wait(_e, "dO", p.cnt["dO"])

    unit_hist = []
    hsub_s = [0, 0]
    last_vW = [0]
    for sub in range(NSUB):
        s0 = sub * TS
        p.wait(Tn, "v", evac["ptBf"])
        for i2 in range(2):
            tt16 = sub * 2 + i2
            wt = p.op(Tn, "transpose", ps["ptBf"][0:1, 256:384],
                      sb["rrmscols"][:, tt16:tt16 + 1], sb["idf"][:, :],
                      inc="t")
            p.wait(V, "t", wt)
            p.wait(V, "t", rowscr_t[0])
            rs = p.op(V, "tensor_copy",
                      sb["rowscr"][0:1, i2 * 128:(i2 + 1) * 128],
                      ps["ptBf"][0:1, 256:384], inc="v")
            p.wait(Tn, "v", rs)
        wb = p.op(Tn, "matmul", ps["ptBf"][:, 0:256], sb["onesf"][0:1, 0:128],
                  sb["rowscr"][0:1, 0:256], start=True, stop=True, inc="t")
        rowscr_t[0] = wb
        p.wait(V, "t", wb)
        rep_r = p.op(V, "tensor_copy", sb["repsb"][:, 0:256],
                     ps["ptBf"][:, 0:256], inc="v")
        evac["ptBf"] = rep_r
        for k in range(8):
            fsl2 = sb["fstg"][:, (k % 2) * 256:(k % 2) * 256 + 256]
            p.wait(V, "v", rep_r)
            p.wait(V, "s", hsub_s[k % 2])
            hv = p.op(V, "tensor_tensor", fsl2,
                      sb["xnhi"][:, k * 2048 + s0:k * 2048 + s0 + TS],
                      sb["repsb"][:, 0:256], ALU.mult, inc="v")
            p.wait(Sc, "v", hv)
            hsub_s[k % 2] = p.op(Sc, "activation",
                                 sb["hsub"][:, k * 256:(k + 1) * 256], fsl2,
                                 ACT.Identity, bias=0.0,
                                 scale=sb["wrmsfm"][:, k:k + 1], inc="s")
            ka_mm("s", n=2)
        hsub_ready = p.cnt["s"]
        o_order = (list(range(16, 24)) + list(range(0, 8)) +
                   list(range(8, 16)) + [24, 25])
        g_evac = {}
        for oi, o in enumerate(o_order):
            tok = ring_load(dr["wint"][o])
            pmname = "pu%d" % (2 + oi % 2)
            pp = ps[pmname][:, 0:256]
            ka_mm(n=1)
            p.wait(Tn, "v", evac[pmname])
            if oi == 0:
                p.wait(Tn, "s", hsub_ready)
            wsl, slot = ring_use(tok)
            for k in range(8):
                p.op(Tn, "matmul", pp, wsl[:, k * 128:(k + 1) * 128],
                     sb["hsub"][:, k * 256:(k + 1) * 256],
                     start=(k == 0), stop=(k == 7),
                     inc="t" if k == 7 else None)
            ring_done(slot)
            wt = p.cnt["t"]
            if 16 <= o < 24:
                tf = o - 16
                p.wait(Sc, "t", wt)
                ws = p.op(Sc, "activation",
                          sb["scrA"][:, tf * 256:(tf + 1) * 256], pp,
                          ACT.Sigmoid, bias=sb["gbs"][:, tf:tf + 1],
                          scale=1.0, inc="s")
                g_evac[tf] = ws
                p.wait(Tn, "s", ws)
                evac[pmname] = 0
            elif o < 8:
                tf = o
                p.wait(Sc, "t", wt)
                ws = p.op(Sc, "activation",
                          sb["scrE"][:, tf * 256:(tf + 1) * 256], pp,
                          ACT.Sigmoid, inc="s")
                p.wait(Tn, "s", ws)
                evac[pmname] = 0
            elif o < 16:
                tf = o - 8
                p.wait(V, "t", wt)
                p.wait(V, "s", g_evac[tf])
                p.op(V, "tensor_tensor",
                     sb["scrB"][:, tf * 256:(tf + 1) * 256], pp,
                     sb["scrA"][:, tf * 256:(tf + 1) * 256], ALU.mult,
                     inc="v")
                p.wait(V, "v")
                evac[pmname] = p.op(V, "tensor_tensor",
                                    sb["scrB"][:, tf * 256:(tf + 1) * 256],
                                    pp, sb["scrB"][:, tf * 256:(tf + 1) * 256],
                                    ALU.subtract, inc="v")
            elif o == 24:
                p.wait(V, "t", wt)
                p.op(V, "tensor_scalar", sb["qf"][:, :], pp, 0.125, None,
                     ALU.mult, inc="v")
                evac[pmname] = p.op(V, "tensor_scalar", sb["qb"][:, :], pp,
                                    0.125, None, ALU.mult, inc="v")
            else:
                p.wait(V, "t", wt)
                evac[pmname] = p.op(V, "tensor_copy", sb["kf"][:, :], pp,
                                    inc="v")
                for tf in range(8):
                    p.wait(V, "s", g_evac[tf])
                    p.op(V, "tensor_tensor", sb["fstg"][:, 0:256],
                         sb["kf"][:, :],
                         sb["scrA"][:, tf * 256:(tf + 1) * 256], ALU.mult,
                         inc="v")
                    p.wait(V, "v")
                    p.op(V, "tensor_tensor",
                         sb["scrC"][:, tf * 256:(tf + 1) * 256],
                         sb["kf"][:, :], sb["fstg"][:, 0:256], ALU.subtract,
                         inc="v")
                    ka_mm("v", n=2)
        p.op(V, "memset", sb["scrA"][:, 0:2048:64], 0.0, inc="v")
        p.wait(V, "v")
        for tf in range(8):
            p.op(V, "tensor_tensor_scan",
                 sb["scrD"][:, tf * 256:(tf + 1) * 256],
                 sb["scrA"][:, tf * 256:(tf + 1) * 256],
                 sb["scrB"][:, tf * 256:(tf + 1) * 256], 0.0, ALU.mult,
                 ALU.add, inc="v")
            ka_mm("v", n=2)
        for tf in range(8):
            p.op(V, "tensor_tensor_scan", sb["fstg"][:, 256:512],
                 sb["scrA"][:, tf * 256:(tf + 1) * 256],
                 sb["scrC"][:, tf * 256:(tf + 1) * 256], 0.0, ALU.mult,
                 ALU.add, inc="v")
            p.wait(V, "v")
            p.op(V, "tensor_tensor", sb["fstg"][:, 0:256], sb["qf"][:, :],
                 sb["fstg"][:, 256:512], ALU.mult, inc="v")
            ka_mm("v", n=1)
            p.wait(V, "v")
            p.op(V, "tensor_copy", sb["tmph"][:, 0:256], sb["fstg"][:, 0:256],
                 inc="v")
            p.wait(V, "v")
            tl_ready = p.op(V, "tensor_tensor", sb["tmpl"][:, :],
                            sb["fstg"][:, 0:256], sb["tmph"][:, 0:256],
                            ALU.subtract, inc="v")
            p.wait(Tn, "v", tl_ready)
            if tf == 0:
                p.wait(Tn, "v", evac["pu2"])
            for tti in range(2):
                col = tti * 16 + 2 * tf
                p.op(Tn, "matmul", ps["pu2"][:, col:col + 2],
                     sb["tmph"][:, tti * 128:(tti + 1) * 128],
                     sb["selb"][:, :], start=True, stop=False)
                p.op(Tn, "matmul", ps["pu2"][:, col:col + 2],
                     sb["tmpl"][:, tti * 128:(tti + 1) * 128],
                     sb["selb"][:, :], start=False, stop=True,
                     inc="t" if (tf == 7 and tti == 1) else None)
        p.wait(V, "t")
        p.op(V, "tensor_copy", sb["ilse"][:, :], ps["pu2"][:, 0:32], inc="v")
        p.wait(V, "v")
        ilse_ready = p.op(V, "tensor_scalar", sb["nilse"][:, :],
                          sb["ilse"][:, :], -1.0, None, ALU.mult, inc="v")
        evac["pu2"] = ilse_ready

        ST = lambda i: sb["stat"][:, 32 + i:32 + i + 1]
        for bt in range(4):
            tti, hb = bt // 2, bt % 2
            gt = sub * 2 + tti
            heads = range(8 * hb, 8 * hb + 8)
            prev_sI = unit_hist[-1]["sI"] if unit_hist else 0
            if len(unit_hist) >= 2:
                p.wait(Tn, "v", unit_hist[-2]["evac_v"])
            p.wait(Tn, "v", ilse_ready)
            BQ = ((ps["pu2"], ps["pu3"], ps["ptAf"], ps["ptBf"]),
                  (ps["pm0"], ps["pm1"], ps["pu0"], ps["pu1"]))
            nb = len(unit_hist)
            scb = list(BQ[nb % 2])
            for j, hd in enumerate(heads):
                sc_ps = scb[j // 2][:, (j % 2) * 256:(j % 2) * 256 + 256]
                rq = (hd % 2) * 64
                tfh = hd // 2
                p.op(Tn, "matmul", sc_ps,
                     sb["qb"][rq:rq + 64, tti * 128:(tti + 1) * 128],
                     sb["ckk"][rq:rq + 64, tfh * 256:(tfh + 1) * 256],
                     start=True, stop=False)
                p.op(Tn, "matmul", sc_ps, sb["ub"][:, :],
                     sb["msb"][0:2, gt * 256:(gt + 1) * 256],
                     start=False, stop=True,
                     inc="t" if j == 7 else None)
            wtS = p.cnt["t"]
            p.wait(Sc, "t", wtS)
            for j in range(8):
                p.op(Sc, "activation",
                     sb["scsB"][:, j * 256:(j + 1) * 256],
                     scb[j // 2][:, (j % 2) * 256:(j % 2) * 256 + 256],
                     ACT.Identity, bias=0.0, scale=1.0,
                     inc="s" if j == 7 else None)
            sCP = p.cnt["s"]
            if bt == 0:
                ka_mm("s", at=sCP - 1, n=2)
                ka_mm("s", at=sCP, n=2)
            p.wait(V, "s", sCP)
            m0_8 = sb["stat"][:, 32:40]
            mm8 = sb["stat"][:, 40:48]
            mn8 = sb["stat"][:, 48:56]
            st8 = sb["stat"][:, 56:64]
            den0 = sb["mx8"][:, 0:8]
            den8 = sb["mx8"][:, 32:40]
            rec8 = sb["mx8"][:, 40:48]
            ils = sb["ilse"][:, tti * 16 + 8 * hb:tti * 16 + 8 * hb + 8]
            scv = sb["scsB"][:, :].rearrange("p (j c) -> p j c", c=256)
            p.op(V, "reduce_max", m0_8, scv, axis=AX, inc="v")
            p.wait(V, "v")
            p.op(V, "tensor_tensor", mm8, m0_8, ils, ALU.max, inc="v")
            p.wait(V, "v")
            vMN = p.op(V, "tensor_scalar", mn8, mm8, -1.0, None, ALU.mult,
                       inc="v")
            p.wait(V, "v", vMN)
            p.wait(V, "s", prev_sI)
            vST = p.op(V, "tensor_tensor", st8, ils, mn8, ALU.add, inc="v")
            if bt == 0:
                ka_mm("v", at=vST - 2, n=1)
                ka_mm("v", at=vST, n=1)
            p.wait(V, "v", vST)
            p.wait(Tn, "v", vST)
            wT1 = p.op(Tn, "transpose", scb[0][0:8, 0:128], mn8,
                       sb["idf"][:, :], inc="t")
            p.wait(V, "t", wT1)
            vR1 = p.op(V, "tensor_copy", sb["r8"][:, :], scb[0][0:8, 0:128],
                       inc="v")
            p.wait(Tn, "v", vR1)
            for q4 in range(4):
                p.op(Tn, "matmul", scb[q4][:, 0:512], sb["r8"][:, :],
                     sb["bo8"][0:8, q4 * 512:(q4 + 1) * 512],
                     start=True, stop=True, inc="t" if q4 == 3 else None)
            wT2 = p.cnt["t"]
            p.wait(V, "t", wT2)
            for q4 in range(4):
                p.op(V, "tensor_tensor",
                     sb["scsB"][:, q4 * 512:(q4 + 1) * 512],
                     scb[q4][:, 0:512],
                     sb["scsB"][:, q4 * 512:(q4 + 1) * 512], ALU.add,
                     inc="v" if q4 in (1, 3) else None)
            vS2 = p.cnt["v"]
            p.wait(V, "v", vS2)
            p.wait(Sc, "v", vS2)
            for j in range(8):
                p.op(Sc, "activation", sb["wbig"][:, j * 256:(j + 1) * 256],
                     sb["scsB"][:, j * 256:(j + 1) * 256], ACT.Exp,
                     bias=0.0, scale=1.0,
                     accum_out=sb["mx8"][:, j:j + 1],
                     inc="s" if j == 7 else None)
            iw8 = sb["mx8"][:, 48:56]
            sI = p.op(Sc, "activation", iw8, st8, ACT.Exp, bias=0.0,
                      scale=1.0, inc="s")
            if bt == 0:
                ka_mm("s", at=sI - 1, n=2)
                ka_mm("s", at=sI, n=1)
            p.wait(V, "s", sI)
            p.op(V, "tensor_tensor", den8, den0, iw8, ALU.add, inc="v")
            p.wait(V, "v")
            vD = p.op(V, "reciprocal", rec8, den8, inc="v")
            p.wait(V, "v", vD)
            p.wait(Tn, "v", vD)
            p.op(Tn, "transpose", scb[0][0:8, 0:128], iw8, sb["idf"][:, :])
            wT3 = p.op(Tn, "transpose", scb[0][0:8, 128:256], rec8,
                       sb["idf"][:, :], inc="t")
            p.wait(V, "t", wT3)
            p.op(V, "tensor_copy", sb["r8b"][0:8, 0:128],
                 scb[0][0:8, 0:128])
            vR3 = p.op(V, "tensor_copy", sb["r8b"][0:8, 128:256],
                       scb[0][0:8, 128:256], inc="v")
            p.wait(Tn, "v", vR3)
            for tfl in range(4):
                p.op(Tn, "matmul", scb[0][:, tfl * 128:(tfl + 1) * 128],
                     sb["selh"][0:8, tfl * 128:(tfl + 1) * 128],
                     sb["r8b"][0:8, 0:128], start=True, stop=True)
            for tfl in range(4):
                p.op(Tn, "matmul", scb[1][:, tfl * 128:(tfl + 1) * 128],
                     sb["selh"][0:8, tfl * 128:(tfl + 1) * 128],
                     sb["r8b"][0:8, 128:256], start=True, stop=True)
            p.wait(Tn, "v", last_vW[0])
            for j in range(8):
                for ct in range(2):
                    idx = j * 2 + ct
                    pt = ps["ptA"] if idx < 8 else ps["ptB"]
                    p.op(Tn, "transpose",
                         pt[0:128, (idx % 8) * 128:(idx % 8) * 128 + 128],
                         sb["wbig"][:, j * 256 + ct * 128:
                                    j * 256 + ct * 128 + 128],
                         sb["idb"][:, :],
                         inc="t" if idx == 15 else None)
            wT4 = p.cnt["t"]
            p.wait(V, "t", wT4)
            p.op(V, "tensor_copy", sb["wbig"][:, 2048:3072],
                 ps["ptA"][0:128, 0:1024])
            vW = p.op(V, "tensor_copy", sb["wbig"][:, 3072:4096],
                      ps["ptB"][0:128, 0:1024], inc="v")
            last_vW[0] = vW
            if bt == 0:
                ka_mm("v", at=vW - 1, n=1)
                ka_mm("v", at=vW, n=1)
            p.wait(Tn, "v", vW)
            for j, hd in enumerate(heads):
                rq = (hd % 2) * 64
                tfl = j // 2
                for ct in range(2):
                    idx = j * 2 + ct
                    p.op(Tn, "matmul",
                         scb[2][rq:rq + 64, tfl * 128:(tfl + 1) * 128],
                         sb["ckx"][:, ct * 1024 + hd * 64:
                                   ct * 1024 + hd * 64 + 64],
                         sb["wbig"][:, 2048 + idx * 128:2048 + idx * 128 + 128],
                         start=(ct == 0), stop=(ct == 1),
                         inc="t" if j == 7 and ct == 1 else None)
            wtO = p.cnt["t"]
            p.wait(V, "t", wtO)
            xkv = sb["scrD"][:, :].rearrange("p (tf c) -> p tf c", c=256)[
                :, 4 * hb:4 * hb + 4, tti * 128:(tti + 1) * 128]
            zv = sb["scrE"][:, :].rearrange("p (tf c) -> p tf c", c=256)[
                :, 4 * hb:4 * hb + 4, tti * 128:(tti + 1) * 128]
            azv = sb["azt"][:, :].rearrange("p (tf c) -> p tf c", c=256)[
                :, 4 * hb:4 * hb + 4, tti * 128:(tti + 1) * 128]
            p.op(V, "tensor_tensor", sb["fstg"][:, :], scb[0][:, 0:512],
                 xkv, ALU.mult, inc="v")
            p.wait(V, "v")
            p.op(V, "tensor_tensor", sb["scsB"][:, 0:512],
                 scb[2][:, 0:512], sb["fstg"][:, :], ALU.add, inc="v")
            p.wait(V, "v")
            p.op(V, "tensor_tensor", sb["fstg"][:, :], sb["scsB"][:, 0:512],
                 scb[1][:, 0:512], ALU.mult, inc="v")
            p.wait(V, "v")
            ev = p.op(V, "tensor_tensor", azv, sb["fstg"][:, :], zv,
                      ALU.mult, inc="v")
            unit_hist.append({"evac_v": ev, "sI": sI})
            qnames = (("pu2", "pu3", "ptAf", "ptBf"),
                      ("pm0", "pm1", "pu0", "pu1"))[nb % 2]
            for kk in qnames:
                evac[kk] = ev
            evac["ptA"] = ev
            evac["ptB"] = ev
        p.wait(Tn, "v", unit_hist[-1]["evac_v"])
        for o in range(8):
            tok = ring_load(dr["woutt"][o])
            pmname = "pu%d" % (2 + o % 2)
            pp = ps[pmname][:, 0:256]
            ka_mm(n=1)
            p.wait(Tn, "v", evac[pmname])
            wsl, slot = ring_use(tok)
            for k in range(8):
                p.op(Tn, "matmul", pp, wsl[:, k * 128:(k + 1) * 128],
                     sb["azt"][:, k * 256:(k + 1) * 256],
                     start=(k == 0), stop=(k == 7),
                     inc="t" if k == 7 else None)
            ring_done(slot)
            p.wait(V, "t")
            evac[pmname] = p.op(V, "tensor_tensor",
                                sb["sreg"][:, o * 2048 + s0:
                                           o * 2048 + s0 + TS],
                                pp, sb["xnhi"][:, o * 2048 + s0:
                                               o * 2048 + s0 + TS],
                                ALU.add, inc="v")

    p.barrier()
    if core_debug:
        tapout("sreg", (128, 16384), BF16)
        p.wait(Sy, "v")
        p.dma(Sy, taps["sreg"], sb["sreg"][:, :], "dO")
        for _e in ENGS:
            p.wait(_e, "dO", p.cnt["dO"])

    w1_tcnt = [0, 0]
    w2_tcnt = [0, 0]
    evac_s = {}
    QUADS = (("pm0", "pm1", "pu0", "pu1"), ("pu2", "pu3", "ptAf", "ptBf"))
    b1_scnt = [0, 0]
    stg_vcnt = [0, 0]
    db_tok = {0: db0_early}
    for e_i in range(NE):
        bp = e_i % 2
        db = db_tok[e_i]
        if e_i + 1 < NE:
            bpn = (e_i + 1) % 2
            p.wait(Gp, "s", b1_scnt[bpn])
            db_tok[e_i + 1] = p.dma(Gp, sb["b1s"][:, bpn * 32:(bpn + 1) * 32],
                                    dr["b1t"][e_i + 1], "db%d" % bpn)
        for q4 in range(4):
            pmname = "pm%d" % (q4 % 2)
            pp = ps[pmname][:, 0:512]
            p.wait(Tn, "v", evac[pmname])
            p.wait(Tn, "s", evac_s.get(pmname, 0))
            wt = p.op(Tn, "matmul", pp,
                      sb["sel8"][:, e_i * 128:(e_i + 1) * 128],
                      sb["oh"][0:8, q4 * 512:(q4 + 1) * 512],
                      start=True, stop=True, inc="t")
            p.wait(V, "t", wt)
            evac[pmname] = p.op(V, "tensor_copy",
                                sb["scrE"][:, q4 * 512:(q4 + 1) * 512], pp,
                                inc="v")
        mask_ready = p.cnt["v"]
        for q in range(4):
            p.wait(V, "t")
            h_ready_k = []
            for ft in range(8):
                wp = ft % 2
                p.wait(Gp, "t", w1_tcnt[wp])
                c1 = p.dma(Gp, sb["w1buf"][:, wp * 1024:(wp + 1) * 1024],
                           dr["w1t"][e_i, q, ft], "dW1%d" % wp)
                w1sl = sb["w1buf"][:, wp * 1024:(wp + 1) * 1024]
                banks = QUADS[ft % 2]
                for bn in banks:
                    p.wait(Tn, "v", evac[bn])
                    p.wait(Tn, "s", evac_s.get(bn, 0))
                p.wait(Tn, "dW1%d" % wp, c1)
                for k in range(8):
                    for s4 in range(4):
                        p.op(Tn, "matmul", ps[banks[s4]][:, 0:512],
                             w1sl[:, k * 128:(k + 1) * 128],
                             sb["xnhi"][:, k * 2048 + s4 * 512:
                                        k * 2048 + (s4 + 1) * 512],
                             start=(k == 0), stop=(k == 7),
                             inc="t" if (k == 7 and s4 == 3) else None)
                wt = p.cnt["t"]
                bcol = sb["b1s"][:, bp * 32 + q * 8 + ft:
                                 bp * 32 + q * 8 + ft + 1]
                p.wait(Sc, "t", wt)
                p.wait(Sc, "db%d" % bp, db)
                for s4 in range(4):
                    sp = s4 % 2
                    stgsl = sb["stg"][:, sp * 512:(sp + 1) * 512]
                    p.wait(Sc, "v", stg_vcnt[sp])
                    ws = p.op(Sc, "activation", stgsl, ps[banks[s4]][:, 0:512],
                              ACT.Gelu, bias=bcol, scale=1.0, inc="s")
                    evac_s[banks[s4]] = ws
                    evac[banks[s4]] = 0
                    p.wait(V, "s", ws)
                    p.wait(V, "v", mask_ready)
                    hsl = sb["big"][:, ft * 2048 + s4 * 512:
                                    ft * 2048 + (s4 + 1) * 512]
                    stg_vcnt[sp] = p.op(V, "tensor_tensor", hsl, stgsl,
                                        sb["scrE"][:, s4 * 512:(s4 + 1) * 512],
                                        ALU.mult, inc="v")
                w1_tcnt[wp] = p.cnt["t"]
                h_ready_k.append(p.cnt["v"])
            h_ready = p.cnt["v"]
            for d in range(8):
                wp = d % 2
                p.wait(Gp, "t", w2_tcnt[wp])
                c2 = p.dma(Gp, sb["w2buf"][:, wp * 1024:(wp + 1) * 1024],
                           dr["w2t"][e_i, d, q], "dW2%d" % wp)
                w2sl = sb["w2buf"][:, wp * 1024:(wp + 1) * 1024]
                banks = QUADS[d % 2]
                for bn in banks:
                    p.wait(Tn, "v", evac[bn])
                    p.wait(Tn, "s", evac_s.get(bn, 0))
                p.wait(Tn, "dW2%d" % wp, c2)
                if d > 0:
                    p.wait(Tn, "v", h_ready)
                last = (e_i == NE - 1 and q == 3)
                for k in range(8):
                    if d == 0:
                        p.wait(Tn, "v", h_ready_k[k])
                    for s4 in range(4):
                        p.op(Tn, "matmul", ps[banks[s4]][:, 0:512],
                             w2sl[:, k * 128:(k + 1) * 128],
                             sb["big"][:, k * 2048 + s4 * 512:
                                       k * 2048 + (s4 + 1) * 512],
                             start=(k == 0), stop=(k == 7 and not last),
                             inc="t" if (k == 7 and s4 == 3 and not last)
                             else None)
                if last:
                    for s4 in range(4):
                        p.op(Tn, "matmul", ps[banks[s4]][:, 0:512],
                             sb["b2hi"][0:8, d * 128:(d + 1) * 128],
                             sb["oh"][0:8, s4 * 512:(s4 + 1) * 512],
                             start=False, stop=False)
                        p.op(Tn, "matmul", ps[banks[s4]][:, 0:512],
                             sb["b2lo"][0:8, d * 128:(d + 1) * 128],
                             sb["oh"][0:8, s4 * 512:(s4 + 1) * 512],
                             start=False, stop=True,
                             inc="t" if s4 == 3 else None)
                wt = p.cnt["t"]
                p.wait(V, "t", wt)
                for s4 in range(4):
                    osl = sb["offn"][:, d * 2048 + s4 * 512:
                                     d * 2048 + (s4 + 1) * 512]
                    if e_i == 0 and q == 0:
                        ev = p.op(V, "tensor_copy", osl,
                                  ps[banks[s4]][:, 0:512], inc="v")
                    else:
                        ev = p.op(V, "tensor_tensor", osl,
                                  ps[banks[s4]][:, 0:512], osl, ALU.add,
                                  inc="v")
                    evac[banks[s4]] = ev
                w2_tcnt[wp] = p.cnt["t"]
        b1_scnt[bp] = p.cnt["s"]

    p.barrier()
    if core_debug:
        tapout("offn", (128, 16384), BF16)
        p.wait(Sy, "v")
        p.dma(Sy, taps["offn"], sb["offn"][:, :], "dO")
        for _e in ENGS:
            p.wait(_e, "dO", p.cnt["dO"])

    xrl_used = [0, 0]
    ot_stored = [0, 0]
    last_tpF = [0]
    for tt in range(NTT):
        par = tt % 2
        xr = sb["xrl"][:, par * 1024:(par + 1) * 1024]
        ot = sb["otile"][:, par * 1024:(par + 1) * 1024]
        p.wait(Sy, "v", xrl_used[par])
        dx = p.dma(Sy, xr, dr["xs"][tt * 128:(tt + 1) * 128, :], "dX%d" % par)
        p.wait(Tn, "v", evac["pu3"])
        wtp = p.op(Tn, "transpose", ps["pu3"][0:1, 128:256],
                   sb["pmcols"][:, tt:tt + 1], sb["idf"][:, :], inc="t")
        p.wait(V, "t", wtp)
        p.wait(V, "t", rowscr_t[0])
        rsc = p.op(V, "tensor_copy", sb["rowscr"][0:1, 0:128],
                   ps["pu3"][0:1, 128:256], inc="v")
        p.wait(Tn, "v", rsc)
        wt = p.op(Tn, "matmul", ps["pu3"][:, 0:128], sb["onesf"][0:1, 0:128],
                  sb["rowscr"][0:1, 0:128], start=True, stop=True, inc="t")
        rowscr_t[0] = wt
        ptn = "ptA" if par == 0 else "ptB"
        pt = ps[ptn]
        p.wait(Tn, "v", evac[ptn])
        p.wait(V, "t", wt)
        p.wait(V, "t", last_tpF[0])
        for d in range(8):
            dsl = slice(d * 2048 + tt * 128, d * 2048 + tt * 128 + 128)
            p.op(V, "tensor_tensor", sb["scrE"][:, d * 128:(d + 1) * 128],
                 sb["offn"][:, dsl], ps["pu3"][:, 0:128], ALU.mult,
                 inc="v" if d == 7 else None)
        vb = p.cnt["v"]
        p.wait(Sc, "v", vb)
        ws = p.op(Sc, "activation", sb["stg"][:, 0:1024],
                  sb["scrE"][:, 0:1024], ACT.Silu, inc="s")
        p.wait(V, "s", ws)
        srv = sb["sreg"][:, :].rearrange("p (d c) -> p d c", c=2048)[
            :, :, tt * 128:tt * 128 + 128]
        rv = p.op(V, "tensor_tensor",
                  sb["scrE"][:, 0:1024].rearrange("p (d c) -> p d c", c=128),
                  sb["stg"][:, 0:1024].rearrange("p (d c) -> p d c", c=128),
                  srv, ALU.mult, inc="v")
        p.wait(Tn, "v", rv)
        for d in range(8):
            p.op(Tn, "transpose", pt[0:128, d * 128:(d + 1) * 128],
                 sb["scrE"][:, d * 128:(d + 1) * 128], sb["idb"][:, :],
                 inc="t" if d == 7 else None)
        last_tpF[0] = p.cnt["t"]
        evac["pu3"] = p.cnt["v"]
        wt2 = p.cnt["t"]
        p.wait(V, "t", wt2)
        p.wait(V, "dX%d" % par, dx)
        addv = p.op(V, "tensor_tensor", ot, xr, pt[0:128, 0:1024], ALU.add,
                    inc="v")
        evac[ptn] = addv
        evac["pu3"] = addv
        xrl_used[par] = addv
        p.wait(Sy, "v", addv)
        ot_stored[par] = p.dma(Sy, y[tt * 128:(tt + 1) * 128, :], ot, "dO")

    p.wait(Sy, "dO", p.cnt["dO"])
    p.build()
    return nc, taps


def _prep_inputs(L0_attn_hidden_states, L0_attn_rms_weight, L0_attn_W_in,
                 L0_attn_gate_bias, L0_attn_W_out, L0_ffn_Wr, L0_ffn_W1,
                 L0_ffn_b1, L0_ffn_W2, L0_ffn_b2):
    x2 = np.ascontiguousarray(
        np.asarray(L0_attn_hidden_states, np.float32).reshape(16384, D))
    W_in = np.asarray(L0_attn_W_in, np.float32)
    W_out = np.asarray(L0_attn_W_out, np.float32)
    Wr = np.asarray(L0_ffn_Wr, np.float32)
    W1 = np.asarray(L0_ffn_W1, np.float32)
    W2 = np.asarray(L0_ffn_W2, np.float32)
    b1 = np.asarray(L0_ffn_b1, np.float32)
    b2 = np.asarray(L0_ffn_b2, np.float32)
    rms = np.asarray(L0_attn_rms_weight, np.float32)
    gb = np.asarray(L0_attn_gate_bias, np.float32)

    com = {}
    com["xsum"] = np.ascontiguousarray(x2[0::CHUNK])
    com["wint"] = np.ascontiguousarray(
        W_in.reshape(8, 128, 26, 128).transpose(2, 1, 0, 3)
        .reshape(26, 128, 1024))
    com["woutt"] = np.ascontiguousarray(
        W_out.reshape(8, 128, 8, 128).transpose(2, 1, 0, 3)
        .reshape(8, 128, 1024))
    com["wrt"] = np.ascontiguousarray(
        Wr.reshape(8, 128, 8).transpose(1, 0, 2).reshape(128, 64))
    com["w1t"] = np.ascontiguousarray(
        W1.reshape(NE, 8, 128, 32, 128).transpose(0, 3, 2, 1, 4)
        .reshape(NE, 4, 8, 128, 1024))
    com["w2t"] = np.ascontiguousarray(
        W2.reshape(NE, 4, 8, 128, 8, 128).transpose(0, 4, 1, 3, 2, 5)
        .reshape(NE, 8, 4, 128, 1024))
    com["b1t"] = np.ascontiguousarray(
        b1.reshape(NE, 32, 128).transpose(0, 2, 1))
    com["b2m"] = b2
    com["rmswt"] = np.ascontiguousarray(rms.reshape(8, 128).T)
    com["gbt"] = np.ascontiguousarray(gb.reshape(8, 128).T)
    com["ubh"] = np.zeros((2, 128), BF)
    com["ubh"][0, 0:64] = 1
    com["ubh"][1, 64:128] = 1
    com["idfh"] = np.eye(128, dtype=np.float32)
    com["idbh"] = np.eye(128, dtype=np.float32).astype(BF)
    com["onesfh"] = np.ones((1, 256), np.float32)
    com["onesbh"] = np.ones((1, 128), BF)
    selb = np.zeros((128, 2), BF)
    selb[0:64, 0] = 1
    selb[64:128, 1] = 1
    com["selbh"] = selb
    sel8 = np.zeros((8, 1024), BF)
    for e in range(8):
        sel8[e, e * 128:(e + 1) * 128] = 1
    com["sel8h"] = sel8
    com["iota8h"] = np.tile(np.arange(8, dtype=np.float32), (128, 1))
    bo8 = np.zeros((8, 2048), np.float32)
    for h in range(8):
        bo8[h, h * 256:(h + 1) * 256] = 1.0
    com["bo8h"] = bo8
    selh = np.zeros((8, 512), np.float32)
    for tfl in range(4):
        for pp_ in range(128):
            selh[2 * tfl + pp_ // 64, tfl * 128 + pp_] = 1.0
    com["selhh"] = selh

    in_maps = []
    for c in range(NCORES):
        m = dict(com)
        m["xs"] = np.ascontiguousarray(x2[c * T:(c + 1) * T])
        msb = np.zeros((2, 16 * 256), np.float32)
        for tt in range(16):
            c0 = (c * T + tt * 128) // CHUNK
            cc = np.arange(NCH)
            msb[0, tt * 256:(tt + 1) * 256] = np.where(cc >= c0, NEG, 0.0)
            msb[1, tt * 256:(tt + 1) * 256] = np.where(cc >= c0 + 1, NEG, 0.0)
        m["msbh"] = msb.astype(BF)
        in_maps.append(m)
    return in_maps


_NC_CACHE = {}


def kernel(**inputs):
    in_maps = _prep_inputs(**inputs)
    if "nc" not in _NC_CACHE:
        _NC_CACHE["nc"] = build(core_debug=False)[0]
    nc = _NC_CACHE["nc"]
    res = run_bass_kernel_spmd(nc, in_maps, core_ids=list(range(NCORES)))
    out = np.concatenate([np.asarray(r["y"]) for r in res.results], axis=0)
    return out.reshape(1, 16384, D)
